# version 08
# baseline (speedup 1.0000x reference)
import sys
sys.path.insert(0, "/opt/trn_rl_repo")
from contextlib import ExitStack

import numpy as np
import concourse.bass as bass
import concourse.mybir as mybir

F32 = mybir.dt.float32
BF16 = mybir.dt.bfloat16
AF = mybir.ActivationFunctionType
ALU = mybir.AluOpType
AX = mybir.AxisListType

ENGINES = ("sync", "tensor", "vector", "scalar", "gpsimd")


def _key(x):
    if isinstance(x, str):
        return x
    return x.name


class Op:
    __slots__ = ("eng", "emit", "deps", "inc", "kind", "cnt", "tag", "idx", "sem_idx")

    def __init__(self, eng, emit, kind, tag):
        self.eng = eng
        self.emit = emit
        self.kind = kind
        self.deps = []
        self.inc = False
        self.cnt = None
        self.tag = tag
        self.sem_idx = 0


class Prog:
    DMA_SEM_POOL = 12

    def __init__(self, nc, name="prog"):
        self.nc = nc
        self.name = name
        self.stack = ExitStack()
        self.scopes = []
        self.fence_op = None
        self.ops = {e: [] for e in ENGINES}
        self.order = []
        self.last_writer = {}
        self.readers = {}
        self.out_dmas = []
        self.sbuf_bytes_pp = 0
        self._uid = 0

    def scope(self):
        import contextlib

        @contextlib.contextmanager
        def _cm():
            st = ExitStack()
            self.scopes.append(st)
            try:
                yield
            finally:
                self.barrier()
                self.scopes.pop()
                st.close()
        return _cm()

    def barrier(self):
        lasts = [self.ops[e][-1] for e in ENGINES if self.ops[e]]
        o = Op("vector", lambda E: E.nop(), "exec", "fence")
        for d in lasts:
            if d is not o:
                o.deps.append(d)
        self.ops["vector"].append(o)
        self.order.append(o)
        self.fence_op = o

    def sb(self, name, shape, dtype, root=False):
        tgt = self.stack if (root or not self.scopes) else self.scopes[-1]
        t = tgt.enter_context(self.nc.sbuf_tensor(name, list(shape), dtype))
        isz = np.dtype(mybir.dt.np(dtype)).itemsize
        per = int(np.prod(shape[1:])) * isz if len(shape) > 1 else isz
        self.sbuf_bytes_pp += per
        if not hasattr(self, "sbuf_allocs"):
            self.sbuf_allocs = []
        self.sbuf_allocs.append((name, per))
        return t

    def ps(self, name, shape, dtype=F32):
        return self.stack.enter_context(self.nc.psum_tensor(name, list(shape), dtype))

    def op(self, eng, emit, reads=(), writes=(), dma=False, tag=None):
        o = Op(eng, emit, "dma" if dma else "exec", tag)
        rk = [_key(r) for r in reads]
        wk = [_key(w) for w in writes]
        deps = []
        for k in rk:
            w = self.last_writer.get(k)
            if w is not None:
                deps.append(w)
        for k in wk:
            w = self.last_writer.get(k)
            if w is not None and (w.eng != eng or w.kind == "dma" or dma):
                deps.append(w)
            for r in self.readers.get(k, ()):
                if r.eng != eng or r.kind == "dma" or dma:
                    deps.append(r)
        if self.fence_op is not None and self.fence_op is not o:
            deps.append(self.fence_op)
        seen = set()
        for d in deps:
            if d is o or id(d) in seen:
                continue
            seen.add(id(d))
            o.deps.append(d)
        for k in wk:
            self.last_writer[k] = o
            self.readers[k] = []
        for k in rk:
            self.readers.setdefault(k, []).append(o)
        self.ops[eng].append(o)
        self.order.append(o)
        return o

    def dma(self, eng, out, in_, out_is_output=False, rkey=None, wkey=None, **kw):
        o = self.op(eng, lambda E: E.dma_start(out=out, in_=in_, **kw),
                    reads=[rkey if rkey is not None else in_],
                    writes=[wkey if wkey is not None else out], dma=True, tag="dma")
        if out_is_output:
            self.out_dmas.append(o)
        return o

    def cp(self, eng, out, in_, extra_reads=()):
        return self.op(eng, lambda E: E.tensor_copy(out, in_),
                       reads=[in_, *extra_reads], writes=[out], tag="cp")

    def tt(self, eng, out, a, b, aluop):
        return self.op(eng, lambda E: E.tensor_tensor(out, a, b, aluop),
                       reads=[a, b], writes=[out], tag="tt")

    def ts(self, eng, out, in_, s1, s2, op0, op1=None, accum_out=None):
        rd = [in_] + [s for s in (s1, s2) if hasattr(s, "name")]
        wr = [out] + ([accum_out] if accum_out is not None else [])
        if op1 is None:
            return self.op(eng, lambda E: E.tensor_scalar(out, in_, s1, s2, op0),
                           reads=rd, writes=wr, tag="ts")
        return self.op(eng,
                       lambda E: E.tensor_scalar(out, in_, s1, s2, op0, op1,
                                                 accum_out=accum_out),
                       reads=rd, writes=wr, tag="ts")

    def stt(self, eng, out, in0, scalar, in1, op0, op1):
        rd = [in0, in1] + ([scalar] if hasattr(scalar, "name") else [])
        return self.op(eng, lambda E: E.scalar_tensor_tensor(out, in0, scalar, in1, op0, op1),
                       reads=rd, writes=[out], tag="stt")

    def act(self, eng, out, in_, fn, bias=0.0, scale=1.0, accum_out=None):
        rd = [in_] + [s for s in (bias, scale) if hasattr(s, "name")]
        wr = [out] + ([accum_out] if accum_out is not None else [])
        return self.op(eng,
                       lambda E: E.activation(out, in_, fn, bias=bias, scale=scale,
                                              accum_out=accum_out),
                       reads=rd, writes=wr, tag="act")

    def red(self, eng, out, in_, aluop, axis=AX.X, negate=None):
        return self.op(eng,
                       lambda E: E.tensor_reduce(out, in_, axis=axis, op=aluop,
                                                 negate=negate),
                       reads=[in_], writes=[out], tag="red")

    def memset(self, eng, ap, v):
        return self.op(eng, lambda E: E.memset(ap, v), writes=[ap], tag="memset")

    def mm(self, psum, lhsT, rhs, start, stop, extra_reads=()):
        return self.op("tensor",
                       lambda E: E.matmul(psum, lhsT, rhs, start=start, stop=stop),
                       reads=[lhsT, rhs, *extra_reads], writes=[psum], tag="mm")

    def mm_acc(self, psum, pairs, extra_reads=()):
        n = len(pairs)
        ops = []
        for i, (l, r) in enumerate(pairs):
            ops.append(self.mm(psum, l, r, i == 0, i == n - 1,
                               extra_reads=extra_reads if i == 0 else ()))
        return ops

    def transpose(self, psum, in_, ident):
        return self.op("tensor", lambda E: E.transpose(psum, in_, ident),
                       reads=[in_, ident], writes=[psum], tag="tr")

    def scan(self, eng, out, data0, data1, initial, op0=ALU.mult, op1=ALU.add):
        rd = [data0, data1] + ([initial] if hasattr(initial, "name") else [])
        return self.op(eng,
                       lambda E: E.tensor_tensor_scan(out, data0, data1, initial,
                                                      op0, op1),
                       reads=rd, writes=[out], tag="scan")

    def mr(self, eng, out, m8, vals, imm):
        return self.op(eng, lambda E: E.match_replace(out, m8, vals, imm),
                       reads=[m8, vals], writes=[out], tag="mr")

    def recip(self, eng, out, in_):
        return self.op(eng, lambda E: E.reciprocal(out, in_),
                       reads=[in_], writes=[out], tag="recip")

    def nop_dep(self, eng, reads, tag="join"):
        return self.op(eng, lambda E: E.nop(), reads=reads, tag=tag)

    def finish(self):
        deps = list(self.out_dmas)
        for e in ENGINES:
            if e != "sync" and self.ops[e]:
                deps.append(self.ops[e][-1])
        o = Op("sync", lambda E: E.nop(), "exec", "finish")
        seen = set()
        for d in deps:
            if id(d) not in seen:
                seen.add(id(d))
                o.deps.append(d)
        self.ops["sync"].append(o)
        self.order.append(o)

    def finalize(self, verbose=False):
        self.finish()
        for e in ENGINES:
            for op in self.ops[e]:
                for d in op.deps:
                    d.inc = True
        for o in self.out_dmas:
            o.inc = True
        sems = {}
        for e in ENGINES:
            c_exec = 0
            dma_i = 0
            dma_counts = {}
            any_exec_inc = False
            for op in self.ops[e]:
                if op.kind == "dma":
                    si = dma_i % self.DMA_SEM_POOL
                    dma_i += 1
                    op.sem_idx = si
                    dma_counts[si] = dma_counts.get(si, 0) + 16
                    op.cnt = dma_counts[si]
                    op.inc = True
                elif op.inc:
                    c_exec += 1
                    op.cnt = c_exec
                    any_exec_inc = True
                else:
                    op.cnt = None
            if any_exec_inc:
                sems[(e, "exec", 0)] = self.stack.enter_context(
                    self.nc.semaphore(f"s_{e}_x"))
            for si in sorted(dma_counts):
                sems[(e, "dma", si)] = self.stack.enter_context(
                    self.nc.semaphore(f"s_{e}_d{si}"))
        self.sems = sems

        nins = {e: len(self.ops[e]) for e in ENGINES if self.ops[e]}
        if verbose:
            print(f"[bk] {self.name}: ops={nins} sbuf_pp={self.sbuf_bytes_pp}")

        block = self.stack.enter_context(self.nc.Block())
        for e in ENGINES:
            if not self.ops[e]:
                continue
            self._emit_section(block, e)

    def _emit_section(self, block, e):
        ops = self.ops[e]
        sems = self.sems

        def section(E):
            seen_wm = {}
            for op in ops:
                for d in op.deps:
                    k = (d.eng, d.kind, d.sem_idx if d.kind == "dma" else 0)
                    sem = sems[k]
                    if seen_wm.get(k, -1) < d.cnt:
                        E.wait_ge(sem, d.cnt)
                        seen_wm[k] = d.cnt
                inst = op.emit(E)
                if op.kind == "dma":
                    inst.then_inc(sems[(e, "dma", op.sem_idx)], 16)
                elif op.inc:
                    inst.then_inc(sems[(e, "exec", 0)], 1)

        getattr(block, e)(section)


def build_nc():
    return bass.Bass("TRN2", target_bir_lowering=False)




NEG_BIG = -1e30


class Ctx:
    def __init__(self, p, T, Tc):
        self.p = p
        self.nc = p.nc
        self.T = T
        self.Tc = Tc
        self.uid = 0
        nc = self.nc
        self.identb_d = nc.dram_tensor("c_identb", [128, 128], BF16,
                                       kind="ExternalInput").ap()
        self.ones_d = nc.dram_tensor("c_ones", [1, 128], F32,
                                     kind="ExternalInput").ap()
        self.identb = p.sb("identb", [128, 128], BF16)
        self.ones = p.sb("ones", [1, 128], F32)
        p.dma("sync", self.identb[:, :], self.identb_d)
        p.dma("sync", self.ones[:, :], self.ones_d)
        self.colconst = {}
        self.wstage = [p.sb("wstage0", [128, 512], F32),
                       p.sb("wstage1", [128, 512], F32)]
        self.seln_d = nc.dram_tensor("c_seln", [16, 16 * 128], BF16,
                                     kind="ExternalInput").ap()
        self.seln = p.sb("seln", [16, 16 * 128], BF16)
        self.rowbuf = p.sb("rowbuf", [1, 1024], F32)
        self.bstage = p.sb("bstage", [128, 512], BF16)
        self._bf16_cache = {}
        p.dma("sync", self.seln[:, :], self.seln_d)
        self._wstage_i = 0
        self.pmm = [p.ps(f"pmm{i}", [128, 512], F32) for i in range(4)]
        self.ptr = [p.ps(f"ptr{i}", [128, 128], BF16) for i in range(2)]
        self.pbc = [p.ps(f"pbc{i}", [128, 512], F32) for i in range(2)]

    def n(self, base):
        self.uid += 1
        return f"{base}_{self.uid}"

    def tmp(self, tag, shape, dtype=F32):
        if not hasattr(self, "_tmps"):
            self._tmps = {}
        key = (tag, tuple(shape), dtype)
        if key not in self._tmps:
            assert not self.p.scopes, f"ctx.tmp({tag}) first-created inside scope"
            self._tmps[key] = self.p.sb(self.n("tmp_" + tag), list(shape), dtype,
                                        root=True)
        return self._tmps[key]

    def prewarm(self):
        self.tmp("scrA", [128, 4096])
        self.tmp("scrB_bf16", [128, 4096], BF16)
        for t in ("ln_sum", "ln_ssq", "ln_mu", "ln_var", "ln_rstd"):
            self.tmp(t, [128, 4])
        self.tmp("sp_e1", [1, 512])
        self.tmp("mb_e1", [128, 256])
        self.tmp("rt_logits", [128, 4, 16])
        self.tmp("rt_probs", [128, 4, 16])
        self.tmp("rt_nmax", [128, 1])
        self.tmp("rt_esum", [128, 1])
        self.tmp("tk_m8", [128, 8])
        self.tmp("tk_mr1", [128, 16])
        self.tmp("tk_mr2", [128, 16])
        self.tmp("tk_oh", [128, 4, 32])
        self.tmp("tk_mx", [128, 4, 2])
        self.tmp("ff_tmp", [128, 512])
        self.tmp("f1_s12", [128, 1])
        self.tmp("f1_c1", [128, 1])
        self.tmp("f1_c2", [128, 1])
        self.tmp("f2_w2c", [128, 1])
        self.const_col(1e-5)
        self.const_col(1.0)

    def const_col(self, val, P=128):
        if val not in self.colconst:
            t = self.p.sb(self.n("ccol"), [128, 1], F32)
            self.p.memset("vector", t[:, :], float(val))
            self.colconst[val] = t
        return self.colconst[val][0:P, :]

    def wload(self, dram_ap, K, M, name="w", dt=BF16):
        p = self.p
        G = K // 128
        assert K % 128 == 0
        out = p.sb(self.n(name), [128, G, M], dt)
        for g in range(G):
            mo = 0
            while mo < M:
                mw = min(512, M - mo)
                st = self.wstage[self._wstage_i]
                self._wstage_i = (self._wstage_i + 1) % len(self.wstage)
                p.dma("sync", st[:, 0:mw], dram_ap[g * 128:(g + 1) * 128, mo:mo + mw])
                p.cp("vector", out[:, g, mo:mo + mw], st[:, 0:mw])
                mo += mw
        return out

    def to_bf16_dram(self, dram_f32_ap, K, M, name):
        if name in self._bf16_cache:
            return self._bf16_cache[name]
        p = self.p
        out = self.nc.dram_tensor(name, [K, M], BF16).ap()
        for g in range(K // 128):
            mo = 0
            while mo < M:
                mw = min(512, M - mo)
                st = self.wstage[self._wstage_i]
                self._wstage_i = (self._wstage_i + 1) % len(self.wstage)
                p.dma("sync", st[:, 0:mw],
                      dram_f32_ap[g * 128:(g + 1) * 128, mo:mo + mw])
                p.cp("gpsimd", self.bstage[:, 0:mw], st[:, 0:mw])
                p.dma("scalar", out[g * 128:(g + 1) * 128, mo:mo + mw],
                      self.bstage[:, 0:mw])
                mo += mw
        self._bf16_cache[name] = out
        return out

    def wload_into(self, out, dram_ap, K, M, eng="gpsimd"):
        p = self.p
        G = K // 128
        for g in range(G):
            mo = 0
            while mo < M:
                mw = min(512, M - mo)
                st = self.wstage[self._wstage_i]
                self._wstage_i = (self._wstage_i + 1) % len(self.wstage)
                p.dma("sync", st[:, 0:mw], dram_ap[g * 128:(g + 1) * 128, mo:mo + mw])
                p.cp(eng, out[:, g, mo:mo + mw], st[:, 0:mw])
                mo += mw
        return out

    def row_tile_into(self, out, dram_row_ap, W, rowbuf):
        p = self.p
        p.dma("sync", rowbuf[0:1, 0:W], dram_row_ap)
        o = 0
        while o < W:
            w = min(512, W - o)
            pb = self.pbc[0]
            p.mm(pb[:, 0:w], self.ones[0:1, :], rowbuf[0:1, o:o + w], True, True)
            p.cp("vector", out[:, o:o + w], pb[:, 0:w])
            o += w
        return out

    def vload(self, dram_ap, G, name="v"):
        p = self.p
        out = p.sb(self.n(name), [128, G], F32)
        p.dma("sync", out[:, :], dram_ap)
        return out

    def row_tile(self, dram_row_ap, W, name="row", dt=F32):
        p = self.p
        out = p.sb(self.n(name), [128, W], dt)
        return self.row_tile_into(out, dram_row_ap, W, self.rowbuf)

    def transpose_tmaj_to_dmaj(self, dst, src_bf16, ntt, dgroups,
                               src_t0=0, src_d0=0, dst_t0=0, dst_g0=0):
        p = self.p
        for tt in range(ntt):
            for g in range(dgroups):
                pt = self.ptr[(tt * dgroups + g) % 2]
                p.transpose(pt[:, :],
                            src_bf16[:, src_t0 + tt,
                                     src_d0 + g * 128: src_d0 + (g + 1) * 128],
                            self.identb[:, :])
                p.cp("vector",
                     dst[:, dst_g0 + g, dst_t0 + tt * 128: dst_t0 + (tt + 1) * 128],
                     pt[:, :])


def _tn(t):
    try:
        return t.name
    except AttributeError:
        return t[:, 0:1].name


def bct(ap2, ntt, D):
    return ap2.unsqueeze(2).broadcast_to([128, ntt, D])


def ln_chunk(ctx, x_ap, ntt, D, out_ap, gam_row=None, beta_row=None):
    p = ctx.p
    sm = ctx.tmp("ln_sum", [128, 4])[:, 0:ntt]
    sq = ctx.tmp("scrA", [128, 4096])[:, 0:D]
    ssq = ctx.tmp("ln_ssq", [128, 4])[:, 0:ntt]
    mu = ctx.tmp("ln_mu", [128, 4])[:, 0:ntt]
    var = ctx.tmp("ln_var", [128, 4])[:, 0:ntt]
    rstd = ctx.tmp("ln_rstd", [128, 4])[:, 0:ntt]
    xc = ctx.tmp("scrA", [128, 4096])[:, 0:ntt * D].rearrange(
        "p (a b) -> p a b", b=D)
    p.red("vector", sm, x_ap, ALU.add)
    for tt in range(ntt):
        p.act("scalar", sq, x_ap[:, tt, :], AF.Square,
              accum_out=ssq[:, tt:tt + 1])
    p.ts("vector", mu, sm, 1.0 / D, None, ALU.mult)
    p.ts("vector", ssq, ssq, 1.0 / D, None, ALU.mult)
    p.tt("vector", var, mu, mu, ALU.mult)
    p.tt("vector", var, ssq, var, ALU.subtract)
    p.act("scalar", rstd, var, AF.Ln, bias=ctx.const_col(1e-5))
    p.act("scalar", rstd, rstd, AF.Exp, scale=-0.5)
    p.tt("vector", xc, x_ap, bct(rstd if False else mu, ntt, D), ALU.subtract)
    if gam_row is None:
        p.tt("vector", out_ap, xc, bct(rstd, ntt, D), ALU.mult)
    else:
        p.tt("vector", xc, xc, bct(rstd, ntt, D), ALU.mult)
        p.tt("vector", xc, xc,
             gam_row[:, :].unsqueeze(1).broadcast_to([128, ntt, D]), ALU.mult)
        p.tt("vector", out_ap, xc,
             beta_row[:, :].unsqueeze(1).broadcast_to([128, ntt, D]), ALU.add)
    return mu, rstd


def softplus(ctx, out_ap, in_ap, shape):
    p = ctx.p
    e1 = ctx.tmp("sp_e1", [1, 512])[0:shape[0], 0:shape[1]]
    sl = tuple([slice(None)] * len(shape))
    p.act("scalar", e1[sl], in_ap, AF.Abs)
    p.act("scalar", e1[sl], e1[sl], AF.Exp, scale=-1.0)
    p.act("scalar", e1[sl], e1[sl], AF.Ln, bias=ctx.const_col(1.0, shape[0]))
    p.act("scalar", out_ap, in_ap, AF.Relu)
    p.tt("vector", out_ap, out_ap, e1[sl], ALU.add)


def conv_silu(ctx, xc, xi, cw, cb, G, Tc, tmp):
    p = ctx.p
    for k in range(4):
        cwb = cw[:, :, k].broadcast_to([128, G]).unsqueeze(2).broadcast_to([128, G, Tc]) \
            if False else cw[:, :, k:k + 1].broadcast_to([128, G, Tc])
        if k == 0:
            p.tt("vector", xc[:, :, 0:Tc], xi[:, :, 0:Tc], cwb, ALU.mult)
        else:
            p.tt("vector", tmp[:, :, 0:Tc], xi[:, :, k:k + Tc], cwb, ALU.mult)
            p.tt("vector", xc[:, :, 0:Tc], xc[:, :, 0:Tc], tmp[:, :, 0:Tc], ALU.add)
    p.tt("vector", xc[:, :, 0:Tc], xc[:, :, 0:Tc],
         cb[:, :].unsqueeze(2).broadcast_to([128, G, Tc]), ALU.add)
    p.act("scalar", xc[:, :, 0:Tc], xc[:, :, 0:Tc], AF.Silu)


def emit_l2_attn(ctx, x_dram, xo_dram, W):
    p = ctx.p
    T, Tc = ctx.T, ctx.Tc
    ntt = Tc // 128
    nch = T // Tc
    G = 8
    N = 16

    win = ctx.wload(W["W_in"], 1024, 2048, "l2win")
    wxp = ctx.wload(W["W_xproj"], 1024, 33, "l2wxp")
    wout = ctx.wload(W["W_out"], 1024, 1024, "l2wout")
    cw = ctx.vload(W["conv_w"], 32, "l2cw")
    cb = ctx.vload(W["conv_b"], G, "l2cb")
    Dp = ctx.vload(W["D_param"], G, "l2dp")
    Alog = ctx.vload(W["log_A"], G * N, "l2alog")
    A = p.sb("l2A", [128, G * N], F32)
    p.act("scalar", A[:, :], Alog[:, :], AF.Exp)
    p.ts("vector", A[:, :], A[:, :], -1.0, None, ALU.mult)

    xt = p.sb("l2_xt", [128, ntt, 1024], F32)
    scrB = ctx.tmp("scrB_bf16", [128, 4096], BF16)
    xnb = scrB[:, 0:ntt * 1024].rearrange("p (a b) -> p a b", b=1024)
    xnT = p.sb("l2_xnT", [128, G, Tc], BF16)
    xi = p.sb("l2_xi", [128, G, Tc + 3], F32)
    z = p.sb("l2_z", [128, G, Tc], F32)
    xc = p.sb("l2_xc", [128, G, Tc], F32)
    xcb = scrB[:, 0:G * Tc].rearrange("p (a b) -> p a b", b=Tc)
    gtmp = ctx.tmp("scrA", [128, 4096])[:, 0:G * Tc].rearrange(
        "p (a b) -> p a b", b=Tc)
    sB = p.sb("l2_sB", [16, Tc], F32)
    sC = p.sb("l2_sC", [16, Tc], F32)
    sDl = p.sb("l2_sDl", [1, Tc], F32)
    brow = p.sb("l2_brow", [1, Tc], F32)
    crow = p.sb("l2_crow", [1, Tc], F32)
    dtrow = p.sb("l2_dtrow", [1, Tc], F32)
    dt_sb = p.sb("l2_dt", [128, Tc], F32)
    at = [p.sb("l2_a0", [128, Tc], F32)]
    ut = [p.sb("l2_u0", [128, Tc], F32)]
    ht = [p.sb(f"l2_h{i}", [128, Tc], F32) for i in range(2)]
    stmp = p.sb("l2_stmp", [128, Tc], F32)
    y = p.sb("l2_y", [128, G, Tc], F32)
    yvb = scrB[:, 0:G * Tc].rearrange("p (a b) -> p a b", b=Tc)
    hst = p.sb("l2_hst", [128, G, N], F32)

    p.memset("vector", hst[:, :, :], 0.0)
    p.memset("vector", xi[:, :, 0:3], 0.0)
    cw3 = cw[:, :].rearrange("p (g k) -> p g k", k=4)

    for c in range(nch):
        t0 = c * Tc
        for tt in range(ntt):
            r0 = t0 + tt * 128
            p.dma("sync", xt[:, tt, :], x_dram[r0:r0 + 128, :])
        ln_chunk(ctx, xt[:, 0:ntt, :], ntt, 1024, xnb[:, 0:ntt, :])
        ctx.transpose_tmaj_to_dmaj(xnT, xnb, ntt, G)
        if c > 0:
            p.cp("vector", xi[:, :, 0:3], xi[:, :, Tc:Tc + 3])
        for j in range(16):
            pm = ctx.pmm[j % 2]
            p.mm_acc(pm[:, 0:Tc],
                     [(win[:, k, j * 128:(j + 1) * 128], xnT[:, k, 0:Tc])
                      for k in range(8)])
            if j < 8:
                p.cp("vector", xi[:, j, 3:3 + Tc], pm[:, 0:Tc])
            else:
                p.cp("vector", z[:, j - 8, 0:Tc], pm[:, 0:Tc])
        conv_silu(ctx, xc, xi, cw3, cb, G, Tc, gtmp)
        p.cp("vector", xcb[:, :, :], xc[:, :, :])
        pB = ctx.pmm[2]
        pC = ctx.pmm[3]
        pD = ctx.pmm[0]
        p.mm_acc(pB[0:16, 0:Tc], [(wxp[:, k, 0:16], xcb[:, k, 0:Tc]) for k in range(8)])
        p.mm_acc(pC[0:16, 0:Tc], [(wxp[:, k, 16:32], xcb[:, k, 0:Tc]) for k in range(8)])
        p.mm_acc(pD[0:1, 0:Tc], [(wxp[:, k, 32:33], xcb[:, k, 0:Tc]) for k in range(8)])
        p.cp("vector", sB[:, :], pB[0:16, 0:Tc])
        p.cp("vector", sC[:, :], pC[0:16, 0:Tc])
        p.cp("vector", sDl[:, :], pD[0:1, 0:Tc])
        softplus(ctx, dtrow[:, :], sDl[:, :], [1, Tc])
        assert Tc <= 512
        pb = ctx.pbc[0]
        p.mm(pb[:, 0:Tc], ctx.ones[0:1, :], dtrow[0:1, 0:Tc], True, True)
        p.cp("vector", dt_sb[:, 0:Tc], pb[:, 0:Tc])
        for n in range(N):
            pBb = ctx.pbc[0]
            pCb = ctx.pbc[1]
            p.dma("sync", brow[:, :], sB[n:n + 1, :])
            p.dma("sync", crow[:, :], sC[n:n + 1, :])
            p.tt("vector", brow[:, :], brow[:, :], dtrow[:, :], ALU.mult)
            p.mm(pBb[:, 0:Tc], ctx.ones[0:1, :], brow[0:1, 0:Tc], True, True)
            p.mm(pCb[:, 0:Tc], ctx.ones[0:1, :], crow[0:1, 0:Tc], True, True)
            for g in range(G):
                a = at[0]
                u = ut[0]
                h = ht[g % 2]
                p.act("scalar", a[:, 0:Tc], dt_sb[:, 0:Tc], AF.Exp,
                      scale=A[:, g * N + n: g * N + n + 1])
                p.tt("vector", u[:, 0:Tc], xc[:, g, 0:Tc], pBb[:, 0:Tc], ALU.mult)
                p.scan("vector", h[:, 0:Tc], a[:, 0:Tc], u[:, 0:Tc],
                       hst[:, g, n:n + 1])
                p.cp("vector", hst[:, g, n:n + 1], h[:, Tc - 1:Tc])
                if n == 0:
                    p.tt("vector", y[:, g, 0:Tc], h[:, 0:Tc], pCb[:, 0:Tc], ALU.mult)
                else:
                    p.tt("vector", stmp[:, 0:Tc], h[:, 0:Tc], pCb[:, 0:Tc], ALU.mult)
                    p.tt("vector", y[:, g, 0:Tc], y[:, g, 0:Tc], stmp[:, 0:Tc],
                         ALU.add)
        p.tt("vector", gtmp[:, :, :], xc[:, :, :],
             Dp[:, :].unsqueeze(2).broadcast_to([128, G, Tc]), ALU.mult)
        p.tt("vector", y[:, :, :], y[:, :, :], gtmp[:, :, :], ALU.add)
        p.act("scalar", z[:, :, :], z[:, :, :], AF.Silu)
        p.tt("vector", y[:, :, :], y[:, :, :], z[:, :, :], ALU.mult)
        p.cp("vector", yvb[:, :, :], y[:, :, :])
        for m in range(ntt):
            for no in range(0, 1024, 512):
                pm = ctx.pmm[(m * 2 + no // 512) % 2]
                p.mm_acc(pm[:, 0:512],
                         [(yvb[:, k, m * 128:(m + 1) * 128],
                           wout[:, k, no:no + 512]) for k in range(G)])
                p.tt("vector", xt[:, m, no:no + 512], xt[:, m, no:no + 512],
                     pm[:, 0:512], ALU.add)
        for tt in range(ntt):
            r0 = t0 + tt * 128
            p.dma("sync", xo_dram[r0:r0 + 128, :], xt[:, tt, :], out_is_output=True)



class MambaCfg:

    def __init__(self, tag, d_in, DI, R, rev, W, stream_inw=False):
        self.tag = tag
        self.d_in = d_in
        self.DI = DI
        self.GI = DI // 128
        self.R = R
        self.rev = rev
        self.W = W
        self.stream_inw = stream_inw


def mamba_dir_setup(ctx, cfg):
    p = ctx.p
    t = cfg.tag
    h = {}
    if not cfg.stream_inw:
        h["inw"] = ctx.wload(cfg.W["inw_lhsT"], cfg.d_in, 2 * cfg.DI, t + "inw")
    h["xproj"] = ctx.wload(cfg.W["xproj_lhsT"], cfg.DI, cfg.R + 32, t + "xp")
    dtw = p.sb(ctx.n(t + "dtw"), [cfg.R, cfg.DI], BF16)
    mo = 0
    while mo < cfg.DI:
        mw = min(512, cfg.DI - mo)
        st = ctx.wstage[0]
        p.dma("sync", st[0:cfg.R, 0:mw], cfg.W["dtw_lhsT"][:, mo:mo + mw])
        p.cp("gpsimd", dtw[:, mo:mo + mw], st[0:cfg.R, 0:mw])
        mo += mw
    h["dtw"] = dtw
    h["cw"] = ctx.vload(cfg.W["conv_w"], cfg.GI * 4, t + "cw")
    h["cb"] = ctx.vload(cfg.W["conv_b"], cfg.GI, t + "cb")
    h["dtb"] = ctx.vload(cfg.W["dt_b"], cfg.GI, t + "dtb")
    h["Dp"] = ctx.vload(cfg.W["D"], cfg.GI, t + "Dp")
    Alog = ctx.vload(cfg.W["A_log"], cfg.GI * 16, t + "Alog")
    A = p.sb(ctx.n(t + "A"), [128, cfg.GI * 16], F32)
    p.act("scalar", A[:, :], Alog[:, :], AF.Exp)
    p.ts("vector", A[:, :], A[:, :], -1.0, None, ALU.mult)
    h["A"] = A
    return h


def mamba_dir_chunk(ctx, cfg, h, buf, xnT_ap, kgroups, ci, nch, yv_out):
    p = ctx.p
    Tc = ctx.Tc
    GI, R, N = cfg.GI, cfg.R, 16
    rev = cfg.rev
    first = (ci == (nch - 1 if rev else 0))
    xi, z, xc, xcb = buf["xi"], buf["z"], buf["xc"], buf["xcb"]
    dt, dtx = buf["dt"], buf["dtx"]
    hst = buf["hst"]
    sB, sC, sDt = buf["sB"], buf["sC"], buf["sDt"]
    dtrawb = buf["dtrawb"]
    y = buf["y"]

    dataoff = 3 if not rev else 0
    haloslice = (slice(0, 3) if not rev else slice(Tc, Tc + 3))
    saveslice = (slice(Tc, Tc + 3) if not rev else slice(0, 3))
    if first:
        p.memset("vector", hst[:, :, :], 0.0)
        p.memset("vector", xi[:, :, haloslice], 0.0)
    else:
        p.cp("vector", xi[:, :, haloslice], buf["halo3"][:, :, :])
    if cfg.stream_inw:
        wj = buf["wjk"]
        inwb = buf["inwb_dram"]
        for j in range(2 * GI):
            wjt = wj[j % 2]
            for ki, kg in enumerate(kgroups):
                p.dma("sync", wjt[:, ki, :],
                      inwb[kg * 128:(kg + 1) * 128, j * 128:(j + 1) * 128])
            pm = ctx.pmm[j % 2]
            p.mm_acc(pm[:, 0:Tc],
                     [(wjt[:, ki, :], xnT_ap[:, kg, 0:Tc])
                      for ki, kg in enumerate(kgroups)])
            if j < GI:
                p.cp("vector", xi[:, j, dataoff:dataoff + Tc], pm[:, 0:Tc])
            else:
                p.cp("vector", z[:, j - GI, 0:Tc], pm[:, 0:Tc])
    else:
        inw = h["inw"]
        for j in range(2 * GI):
            pm = ctx.pmm[j % 2]
            p.mm_acc(pm[:, 0:Tc],
                     [(inw[:, ki, j * 128:(j + 1) * 128], xnT_ap[:, kg, 0:Tc])
                      for ki, kg in enumerate(kgroups)])
            if j < GI:
                p.cp("vector", xi[:, j, dataoff:dataoff + Tc], pm[:, 0:Tc])
            else:
                p.cp("vector", z[:, j - GI, 0:Tc], pm[:, 0:Tc])
    cw3 = h["cw"][:, :].rearrange("p (g k) -> p g k", k=4)
    gt = buf["gtmp"]
    acc = dt
    for k in range(4):
        wk = cw3[:, :, (k if not rev else 3 - k): (k if not rev else 3 - k) + 1]
        cwb = wk.broadcast_to([128, GI, Tc])
        if k == 0:
            p.tt("vector", acc[:, :, 0:Tc], xi[:, :, k:k + Tc], cwb, ALU.mult)
        else:
            p.tt("vector", gt[:, :, 0:Tc], xi[:, :, k:k + Tc], cwb, ALU.mult)
            p.tt("vector", acc[:, :, 0:Tc], acc[:, :, 0:Tc], gt[:, :, 0:Tc],
                 ALU.add)
    p.cp("vector", buf["halo3"][:, :, :], xi[:, :, saveslice])
    p.tt("vector", acc[:, :, 0:Tc], acc[:, :, 0:Tc],
         h["cb"][:, :].unsqueeze(2).broadcast_to([128, GI, Tc]), ALU.add)
    p.act("scalar", xc[:, :, 0:Tc], acc[:, :, 0:Tc], AF.Silu)
    xp = h["xproj"]
    pDt, pB, pC = ctx.pmm[2], ctx.pmm[3], ctx.pmm[0]
    p.mm_acc(pDt[0:R, 0:Tc], [(xp[:, k, 0:R], xcb[:, k, 0:Tc]) for k in range(GI)])
    p.mm_acc(pB[0:16, 0:Tc], [(xp[:, k, R:R + 16], xcb[:, k, 0:Tc])
                              for k in range(GI)])
    p.mm_acc(pC[0:16, 0:Tc], [(xp[:, k, R + 16:R + 32], xcb[:, k, 0:Tc])
                              for k in range(GI)])
    p.cp("vector", sB[:, :], pB[0:16, 0:Tc])
    p.cp("vector", sC[:, :], pC[0:16, 0:Tc])
    p.cp("vector", dtrawb[:, :], pDt[0:R, 0:Tc])
    for m in range(GI):
        pm = ctx.pmm[1]
        p.mm(pm[:, 0:Tc], h["dtw"][:, m * 128:(m + 1) * 128], dtrawb[:, 0:Tc],
             True, True)
        dtb_col = h["dtb"][:, m:m + 1]
        e1 = ctx.tmp("mb_e1", [128, 256])
        p.act("scalar", e1[:, 0:Tc], pm[:, 0:Tc], AF.Abs, bias=dtb_col)
        p.act("scalar", e1[:, 0:Tc], e1[:, 0:Tc], AF.Exp, scale=-1.0)
        p.act("scalar", e1[:, 0:Tc], e1[:, 0:Tc], AF.Ln, bias=ctx.const_col(1.0))
        p.act("scalar", dt[:, m, 0:Tc], pm[:, 0:Tc], AF.Relu, bias=dtb_col)
        p.tt("vector", dt[:, m, 0:Tc], dt[:, m, 0:Tc], e1[:, 0:Tc], ALU.add)
    p.tt("vector", dtx[:, :, 0:Tc], dt[:, :, 0:Tc], xc[:, :, 0:Tc], ALU.mult)
    A = h["A"]
    at = buf["at"]
    h_all, B_ball, C_ball = buf["h_all"], buf["B_ball"], buf["C_ball"]
    for n in range(N):
        pBb = ctx.pbc[n % 2]
        pCb = ctx.pmm[2 + n % 2]
        seln = ctx.seln[:, n * 128:(n + 1) * 128]
        p.mm(pBb[:, 0:Tc], seln, sB[0:16, 0:Tc], True, True)
        p.mm(pCb[:, 0:Tc], seln, sC[0:16, 0:Tc], True, True)
        p.cp("vector", B_ball[:, n, 0:Tc], pBb[:, 0:Tc])
        p.cp("vector", C_ball[:, n, 0:Tc], pCb[:, 0:Tc])
    hk = _tn(h_all)
    hallk = [f"{hk}@{n}" for n in range(N)]
    yk_slice = y[:, 0:1, 0:1]
    for g in range(GI):
        p.op("vector", (lambda ha=h_all, dx=dtx, bb=B_ball, g=g:
                        lambda E: E.tensor_tensor(
                            ha[:, :, 0:Tc],
                            dx[:, g, 0:Tc].unsqueeze(1).broadcast_to(
                                [128, N, Tc]),
                            bb[:, :, 0:Tc], ALU.mult))(),
             reads=[dtx[:, 0:1, 0:1], B_ball[:, 0:1, 0:1]], writes=hallk,
             tag="ubulk")
        for n in range(N):
            a = at[n % 2]
            p.act("scalar", a[:, 0:Tc], dt[:, g, 0:Tc], AF.Exp,
                  scale=A[:, g * N + n: g * N + n + 1])
            hstk = f"{_tn(hst)}@{g}_{n}"
            if not rev:
                p.op("vector", (lambda ha=h_all, a=a, hs=hst, g=g, n=n:
                                lambda E: E.tensor_tensor_scan(
                                    ha[:, n, 0:Tc], a[:, 0:Tc],
                                    ha[:, n, 0:Tc], hs[:, g, n:n + 1],
                                    ALU.mult, ALU.add))(),
                     reads=[a, hstk, hallk[n]], writes=[hallk[n]], tag="scan")
                p.op("vector", (lambda ha=h_all, hs=hst, g=g, n=n:
                                lambda E: E.tensor_copy(
                                    hs[:, g, n:n + 1],
                                    ha[:, n, Tc - 1:Tc]))(),
                     reads=[hallk[n]], writes=[hstk], tag="hstcp")
            else:
                p.op("vector", (lambda ha=h_all, a=a, hs=hst, g=g, n=n:
                                lambda E: E.tensor_tensor_scan(
                                    ha[:, n, 0:Tc][:, ::-1],
                                    a[:, 0:Tc][:, ::-1],
                                    ha[:, n, 0:Tc][:, ::-1],
                                    hs[:, g, n:n + 1],
                                    ALU.mult, ALU.add))(),
                     reads=[a, hstk, hallk[n]], writes=[hallk[n]], tag="scan")
                p.op("vector", (lambda ha=h_all, hs=hst, g=g, n=n:
                                lambda E: E.tensor_copy(
                                    hs[:, g, n:n + 1], ha[:, n, 0:1]))(),
                     reads=[hallk[n]], writes=[hstk], tag="hstcp")
        p.op("vector", (lambda ha=h_all, cb=C_ball:
                        lambda E: E.tensor_tensor(
                            ha[:, :, 0:Tc], ha[:, :, 0:Tc],
                            cb[:, :, 0:Tc], ALU.mult))(),
             reads=[C_ball[:, 0:1, 0:1]] + hallk, writes=hallk, tag="cmul")
        w = N
        while w > 2:
            w //= 2
            p.op("vector", (lambda ha=h_all, w=w:
                            lambda E: E.tensor_tensor(
                                ha[:, 0:w, 0:Tc], ha[:, 0:w, 0:Tc],
                                ha[:, w:2 * w, 0:Tc], ALU.add))(),
                 reads=hallk, writes=hallk, tag="tree")
        p.op("vector", (lambda ha=h_all, yy=y, g=g:
                        lambda E: E.tensor_tensor(
                            yy[:, g, 0:Tc], ha[:, 0, 0:Tc],
                            ha[:, 1, 0:Tc], ALU.add))(),
             reads=hallk[0:2], writes=[yk_slice], tag="yg")
    p.tt("vector", gt[:, :, 0:Tc], xc[:, :, 0:Tc],
         h["Dp"][:, :].unsqueeze(2).broadcast_to([128, GI, Tc]), ALU.mult)
    p.tt("vector", y[:, :, 0:Tc], y[:, :, 0:Tc], gt[:, :, 0:Tc], ALU.add)
    zs = buf["at"][0]
    for g in range(GI):
        p.act("scalar", zs[:, 0:Tc], z[:, g, 0:Tc], AF.Silu)
        p.tt("vector", y[:, g, 0:Tc], y[:, g, 0:Tc], zs[:, 0:Tc], ALU.mult)
    p.cp("vector", yv_out[:, :, :], y[:, :, 0:Tc])


def mamba_bufs(ctx, GI, Tc, R, scrB):
    p = ctx.p
    b = {}
    b["xi"] = p.sb(ctx.n("mb_xi"), [128, GI, Tc + 3], F32)
    b["y"] = b["xi"][:, :, 0:Tc]
    b["halo3"] = p.sb(ctx.n("mb_halo"), [128, GI, 3], F32)
    b["z"] = p.sb(ctx.n("mb_z"), [128, GI, Tc], BF16)
    b["xc"] = p.sb(ctx.n("mb_xc"), [128, GI, Tc], BF16)
    b["xcb"] = b["xc"]
    b["dt"] = p.sb(ctx.n("mb_dt"), [128, GI, Tc], F32)
    b["dtx"] = ctx.tmp("scrA", [128, 4096])[:, 0:GI * Tc].rearrange(
        "p (a b) -> p a b", b=Tc)
    b["hst"] = p.sb(ctx.n("mb_hst"), [128, GI, 16], F32)
    b["sB"] = p.sb(ctx.n("mb_sB"), [16, Tc], BF16)
    b["sC"] = p.sb(ctx.n("mb_sC"), [16, Tc], BF16)
    b["sDt"] = None
    b["dtrawb"] = p.sb(ctx.n("mb_dtraw"), [R, Tc], BF16)
    b["at"] = [p.sb(ctx.n(f"mb_at{i}"), [128, Tc], F32) for i in range(2)]
    b["h_all"] = p.sb(ctx.n("mb_hall"), [128, 16, Tc], F32)
    b["B_ball"] = p.sb(ctx.n("mb_Bball"), [128, 16, Tc], BF16)
    b["C_ball"] = p.sb(ctx.n("mb_Cball"), [128, 16, Tc], BF16)

    b["gtmp"] = ctx.tmp("scrA", [128, 4096])[:, 0:GI * Tc].rearrange(
        "p (a b) -> p a b", b=Tc)
    return b


def _halo_save(ctx, cfg, buf):
    p = ctx.p
    Tc = ctx.Tc
    if not cfg.rev:
        p.cp("vector", buf["halo3"][:, :, :], buf["xi"][:, :, Tc:Tc + 3])
    else:
        p.cp("vector", buf["halo3"][:, :, :], buf["xi"][:, :, 0:3])


def emit_l0_dir(ctx, x_dram, xo_dram, W, rev):
    p = ctx.p
    T, Tc = ctx.T, ctx.Tc
    ntt = Tc // 128
    nch = T // Tc
    GI, R = 8, 32
    cfg = MambaCfg("l0b" if rev else "l0f", 512, 1024, 32, rev, W)
    h = mamba_dir_setup(ctx, cfg)
    outw = ctx.wload(W["outw_rhs"], 1024, 512, cfg.tag + "ow")
    gam = ctx.row_tile(W["gam_row"], 1024, cfg.tag + "g")
    bet = ctx.row_tile(W["beta_row"], 1024, cfg.tag + "b")
    scrB = ctx.tmp("scrB_bf16", [128, 4096], BF16)

    xt = p.sb(ctx.n("l0_xt"), [128, ntt, 1024], F32)
    xn_sc = p.sb(ctx.n("l0_xn"), [128, ntt, 1024], F32)
    xn2b = scrB[:, 0:ntt * 1024].rearrange("p (a b) -> p a b", b=1024)
    xn2T = p.sb(ctx.n("l0_xn2T"), [128, 8, Tc], BF16)
    buf = mamba_bufs(ctx, GI, Tc, R, scrB)
    yv = scrB[:, 0:GI * Tc].rearrange("p (a b) -> p a b", b=Tc)
    ot = p.sb(ctx.n("l0_ot"), [128, ntt, 512], F32)

    kgroups = [4, 5, 6, 7] if rev else [0, 1, 2, 3]
    c0, c1 = (0, 512) if not rev else (512, 1024)
    order = range(nch - 1, -1, -1) if rev else range(nch)
    for ci in order:
        t0 = ci * Tc
        for tt in range(ntt):
            r0 = t0 + tt * 128
            p.dma("sync", xt[:, tt, :], x_dram[r0:r0 + 128, :])
        ln_chunk(ctx, xt[:, 0:ntt, :], ntt, 1024, xn_sc[:, 0:ntt, :])
        ln_chunk(ctx, xn_sc[:, 0:ntt, :], ntt, 1024, xn2b[:, 0:ntt, :],
                 gam_row=gam, beta_row=bet)
        ctx.transpose_tmaj_to_dmaj(xn2T, xn2b, ntt, 8)
        mamba_dir_chunk(ctx, cfg, h, buf, xn2T, kgroups, ci, nch, yv)
        for m in range(ntt):
            pm = ctx.pmm[m % 2]
            p.mm_acc(pm[:, 0:512],
                     [(yv[:, k, m * 128:(m + 1) * 128], outw[:, k, 0:512])
                      for k in range(GI)])
            p.tt("vector", ot[:, m, :], xt[:, m, c0:c1], pm[:, 0:512], ALU.add)
            p.tt("vector", ot[:, m, :], ot[:, m, :], xn_sc[:, m, c0:c1], ALU.add)
        for tt in range(ntt):
            r0 = t0 + tt * 128
            p.dma("sync", xo_dram[r0:r0 + 128, c0:c1], ot[:, tt, :],
                  out_is_output=True)


def _stream_upproj(ctx, wdram, KG, m, rhs_tensor, GI, Tc, psum, wb):
    p = ctx.p
    wt = wb[m % 2]
    for k in range(KG):
        p.dma("sync", wt[:, k, :], wdram[k * 128:(k + 1) * 128,
                                         m * 128:(m + 1) * 128])
    p.mm_acc(psum[:, 0:Tc], [(wt[:, k, :], rhs_tensor[:, k, 0:Tc])
                             for k in range(KG)])


def emit_l1_phA(ctx, x_dram, W, rev, xcd, zd, tag):
    p = ctx.p
    T = ctx.T
    Tc = 512
    ntt = Tc // 128
    nch = T // Tc
    GI = 16
    inw = ctx.wload(W["inw_lhsT"], 1024, 4096, tag + "inwA")
    cw = ctx.vload(W["conv_w"], GI * 4, tag + "cwA")
    cb = ctx.vload(W["conv_b"], GI, tag + "cbA")
    gam = ctx.row_tile(W["gam_row"], 1024, tag + "gA", dt=BF16)
    bet = ctx.row_tile(W["beta_row"], 1024, tag + "bA", dt=BF16)
    scrB = ctx.tmp("scrB_bf16", [128, 4096], BF16)
    xt = p.sb(ctx.n(tag + "A_xt"), [128, ntt, 1024], F32)
    xn2b = scrB[:, 0:ntt * 1024].rearrange("p (a b) -> p a b", b=1024)
    xn2T = [p.sb(ctx.n(tag + f"A_x2T{i}"), [128, 8, Tc], BF16) for i in range(2)]
    xi = p.sb(ctx.n(tag + "A_xi"), [128, GI, Tc + 3], F32)
    halo3 = p.sb(ctx.n(tag + "A_halo"), [128, GI, 3], F32)
    zrot = [p.sb(ctx.n(tag + f"A_zr{i}"), [128, Tc], BF16) for i in range(2)]
    xcb = p.sb(ctx.n(tag + "A_xcb"), [128, 8, Tc], BF16)
    cw3 = cw[:, :].rearrange("p (g k) -> p g k", k=4)
    conv_acc = ctx.tmp("scrA", [128, 4096])[:, :].rearrange(
        "p (g t) -> p g t", t=Tc)

    dataoff = 3 if not rev else 0
    haloslice = (slice(0, 3) if not rev else slice(Tc, Tc + 3))
    saveslice = (slice(Tc, Tc + 3) if not rev else slice(0, 3))
    order = range(nch - 1, -1, -1) if rev else range(nch)
    first = True
    for ci in order:
        t0 = ci * Tc
        for tt in range(ntt):
            p.dma("sync", xt[:, tt, :], x_dram[t0 + tt * 128:t0 + (tt + 1) * 128, :])
        ln_chunk(ctx, xt[:, 0:ntt, :], ntt, 1024, xt[:, 0:ntt, :])
        ln_chunk(ctx, xt[:, 0:ntt, :], ntt, 1024, xn2b[:, 0:ntt, :],
                 gam_row=gam, beta_row=bet)
        x2T = xn2T[ci % 2]
        ctx.transpose_tmaj_to_dmaj(x2T, xn2b, ntt, 8)
        if first:
            p.memset("vector", xi[:, :, haloslice], 0.0)
        else:
            p.cp("vector", xi[:, :, haloslice], halo3[:, :, :])
        for j in range(2 * GI):
            pm = ctx.pmm[j % 4]
            p.mm_acc(pm[:, 0:Tc],
                     [(inw[:, k, j * 128:(j + 1) * 128], x2T[:, k, 0:Tc])
                      for k in range(8)])
            if j < GI:
                p.cp("vector", xi[:, j, dataoff:dataoff + Tc], pm[:, 0:Tc])
            else:
                zr = zrot[j % 2]
                p.cp("vector", zr[:, 0:Tc], pm[:, 0:Tc])
                p.dma("scalar" if j % 2 else "sync",
                      zd[:, j - GI, t0:t0 + Tc], zr[:, 0:Tc])
        p.cp("vector", halo3[:, :, :], xi[:, :, saveslice])
        for gh in range(2):
            g0 = gh * 8
            acc = conv_acc
            for k in range(4):
                kk = k if not rev else 3 - k
                cwb = cw3[:, g0:g0 + 8, kk:kk + 1].broadcast_to([128, 8, Tc])
                if k == 0:
                    p.tt("vector", acc[:, :, :], xi[:, g0:g0 + 8, 0:Tc], cwb,
                         ALU.mult)
                else:
                    gt2 = xt[:, :, :].rearrange("p a b -> p (a b)").rearrange(
                        "p (g t) -> p g t", t=Tc)
                    p.tt("vector", gt2[:, :, 0:Tc], xi[:, g0:g0 + 8, k:k + Tc],
                         cwb, ALU.mult)
                    p.tt("vector", acc[:, :, :], acc[:, :, :], gt2[:, :, 0:Tc],
                         ALU.add)
            p.tt("vector", acc[:, :, :], acc[:, :, :],
                 cb[:, g0:g0 + 8].unsqueeze(2).broadcast_to([128, 8, Tc]), ALU.add)
            p.act("scalar", xcb[:, 0:8, 0:Tc], acc[:, :, :], AF.Silu)
            for g in range(8):
                p.dma("scalar" if g % 2 else "sync",
                      xcd[:, g0 + g, t0:t0 + Tc], xcb[:, g, 0:Tc])
        first = False


def emit_l1_phB(ctx, x_dram, W, rev, xcd, zd, tag, yf_dram, xo_dram):
    p = ctx.p
    T = ctx.T
    Tc = 256
    ntt = Tc // 128
    nch = T // Tc
    GI, R, N = 16, 64, 16
    xp = ctx.wload(W["xproj_lhsT"], 2048, R + 32, tag + "xpB")
    dtw = p.sb(ctx.n(tag + "dtwB"), [R, 2048], BF16)
    mo = 0
    while mo < 2048:
        mw = min(512, 2048 - mo)
        st = ctx.wstage[0]
        p.dma("sync", st[0:R, 0:mw], W["dtw_lhsT"][:, mo:mo + mw])
        p.cp("gpsimd", dtw[:, mo:mo + mw], st[0:R, 0:mw])
        mo += mw
    dtb = ctx.vload(W["dt_b"], GI, tag + "dtbB")
    Dp = ctx.vload(W["D"], GI, tag + "DpB")
    Alog = ctx.vload(W["A_log"], GI * 16, tag + "AlB")
    A = p.sb(ctx.n(tag + "AB"), [128, GI * 16], F32)
    p.act("scalar", A[:, :], Alog[:, :], AF.Exp)
    p.ts("vector", A[:, :], A[:, :], -1.0, None, ALU.mult)
    outwb_dram = ctx.to_bf16_dram(W["outw_lhsT"], 2048, 1024, tag + "_outwbB")
    upwb = [p.sb(ctx.n(tag + f"upwB{i}"), [128, 16, 128], BF16) for i in range(2)]
    scrB = ctx.tmp("scrB_bf16", [128, 4096], BF16)
    xcb = [p.sb(ctx.n(tag + f"B_xc{i}"), [128, GI, Tc], BF16) for i in range(2)]
    zb = [p.sb(ctx.n(tag + "B_z0"), [128, GI, Tc], BF16)] * 2
    if rev:
        dt = [p.sb(ctx.n(tag + "B_dt0"), [128, GI, Tc], F32)] * 2
    else:
        dt = [p.sb(ctx.n(tag + f"B_dt{i}"), [128, GI, Tc], F32) for i in range(2)]
    dtx = [ctx.tmp("scrA", [128, 4096])[:, 0:GI * Tc].rearrange(
        "p (a b) -> p a b", b=Tc),
           p.sb(ctx.n(tag + "B_dtx1"), [128, GI, Tc], F32)]
    y = p.sb(ctx.n(tag + "B_y"), [128, GI, Tc], F32)
    hst = p.sb(ctx.n(tag + "B_hst"), [128, GI, N], F32)
    sB = p.sb(ctx.n(tag + "B_sB"), [16, Tc], BF16)
    sC = p.sb(ctx.n(tag + "B_sC"), [16, Tc], BF16)
    dtrawb = p.sb(ctx.n(tag + "B_dtr"), [R, Tc], BF16)
    at = [p.sb(ctx.n(tag + f"B_at{i}"), [128, Tc], F32) for i in range(2)]
    h_all = p.sb(ctx.n(tag + "B_hall"), [128, 16, Tc], F32)
    B_ball = [p.sb(ctx.n(tag + f"B_Bb{i}"), [128, 16, Tc], BF16) for i in range(2)]
    C_ball = [p.sb(ctx.n(tag + f"B_Cb{i}"), [128, 16, Tc], BF16) for i in range(2)]
    yv = scrB[:, 0:GI * Tc].rearrange("p (a b) -> p a b", b=Tc)
    yfT = p.sb(ctx.n(tag + "B_yfT"), [128, 8, Tc], BF16)
    if rev:
        ybT = p.sb(ctx.n(tag + "B_ybT"), [128, 8, Tc], BF16)
        mbr = ctx.row_tile(W["merge_b_row"], 1024, tag + "mbB", dt=BF16)
        mergeb_dram = ctx.to_bf16_dram(W["merge_rhs"], 2048, 1024,
                                       tag + "_mergebB")
        mgb = [p.sb(ctx.n(tag + f"mgbB{i}"), [128, 512], BF16) for i in range(2)]
        xt = p.sb(ctx.n(tag + "B_xt"), [128, ntt, 1024], F32)
        xn_sc = scrB[:, 0:ntt * 1024].rearrange("p (a b) -> p a b", b=1024)

    p.memset("vector", hst[:, :, :], 0.0)
    order = range(nch - 1, -1, -1) if rev else range(nch)
    for ci in order:
        t0 = ci * Tc
        xc_c = xcb[ci % 2]
        z_c = zb[ci % 2]
        dt_c = dt[ci % 2]
        for g in range(GI):
            p.dma("scalar" if g % 2 else "sync", xc_c[:, g, 0:Tc],
                  xcd[:, g, t0:t0 + Tc])
            p.dma("scalar" if g % 2 else "sync", z_c[:, g, 0:Tc],
                  zd[:, g, t0:t0 + Tc])
        pDt, pB, pC = ctx.pmm[2], ctx.pmm[3], ctx.pmm[0]
        p.mm_acc(pDt[0:R, 0:Tc], [(xp[:, k, 0:R], xc_c[:, k, 0:Tc])
                                  for k in range(GI)])
        p.mm_acc(pB[0:16, 0:Tc], [(xp[:, k, R:R + 16], xc_c[:, k, 0:Tc])
                                  for k in range(GI)])
        p.mm_acc(pC[0:16, 0:Tc], [(xp[:, k, R + 16:R + 32], xc_c[:, k, 0:Tc])
                                  for k in range(GI)])
        p.cp("vector", sB[:, :], pB[0:16, 0:Tc])
        p.cp("vector", sC[:, :], pC[0:16, 0:Tc])
        p.cp("vector", dtrawb[:, :], pDt[0:R, 0:Tc])
        for m in range(GI):
            pm = ctx.pmm[1]
            p.mm(pm[:, 0:Tc], dtw[:, m * 128:(m + 1) * 128], dtrawb[:, 0:Tc],
                 True, True)
            dtb_col = dtb[:, m:m + 1]
            e1 = ctx.tmp("mb_e1", [128, 256])
            p.act("scalar", e1[:, 0:Tc], pm[:, 0:Tc], AF.Abs, bias=dtb_col)
            p.act("scalar", e1[:, 0:Tc], e1[:, 0:Tc], AF.Exp, scale=-1.0)
            p.act("scalar", e1[:, 0:Tc], e1[:, 0:Tc], AF.Ln,
                  bias=ctx.const_col(1.0))
            p.act("scalar", dt_c[:, m, 0:Tc], pm[:, 0:Tc], AF.Relu, bias=dtb_col)
            p.tt("vector", dt_c[:, m, 0:Tc], dt_c[:, m, 0:Tc], e1[:, 0:Tc],
                 ALU.add)
        dtx_c = dtx[ci % 2]
        Bb_c = B_ball[ci % 2]
        Cb_c = C_ball[ci % 2]
        p.tt("vector", dtx_c[:, :, 0:Tc], dt_c[:, :, 0:Tc], xc_c[:, :, 0:Tc],
             ALU.mult)
        hk = _tn(h_all)
        hallk = [f"{hk}@{n}" for n in range(N)]
        for n in range(N):
            pBb = ctx.pbc[n % 2]
            pCb = ctx.pmm[2 + n % 2]
            seln = ctx.seln[:, n * 128:(n + 1) * 128]
            p.mm(pBb[:, 0:Tc], seln, sB[0:16, 0:Tc], True, True)
            p.mm(pCb[:, 0:Tc], seln, sC[0:16, 0:Tc], True, True)
            p.cp("vector", Bb_c[:, n, 0:Tc], pBb[:, 0:Tc])
            p.cp("vector", Cb_c[:, n, 0:Tc], pCb[:, 0:Tc])
        for g in range(GI):
            p.op("vector", (lambda ha=h_all, dx=dtx_c, bb=Bb_c, g=g:
                            lambda E: E.tensor_tensor(
                                ha[:, :, 0:Tc],
                                dx[:, g, 0:Tc].unsqueeze(1).broadcast_to(
                                    [128, N, Tc]),
                                bb[:, :, 0:Tc], ALU.mult))(),
                 reads=[dtx_c[:, 0:1, 0:1], Bb_c[:, 0:1, 0:1]], writes=hallk,
                 tag="ubulk")
            for n in range(N):
                a = at[n % 2]
                p.act("scalar", a[:, 0:Tc], dt_c[:, g, 0:Tc], AF.Exp,
                      scale=A[:, g * N + n: g * N + n + 1])
                hstk = f"{_tn(hst)}@{g}_{n}"
                if not rev:
                    p.op("vector", (lambda ha=h_all, a=a, hs=hst, g=g, n=n:
                                    lambda E: E.tensor_tensor_scan(
                                        ha[:, n, 0:Tc], a[:, 0:Tc],
                                        ha[:, n, 0:Tc], hs[:, g, n:n + 1],
                                        ALU.mult, ALU.add))(),
                         reads=[a, hstk, hallk[n]], writes=[hallk[n]],
                         tag="scan")
                    p.op("vector", (lambda ha=h_all, hs=hst, g=g, n=n:
                                    lambda E: E.tensor_copy(
                                        hs[:, g, n:n + 1],
                                        ha[:, n, Tc - 1:Tc]))(),
                         reads=[hallk[n]], writes=[hstk], tag="hstcp")
                else:
                    p.op("vector", (lambda ha=h_all, a=a, hs=hst, g=g, n=n:
                                    lambda E: E.tensor_tensor_scan(
                                        ha[:, n, 0:Tc][:, ::-1],
                                        a[:, 0:Tc][:, ::-1],
                                        ha[:, n, 0:Tc][:, ::-1],
                                        hs[:, g, n:n + 1],
                                        ALU.mult, ALU.add))(),
                         reads=[a, hstk, hallk[n]], writes=[hallk[n]],
                         tag="scan")
                    p.op("vector", (lambda ha=h_all, hs=hst, g=g, n=n:
                                    lambda E: E.tensor_copy(
                                        hs[:, g, n:n + 1], ha[:, n, 0:1]))(),
                         reads=[hallk[n]], writes=[hstk], tag="hstcp")
            p.op("vector", (lambda ha=h_all, cb=Cb_c:
                            lambda E: E.tensor_tensor(
                                ha[:, :, 0:Tc], ha[:, :, 0:Tc],
                                cb[:, :, 0:Tc], ALU.mult))(),
                 reads=[Cb_c[:, 0:1, 0:1]] + hallk, writes=hallk, tag="cmul")
            w = N
            while w > 2:
                w //= 2
                p.op("vector", (lambda ha=h_all, w=w:
                                lambda E: E.tensor_tensor(
                                    ha[:, 0:w, 0:Tc], ha[:, 0:w, 0:Tc],
                                    ha[:, w:2 * w, 0:Tc], ALU.add))(),
                     reads=hallk, writes=hallk, tag="tree")
            p.op("vector", (lambda ha=h_all, yy=y, g=g:
                            lambda E: E.tensor_tensor(
                                yy[:, g, 0:Tc], ha[:, 0, 0:Tc],
                                ha[:, 1, 0:Tc], ALU.add))(),
                 reads=hallk[0:2], writes=[y[:, 0:1, 0:1]], tag="yg")
        gt = dtx_c
        p.tt("vector", gt[:, :, 0:Tc], xc_c[:, :, 0:Tc],
             Dp[:, :].unsqueeze(2).broadcast_to([128, GI, Tc]), ALU.mult)
        p.tt("vector", y[:, :, 0:Tc], y[:, :, 0:Tc], gt[:, :, 0:Tc], ALU.add)
        zs = at[0]
        for g in range(GI):
            p.act("scalar", zs[:, 0:Tc], z_c[:, g, 0:Tc], AF.Silu)
            p.tt("vector", y[:, g, 0:Tc], y[:, g, 0:Tc], zs[:, 0:Tc], ALU.mult)
        p.cp("vector", yv[:, :, :], y[:, :, 0:Tc])
        for m in range(8):
            pm = ctx.pmm[m % 2]
            _stream_upproj(ctx, outwb_dram, GI, m, yv, GI, Tc, pm, upwb)
            if not rev:
                p.cp("vector", yfT[:, m, :], pm[:, 0:Tc])
                p.dma("sync", yf_dram[:, m, t0:t0 + Tc], yfT[:, m, :],
                      out_is_output=True)
            else:
                p.cp("vector", ybT[:, m, :], pm[:, 0:Tc])
        if rev:
            for tt in range(ntt):
                p.dma("sync", xt[:, tt, :],
                      x_dram[t0 + tt * 128:t0 + (tt + 1) * 128, :])
            ln_chunk(ctx, xt[:, 0:ntt, :], ntt, 1024, xn_sc[:, 0:ntt, :])
            for m in range(8):
                p.dma("gpsimd", yfT[:, m, :], yf_dram[:, m, t0:t0 + Tc])
            for mt in range(ntt):
                for no in range(0, 1024, 512):
                    pm = ctx.pmm[2 + (no // 512) % 2]
                    for k in range(16):
                        mg = mgb[k % 2]
                        p.dma("sync", mg[:, :],
                              mergeb_dram[k * 128:(k + 1) * 128, no:no + 512])
                        src = yfT if k < 8 else ybT
                        p.mm(pm[:, 0:512],
                             src[:, k % 8, mt * 128:(mt + 1) * 128], mg[:, :],
                             k == 0, k == 15)
                    p.tt("vector", xt[:, mt, no:no + 512], xt[:, mt, no:no + 512],
                         pm[:, 0:512], ALU.add)
                    p.tt("vector", xt[:, mt, no:no + 512], xt[:, mt, no:no + 512],
                         mbr[:, no:no + 512], ALU.add)
                    p.tt("vector", xt[:, mt, no:no + 512], xt[:, mt, no:no + 512],
                         xn_sc[:, mt, no:no + 512], ALU.add)
            for tt in range(ntt):
                p.dma("sync", xo_dram[t0 + tt * 128:t0 + (tt + 1) * 128, :],
                      xt[:, tt, :], out_is_output=True)


def emit_l1_fwd(ctx, x_dram, yf_dram, W):
    p = ctx.p
    T, Tc = ctx.T, ctx.Tc
    ntt = Tc // 128
    nch = T // Tc
    GI, R = 16, 64
    cfg = MambaCfg("l1f", 1024, 2048, 64, False, W, stream_inw=True)
    inwb_dram = ctx.to_bf16_dram(W["inw_lhsT"], 1024, 4096, "l1fw_inwb")
    outwb_dram = ctx.to_bf16_dram(W["outw_lhsT"], 2048, 1024, "l1fw_outwb")
    h = mamba_dir_setup(ctx, cfg)
    gam = ctx.row_tile(W["gam_row"], 1024, "l1g", dt=BF16)
    bet = ctx.row_tile(W["beta_row"], 1024, "l1b", dt=BF16)
    scrB = ctx.tmp("scrB_bf16", [128, 4096], BF16)
    xt = p.sb(ctx.n("l1_xt"), [128, ntt, 1024], F32)
    xn_sc = p.sb(ctx.n("l1_xn"), [128, ntt, 1024], F32)
    xn2b = scrB[:, 0:ntt * 1024].rearrange("p (a b) -> p a b", b=1024)
    xn2T = p.sb(ctx.n("l1_xn2T"), [128, 8, Tc], BF16)
    buf = mamba_bufs(ctx, GI, Tc, R, scrB)
    buf["wjk"] = [p.sb(ctx.n(f"l1wjk{i}"), [128, 8, 128], BF16) for i in range(2)]
    buf["inwb_dram"] = inwb_dram
    upwb = [p.sb(ctx.n("upw0"), [128, 16, 128], BF16)] * 2
    yv = scrB[:, 0:GI * Tc].rearrange("p (a b) -> p a b", b=Tc)
    yfT = p.sb(ctx.n("l1_yfT"), [128, 8, Tc], BF16)
    kgroups = list(range(8))
    for ci in range(nch):
        t0 = ci * Tc
        for tt in range(ntt):
            r0 = t0 + tt * 128
            p.dma("sync", xt[:, tt, :], x_dram[r0:r0 + 128, :])
        ln_chunk(ctx, xt[:, 0:ntt, :], ntt, 1024, xn_sc[:, 0:ntt, :])
        ln_chunk(ctx, xn_sc[:, 0:ntt, :], ntt, 1024, xn2b[:, 0:ntt, :],
                 gam_row=gam, beta_row=bet)
        ctx.transpose_tmaj_to_dmaj(xn2T, xn2b, ntt, 8)
        mamba_dir_chunk(ctx, cfg, h, buf, xn2T, kgroups, ci, nch, yv)
        for m in range(8):
            pm = ctx.pmm[m % 2]
            _stream_upproj(ctx, outwb_dram, GI, m, yv, GI, Tc, pm, upwb)
            p.cp("vector", yfT[:, m, :], pm[:, 0:Tc])
            p.dma("sync", yf_dram[:, m, t0:t0 + Tc], yfT[:, m, :],
                  out_is_output=True)


def emit_l1_bwd(ctx, x_dram, yf_dram, xo_dram, W):
    p = ctx.p
    T, Tc = ctx.T, ctx.Tc
    ntt = Tc // 128
    nch = T // Tc
    GI, R = 16, 64
    cfg = MambaCfg("l1b", 1024, 2048, 64, True, W, stream_inw=True)
    inwb_dram = ctx.to_bf16_dram(W["inw_lhsT"], 1024, 4096, "l1bw_inwb")
    outwb_dram = ctx.to_bf16_dram(W["outw_lhsT"], 2048, 1024, "l1bw_outwb")
    h = mamba_dir_setup(ctx, cfg)
    gam = ctx.row_tile(W["gam_row"], 1024, "l1g", dt=BF16)
    bet = ctx.row_tile(W["beta_row"], 1024, "l1b", dt=BF16)
    mbr = ctx.row_tile(W["merge_b_row"], 1024, "l1mb")
    mergeb_dram = ctx.to_bf16_dram(W["merge_rhs"], 2048, 1024, "l1_mergeb")
    mgb = [p.sb(ctx.n(f"l1mgb{i}"), [128, 512], BF16) for i in range(2)]
    scrB = ctx.tmp("scrB_bf16", [128, 4096], BF16)
    xt = p.sb(ctx.n("l1_xt"), [128, ntt, 1024], F32)
    xn_sc = p.sb(ctx.n("l1_xn"), [128, ntt, 1024], BF16)
    xn2b = scrB[:, 0:ntt * 1024].rearrange("p (a b) -> p a b", b=1024)
    xn2T = p.sb(ctx.n("l1_xn2T"), [128, 8, Tc], BF16)
    buf = mamba_bufs(ctx, GI, Tc, R, scrB)
    buf["wjk"] = [p.sb(ctx.n(f"l1wjk{i}"), [128, 8, 128], BF16) for i in range(2)]
    buf["inwb_dram"] = inwb_dram
    upwb = [p.sb(ctx.n("upw0"), [128, 16, 128], BF16)] * 2
    yv = scrB[:, 0:GI * Tc].rearrange("p (a b) -> p a b", b=Tc)
    yfT = p.sb(ctx.n("l1_yfT"), [128, 8, Tc], BF16)
    ybT = p.sb(ctx.n("l1_ybT"), [128, 8, Tc], BF16)
    ot = xt
    kgroups = list(range(8))
    for ci in range(nch - 1, -1, -1):
        t0 = ci * Tc
        for tt in range(ntt):
            r0 = t0 + tt * 128
            p.dma("sync", xt[:, tt, :], x_dram[r0:r0 + 128, :])
        ln_chunk(ctx, xt[:, 0:ntt, :], ntt, 1024, xn_sc[:, 0:ntt, :])
        ln_chunk(ctx, xn_sc[:, 0:ntt, :], ntt, 1024, xn2b[:, 0:ntt, :],
                 gam_row=gam, beta_row=bet)
        ctx.transpose_tmaj_to_dmaj(xn2T, xn2b, ntt, 8)
        mamba_dir_chunk(ctx, cfg, h, buf, xn2T, kgroups, ci, nch, yv)
        for m in range(8):
            pm = ctx.pmm[m % 2]
            _stream_upproj(ctx, outwb_dram, GI, m, yv, GI, Tc, pm, upwb)
            p.cp("vector", ybT[:, m, :], pm[:, 0:Tc])
            p.dma("sync", yfT[:, m, :], yf_dram[:, m, t0:t0 + Tc])
        for mt in range(ntt):
            for no in range(0, 1024, 512):
                pm = ctx.pmm[2 + (no // 512) % 2]
                for k in range(16):
                    mg = mgb[k % 2]
                    p.dma("sync", mg[:, :],
                          mergeb_dram[k * 128:(k + 1) * 128, no:no + 512])
                    src = yfT if k < 8 else ybT
                    p.mm(pm[:, 0:512],
                         src[:, k % 8, mt * 128:(mt + 1) * 128], mg[:, :],
                         k == 0, k == 15)
                p.tt("vector", ot[:, mt, no:no + 512], pm[:, 0:512],
                     xt[:, mt, no:no + 512], ALU.add)
                p.tt("vector", ot[:, mt, no:no + 512], ot[:, mt, no:no + 512],
                     mbr[:, no:no + 512], ALU.add)
                p.tt("vector", ot[:, mt, no:no + 512], ot[:, mt, no:no + 512],
                     xn_sc[:, mt, no:no + 512], ALU.add)
        for tt in range(ntt):
            r0 = t0 + tt * 128
            p.dma("sync", xo_dram[r0:r0 + 128, :], ot[:, tt, :], out_is_output=True)



def router_chunk(ctx, xnT, Wr_sb, E, ntt, brow=None):
    p = ctx.p
    Tc = ctx.Tc
    logits = ctx.tmp("rt_logits", [128, 4, 16])
    probs = ctx.tmp("rt_probs", [128, 4, 16])
    nmax = ctx.tmp("rt_nmax", [128, 1])
    esum = ctx.tmp("rt_esum", [128, 1])
    for tt in range(ntt):
        pm = ctx.pbc[0]
        p.mm_acc(pm[:, 0:E], [(xnT[:, k, tt * 128:(tt + 1) * 128],
                               Wr_sb[:, k, 0:E]) for k in range(8)])
        if brow is None:
            p.cp("vector", logits[:, tt, 0:E], pm[:, 0:E])
        else:
            p.tt("vector", logits[:, tt, 0:E], pm[:, 0:E], brow[:, 0:E], ALU.add)
        p.red("vector", nmax[:, :], logits[:, tt, 0:E], ALU.max, negate=True)
        p.act("scalar", probs[:, tt, 0:E], logits[:, tt, 0:E], AF.Exp,
              bias=nmax[:, :], accum_out=esum[:, :])
        p.recip("vector", esum[:, :], esum[:, :])
        p.ts("vector", probs[:, tt, 0:E], probs[:, tt, 0:E], esum[:, 0:1], None,
             ALU.mult)
    return probs, logits


def topk_masks(ctx, vals_ap, E, k, out_oh, out_m, tt):
    p = ctx.p
    m8 = ctx.tmp("tk_m8", [128, 8])
    mr1 = ctx.tmp("tk_mr1", [128, 16])
    mr2 = ctx.tmp("tk_mr2", [128, 16])
    p.red("vector", out_m[:, tt, 0:1], vals_ap, ALU.max)
    p.memset("vector", m8[:, :], -1e38)
    p.cp("vector", m8[:, 0:1], out_m[:, tt, 0:1])
    p.mr("vector", mr1[:, 0:E], m8[:, :], vals_ap, -1e30)
    p.tt("vector", out_oh[:, tt, 0:E], vals_ap, mr1[:, 0:E], ALU.not_equal)
    if k == 2:
        p.red("vector", out_m[:, tt, 1:2], mr1[:, 0:E], ALU.max)
        p.memset("vector", m8[:, :], -1e38)
        p.cp("vector", m8[:, 0:1], out_m[:, tt, 1:2])
        p.mr("vector", mr2[:, 0:E], m8[:, :], mr1[:, 0:E], -1e30)
        p.tt("vector", out_oh[:, tt, E:2 * E], mr1[:, 0:E], mr2[:, 0:E],
             ALU.not_equal)


def sel_to_row(ctx, sel_col_ap, selrow, tt):
    ctx.p.dma("sync", selrow[0:1, tt * 128:(tt + 1) * 128], sel_col_ap)


def emit_l3_ffn(ctx, x_dram, xo_dram, W):
    p = ctx.p
    T, Tc = ctx.T, ctx.Tc
    ntt = Tc // 128
    nch = T // Tc
    NTT = T // 128
    E = 8
    wr = ctx.wload(W["router_wT"], 1024, E, "l3r")
    brow = ctx.row_tile(W["router_b_row"], E, "l3rb")
    scrB = ctx.tmp("scrB_bf16", [128, 4096], BF16)
    xt = p.sb(ctx.n("f3_xt"), [128, ntt, 1024], F32)
    xnb = scrB[:, 0:ntt * 1024].rearrange("p (a b) -> p a b", b=1024)
    xnTf = p.sb(ctx.n("f3_xnTf"), [128, 8, T], BF16)
    comb = p.sb(ctx.n("f3_comb"), [128, NTT, E], F32)
    oh = ctx.tmp("tk_oh", [128, 4, 32])
    mx = ctx.tmp("tk_mx", [128, 4, 2])
    ot = [p.sb(ctx.n(f"f3_ot{i}"), [128, ntt, 1024], F32) for i in range(2)]
    we = [p.sb(ctx.n(f"l3we{i}"), [128, 8, 1024], BF16) for i in range(2)]
    ber = [p.sb(ctx.n(f"l3ber{i}"), [128, 1024], F32) for i in range(2)]
    rowbuf = ctx.rowbuf
    for ci in range(nch):
        t0 = ci * Tc
        p.dma("gpsimd", xo_dram[t0:t0 + Tc, :], x_dram[t0:t0 + Tc, :],
              wkey=f"f3xo@0_{ci}")
        for tt in range(ntt):
            p.dma("sync", xt[:, tt, :], x_dram[t0 + tt * 128:t0 + (tt + 1) * 128, :])
        ln_chunk(ctx, xt[:, 0:ntt, :], ntt, 1024, xnb[:, 0:ntt, :])
        ctx.transpose_tmaj_to_dmaj(xnTf, xnb, ntt, 8, dst_t0=t0)
        probs, logits = router_chunk(ctx, xnTf[:, :, t0:t0 + Tc], wr, E, ntt,
                                     brow=brow)
        for tt in range(ntt):
            topk_masks(ctx, logits[:, tt, 0:E], E, 1, oh, mx, tt)
            p.red("vector", mx[:, tt, 1:2], probs[:, tt, 0:E], ALU.max)
            p.ts("vector", comb[:, ci * ntt + tt, 0:E], oh[:, tt, 0:E],
                 mx[:, tt, 1:2], None, ALU.mult)
    ctx.wload_into(we[0], W["expert_wT"][0:1024, :], 1024, 1024, eng="vector")
    ctx.row_tile_into(ber[0], W["expert_b"][0:1, :], 1024, rowbuf)
    for e in range(E):
        if e + 1 < E:
            ctx.wload_into(we[(e + 1) % 2],
                           W["expert_wT"][(e + 1) * 1024:(e + 2) * 1024, :],
                           1024, 1024, eng="vector")
            ctx.row_tile_into(ber[(e + 1) % 2], W["expert_b"][e + 1:e + 2, :],
                              1024, rowbuf)
        for ci in range(nch):
            t0 = ci * Tc
            o = ot[ci % 2]
            for tt in range(ntt):
                p.dma("scalar", o[:, tt, :],
                      xo_dram[t0 + tt * 128:t0 + (tt + 1) * 128, :],
                      rkey=f"f3xo@{e}_{ci}")
            for mt in range(ntt):
                for no in range(0, 1024, 512):
                    pm = ctx.pmm[(mt * 2 + no // 512) % 4]
                    p.mm_acc(pm[:, 0:512],
                             [(xnTf[:, k, t0 + mt * 128:t0 + (mt + 1) * 128],
                               we[e % 2][:, k, no:no + 512]) for k in range(8)])
                    tmp = ctx.tmp("ff_tmp", [128, 512])
                    p.tt("vector", tmp[:, :], pm[:, 0:512],
                         ber[e % 2][:, no:no + 512], ALU.add)
                    p.stt("vector", o[:, mt, no:no + 512], tmp[:, :],
                          comb[:, ci * ntt + mt, e:e + 1],
                          o[:, mt, no:no + 512], ALU.mult, ALU.add)
            for tt in range(ntt):
                p.dma("sync", xo_dram[t0 + tt * 128:t0 + (tt + 1) * 128, :],
                      o[:, tt, :], out_is_output=(e == E - 1),
                      wkey=f"f3xo@{e + 1}_{ci}")


def emit_l3_ffn_init(ctx, x_dram, xo_dram):
    p = ctx.p
    T = ctx.T
    st = min(512, T)
    for r0 in range(0, T, st):
        p.dma("gpsimd", xo_dram[r0:r0 + st, :], x_dram[r0:r0 + st, :])


def emit_l0_ffn(ctx, x_dram, xo_dram, W):
    p = ctx.p
    T, Tc = ctx.T, ctx.Tc
    ntt = Tc // 128
    nch = T // Tc
    NTT = T // 128
    gw = ctx.wload(W["gate_wT"], 1024, 2, "f0gw")
    scrB = ctx.tmp("scrB_bf16", [128, 4096], BF16)
    xt = p.sb(ctx.n("f0_xt"), [128, ntt, 1024], F32)
    xnb = scrB[:, 0:ntt * 1024].rearrange("p (a b) -> p a b", b=1024)
    xnTf = p.sb(ctx.n("f0_xnTf"), [128, 8, T], BF16)
    xselT = scrB[:, 0:8 * Tc].rearrange("p (a b) -> p a b", b=Tc)
    seltn = p.sb(ctx.n("f0_sel"), [128, NTT, 2], F32)
    selrow = p.sb(ctx.n("f0_selrow"), [1, Tc], F32)
    hb = p.sb(ctx.n("f0_hb"), [128, 16, Tc], BF16)
    ot = [p.sb(ctx.n("f0_ot0"), [128, ntt, 1024], F32)] * 2
    w1 = [p.sb(ctx.n(f"f0_w1{i}"), [128, 8, 2048], BF16) for i in range(2)]
    w2 = p.sb(ctx.n("f0_w2"), [128, 16, 1024], BF16)
    b1 = [p.sb(ctx.n(f"f0_b1{i}"), [128, 16], F32) for i in range(2)]
    b2r = [p.sb(ctx.n("f0_b2r0"), [128, 1024], F32)] * 2
    rowbuf = ctx.rowbuf

    for ci in range(nch):
        t0 = ci * Tc
        p.dma("gpsimd", xo_dram[t0:t0 + Tc, :], x_dram[t0:t0 + Tc, :],
              wkey=f"f0xo@0_{ci}")
    for ci in range(nch):
        t0 = ci * Tc
        for tt in range(ntt):
            p.dma("sync", xt[:, tt, :], x_dram[t0 + tt * 128:t0 + (tt + 1) * 128, :])
        ln_chunk(ctx, xt[:, 0:ntt, :], ntt, 1024, xnb[:, 0:ntt, :])
        ctx.transpose_tmaj_to_dmaj(xnTf, xnb, ntt, 8, dst_t0=t0)
        probs, _ = router_chunk(ctx, xnTf[:, :, t0:t0 + Tc], gw, 2, ntt)
        for tt in range(ntt):
            r = ci * ntt + tt
            p.tt("vector", seltn[:, r, 0:1], probs[:, tt, 0:1], probs[:, tt, 1:2],
                 ALU.is_ge)
            p.ts("vector", seltn[:, r, 1:2], seltn[:, r, 0:1], -1.0, 1.0,
                 ALU.mult, ALU.add)
    phases = []
    for ei, (pw1, pw2, pb1, pb2) in enumerate(
            [(W["w1_e0"], W["w2_e0"], W["b1_e0"], W["b2_e0"]),
             (W["w1_e1"], W["w2_e1"], W["b1_e1"], W["b2_e1"]),
             (W["sw1"], W["sw2"], W["sb1"], W["sb2"])]):
        for hh in range(2):
            phases.append((ei, hh, pw1, pw2, pb1, pb2))
    w1p0 = phases[0]
    ctx.wload_into(w1[0], w1p0[2][:, 0:2048], 1024, 2048)
    for pi, (ei, hh, pw1, pw2, pb1, pb2) in enumerate(phases):
        ctx.wload_into(w2, pw2[hh * 2048:(hh + 1) * 2048, :], 2048, 1024,
                       eng="vector")
        p.dma("sync", b1[pi % 2][:, :], pb1[:, hh * 16:(hh + 1) * 16])
        if hh == 0:
            ctx.row_tile_into(b2r[ei % 2], pb2, 1024, rowbuf)
        if pi + 1 < len(phases):
            nei, nhh, npw1 = phases[pi + 1][0], phases[pi + 1][1], phases[pi + 1][2]
            ctx.wload_into(w1[(pi + 1) % 2],
                           npw1[:, nhh * 2048:(nhh + 1) * 2048], 1024, 2048,
                           eng="vector")
        for ci in range(nch):
            t0 = ci * Tc
            if ei < 2:
                for tt in range(ntt):
                    sel_to_row(ctx, seltn[:, ci * ntt + tt, ei:ei + 1],
                               selrow, tt)
                pb = ctx.pbc[1]
                p.mm(pb[:, 0:Tc], ctx.ones[0:1, :], selrow[0:1, 0:Tc], True, True)
                for g in range(8):
                    p.tt("vector", xselT[:, g, :], xnTf[:, g, t0:t0 + Tc],
                         pb[:, 0:Tc], ALU.mult)
                src = xselT
                soff = 0
            else:
                src = xnTf
                soff = t0
            for j in range(16):
                pm = ctx.pmm[j % 2]
                p.mm_acc(pm[:, 0:Tc], [(w1[pi % 2][:, k, j * 128:(j + 1) * 128],
                                        src[:, k, soff:soff + Tc]) for k in range(8)])
                p.act("scalar", hb[:, j, :], pm[:, 0:Tc], AF.Relu,
                      bias=b1[pi % 2][:, j:j + 1])
            o = ot[ci % 2]
            for tt in range(ntt):
                p.dma("scalar", o[:, tt, :],
                      xo_dram[t0 + tt * 128:t0 + (tt + 1) * 128, :],
                      rkey=f"f0xo@{pi}_{ci}")
            for mt in range(ntt):
                for no in range(0, 1024, 512):
                    pm = ctx.pmm[2 + (no // 512) % 2]
                    p.mm_acc(pm[:, 0:512],
                             [(hb[:, k, mt * 128:(mt + 1) * 128],
                               w2[:, k, no:no + 512]) for k in range(16)])
                    if hh == 0:
                        tmp = ctx.tmp("ff_tmp", [128, 512])
                        p.tt("vector", tmp[:, :], pm[:, 0:512],
                             b2r[ei % 2][:, no:no + 512], ALU.add)
                        p.tt("vector", o[:, mt, no:no + 512],
                             o[:, mt, no:no + 512], tmp[:, :], ALU.add)
                    else:
                        p.tt("vector", o[:, mt, no:no + 512],
                             o[:, mt, no:no + 512], pm[:, 0:512], ALU.add)
            last = (pi == len(phases) - 1)
            for tt in range(ntt):
                p.dma("sync", xo_dram[t0 + tt * 128:t0 + (tt + 1) * 128, :],
                      o[:, tt, :], out_is_output=last,
                      wkey=f"f0xo@{pi + 1}_{ci}")


def emit_l1_ffn(ctx, x_dram, xo_dram, W):
    p = ctx.p
    T, Tc = ctx.T, ctx.Tc
    ntt = Tc // 128
    nch = T // Tc
    NTT = T // 128
    E = 16
    wr = ctx.wload(W["Wr"], 1024, E, "f1wr")
    brow = ctx.row_tile(W["br_row"], E, "f1br")
    scrB = ctx.tmp("scrB_bf16", [128, 4096], BF16)
    xt = p.sb(ctx.n("f1_xt"), [128, ntt, 1024], F32)
    xnb = scrB[:, 0:ntt * 1024].rearrange("p (a b) -> p a b", b=1024)
    xnTf = p.sb(ctx.n("f1_xnTf"), [128, 8, T], BF16)
    comb = p.sb(ctx.n("f1_comb"), [128, NTT, E], F32)
    oh = ctx.tmp("tk_oh", [128, 4, 32])
    mx = ctx.tmp("tk_mx", [128, 4, 2])
    hb = ctx.tmp("scrA", [128, 4096])[:, :].bitcast(BF16)[:, 0:16 * Tc].rearrange(
        "p (a b) -> p a b", b=Tc)
    ot = [p.sb(ctx.n(f"f1_ot{i}"), [128, ntt, 1024], F32) for i in range(2)]
    xn_sc = ot[0]
    w1 = [p.sb(ctx.n(f"f1_w1{i}"), [128, 8, 2048], BF16) for i in range(2)]
    w2 = p.sb(ctx.n("f1_w2"), [128, 16, 1024], BF16)
    b1 = [p.sb(ctx.n(f"f1_b1{i}"), [128, 16], F32) for i in range(2)]
    b2r = [p.sb(ctx.n("f1_b2r0"), [128, 1024], F32)] * 2
    rowbuf = ctx.rowbuf

    for ci in range(nch):
        t0 = ci * Tc
        for tt in range(ntt):
            p.dma("sync", xt[:, tt, :], x_dram[t0 + tt * 128:t0 + (tt + 1) * 128, :])
        ln_chunk(ctx, xt[:, 0:ntt, :], ntt, 1024, xn_sc[:, 0:ntt, :])
        p.cp("vector", xnb[:, 0:ntt, :], xn_sc[:, 0:ntt, :])
        ctx.transpose_tmaj_to_dmaj(xnTf, xnb, ntt, 8, dst_t0=t0)
        probs, _ = router_chunk(ctx, xnTf[:, :, t0:t0 + Tc], wr, E, ntt, brow=brow)
        for tt in range(ntt):
            r = ci * ntt + tt
            topk_masks(ctx, probs[:, tt, 0:E], E, 2, oh, mx, tt)
            s12 = ctx.tmp("f1_s12", [128, 1])
            p.tt("vector", s12[:, :], mx[:, tt, 0:1], mx[:, tt, 1:2], ALU.add)
            p.ts("vector", s12[:, :], s12[:, :], 1e-9, None, ALU.max)
            p.recip("vector", s12[:, :], s12[:, :])
            c1 = ctx.tmp("f1_c1", [128, 1])
            c2 = ctx.tmp("f1_c2", [128, 1])
            p.tt("vector", c1[:, :], mx[:, tt, 0:1], s12[:, :], ALU.mult)
            p.tt("vector", c2[:, :], mx[:, tt, 1:2], s12[:, :], ALU.mult)
            p.ts("vector", comb[:, r, 0:E], oh[:, tt, 0:E], c1[:, 0:1], None,
                 ALU.mult)
            p.stt("vector", comb[:, r, 0:E], oh[:, tt, E:2 * E], c2[:, 0:1],
                  comb[:, r, 0:E], ALU.mult, ALU.add)
        p.tt("vector", xt[:, 0:ntt, :], xt[:, 0:ntt, :], xn_sc[:, 0:ntt, :],
             ALU.add)
        for tt in range(ntt):
            p.dma("sync", xo_dram[t0 + tt * 128:t0 + (tt + 1) * 128, :],
                  xt[:, tt, :], wkey=f"f1xo@0_{ci}")
    ctx.wload_into(w1[0], W["W1"][0:1024, :], 1024, 2048)
    for e in range(E):
        ctx.wload_into(w2, W["W2"][e * 2048:(e + 1) * 2048, :], 2048, 1024,
                       eng="vector")
        p.dma("sync", b1[e % 2][:, :], W["b1"][e * 128:(e + 1) * 128, :])
        ctx.row_tile_into(b2r[e % 2], W["b2"][e:e + 1, :], 1024, rowbuf)
        if e + 1 < E:
            ctx.wload_into(w1[(e + 1) % 2],
                           W["W1"][(e + 1) * 1024:(e + 2) * 1024, :], 1024, 2048,
                           eng="vector")
        for ci in range(nch):
            t0 = ci * Tc
            for j in range(16):
                pm = ctx.pmm[j % 2]
                p.mm_acc(pm[:, 0:Tc], [(w1[e % 2][:, k, j * 128:(j + 1) * 128],
                                        xnTf[:, k, t0:t0 + Tc]) for k in range(8)])
                p.act("scalar", hb[:, j, :], pm[:, 0:Tc], AF.Gelu,
                      bias=b1[e % 2][:, j:j + 1])
            o = ot[ci % 2]
            for tt in range(ntt):
                p.dma("scalar", o[:, tt, :],
                      xo_dram[t0 + tt * 128:t0 + (tt + 1) * 128, :],
                      rkey=f"f1xo@{e}_{ci}")
            for mt in range(ntt):
                for no in range(0, 1024, 512):
                    pm = ctx.pmm[2 + (no // 512) % 2]
                    p.mm_acc(pm[:, 0:512],
                             [(hb[:, k, mt * 128:(mt + 1) * 128],
                               w2[:, k, no:no + 512]) for k in range(16)])
                    tmp = ctx.tmp("ff_tmp", [128, 512])
                    p.tt("vector", tmp[:, :], pm[:, 0:512],
                         b2r[e % 2][:, no:no + 512], ALU.add)
                    p.stt("vector", o[:, mt, no:no + 512], tmp[:, :],
                          comb[:, ci * ntt + mt, e:e + 1],
                          o[:, mt, no:no + 512], ALU.mult, ALU.add)
            last = (e == E - 1)
            for tt in range(ntt):
                p.dma("sync", xo_dram[t0 + tt * 128:t0 + (tt + 1) * 128, :],
                      o[:, tt, :], out_is_output=last, wkey=f"f1xo@{e + 1}_{ci}")


def emit_l2_ffn_a(ctx, x_dram, eo1_dram, g_dram, W):
    p = ctx.p
    T, Tc = ctx.T, ctx.Tc
    ntt = Tc // 128
    nch = T // Tc
    NTT = T // 128
    E = 8
    gw = ctx.wload(W["gate_wT"], 1024, E, "f2gw")
    brow = ctx.row_tile(W["gate_b_row"], E, "f2gb")
    scrB = ctx.tmp("scrB_bf16", [128, 4096], BF16)
    xt = p.sb(ctx.n("f2_xt"), [128, ntt, 1024], F32)
    xnb = scrB[:, 0:ntt * 1024].rearrange("p (a b) -> p a b", b=1024)
    xnTf = p.sb(ctx.n("f2_xnTf"), [128, 8, T], BF16)
    gtile = p.sb(ctx.n("f2_g"), [128, NTT, 18], F32)
    oh = ctx.tmp("tk_oh", [128, 4, 32])
    mx = ctx.tmp("tk_mx", [128, 4, 2])
    hb = ctx.tmp("scrA", [128, 4096])[:, :].bitcast(BF16)[:, 0:16 * Tc].rearrange(
        "p (a b) -> p a b", b=Tc)
    ot = [p.sb(ctx.n(f"f2_ot{i}"), [128, ntt, 1024], F32) for i in range(2)]
    w1 = [p.sb(ctx.n("f2_w10"), [128, 8, 2048], BF16)] * 2
    w2 = p.sb(ctx.n("f2_w2"), [128, 16, 1024], BF16)
    b1 = [p.sb(ctx.n(f"f2_b1{i}"), [128, 16], F32) for i in range(2)]
    b2r = [p.sb(ctx.n("f2_b2r0"), [128, 1024], F32)] * 2
    rowbuf = ctx.rowbuf

    for ci in range(nch):
        t0 = ci * Tc
        for tt in range(ntt):
            p.dma("sync", xt[:, tt, :], x_dram[t0 + tt * 128:t0 + (tt + 1) * 128, :])
        ln_chunk(ctx, xt[:, 0:ntt, :], ntt, 1024, xnb[:, 0:ntt, :])
        ctx.transpose_tmaj_to_dmaj(xnTf, xnb, ntt, 8, dst_t0=t0)
        probs, _ = router_chunk(ctx, xnTf[:, :, t0:t0 + Tc], gw, E, ntt, brow=brow)
        for tt in range(ntt):
            r = ci * ntt + tt
            topk_masks(ctx, probs[:, tt, 0:E], E, 2, oh, mx, tt)
            p.cp("vector", gtile[:, r, 0:2], mx[:, tt, 0:2])
            p.cp("vector", gtile[:, r, 2:10], oh[:, tt, 0:E])
            p.cp("vector", gtile[:, r, 10:18], oh[:, tt, E:2 * E])
            p.dma("sync", g_dram[t0 + tt * 128:t0 + (tt + 1) * 128, :],
                  gtile[:, r, :], out_is_output=True)
    for e in range(E):
        ctx.wload_into(w1[e % 2], W["W1"][e * 1024:(e + 1) * 1024, :], 1024, 2048,
                       eng="vector")
        ctx.wload_into(w2, W["W2"][e * 2048:(e + 1) * 2048, :], 2048, 1024,
                       eng="vector")
        p.dma("sync", b1[e % 2][:, :], W["b1"][e * 128:(e + 1) * 128, :])
        ctx.row_tile_into(b2r[e % 2], W["b2"][e:e + 1, :], 1024, rowbuf)
        if False:
            pass
        for ci in range(nch):
            t0 = ci * Tc
            for j in range(16):
                pm = ctx.pmm[j % 2]
                p.mm_acc(pm[:, 0:Tc], [(w1[e % 2][:, k, j * 128:(j + 1) * 128],
                                        xnTf[:, k, t0:t0 + Tc]) for k in range(8)])
                p.act("scalar", hb[:, j, :], pm[:, 0:Tc], AF.Relu,
                      bias=b1[e % 2][:, j:j + 1])
            o = ot[ci % 2]
            if e > 0:
                for tt in range(ntt):
                    p.dma("scalar", o[:, tt, :],
                          eo1_dram[t0 + tt * 128:t0 + (tt + 1) * 128, :],
                          rkey=f"f2eo@{e}_{ci}")
            for mt in range(ntt):
                for no in range(0, 1024, 512):
                    pm = ctx.pmm[2 + (no // 512) % 2]
                    p.mm_acc(pm[:, 0:512],
                             [(hb[:, k, mt * 128:(mt + 1) * 128],
                               w2[:, k, no:no + 512]) for k in range(16)])
                    tmp = ctx.tmp("ff_tmp", [128, 512])
                    p.tt("vector", tmp[:, :], pm[:, 0:512],
                         b2r[e % 2][:, no:no + 512], ALU.add)
                    if e == 0:
                        p.ts("vector", o[:, mt, no:no + 512], tmp[:, :],
                             gtile[:, ci * ntt + mt, 2 + e:3 + e], None, ALU.mult)
                    else:
                        p.stt("vector", o[:, mt, no:no + 512], tmp[:, :],
                              gtile[:, ci * ntt + mt, 2 + e:3 + e],
                              o[:, mt, no:no + 512], ALU.mult, ALU.add)
            for tt in range(ntt):
                p.dma("sync", eo1_dram[t0 + tt * 128:t0 + (tt + 1) * 128, :],
                      o[:, tt, :], out_is_output=(e == E - 1),
                      wkey=f"f2eo@{e + 1}_{ci}")


def emit_l2_ffn_b(ctx, x_dram, eo1_dram, g_dram, xo_dram, W):
    p = ctx.p
    T, Tc = ctx.T, ctx.Tc
    ntt = Tc // 128
    nch = T // Tc
    NTT = T // 128
    E = 8
    gamr = ctx.row_tile(W["ln_g_row"], 1024, "f2lg")
    betr = ctx.row_tile(W["ln_b_row"], 1024, "f2lb")
    scrB = ctx.tmp("scrB_bf16", [128, 4096], BF16)
    xt = p.sb(ctx.n("f2_xt"), [128, ntt, 1024], F32)
    xn_sc = p.sb(ctx.n("f2_xns"), [128, ntt, 1024], F32)
    bstb = scrB[:, 0:ntt * 1024].rearrange("p (a b) -> p a b", b=1024)
    bstTf = p.sb(ctx.n("f2_bTf"), [128, 8, T], BF16)
    gtile = p.sb(ctx.n("f2_g2"), [128, NTT, 18], F32)
    hb = ctx.tmp("scrA", [128, 4096])[:, :].bitcast(BF16)[:, 0:16 * Tc].rearrange(
        "p (a b) -> p a b", b=Tc)
    ot = [p.sb(ctx.n(f"f2_ot{i}"), [128, ntt, 1024], F32) for i in range(2)]
    w1 = [p.sb(ctx.n("f2_w10"), [128, 8, 2048], BF16)] * 2
    w2 = p.sb(ctx.n("f2_w2"), [128, 16, 1024], BF16)
    b1 = [p.sb(ctx.n(f"f2_b1{i}"), [128, 16], F32) for i in range(2)]
    b2r = [p.sb(ctx.n("f2_b2r0"), [128, 1024], F32)] * 2
    rowbuf = ctx.rowbuf

    for ci in range(nch):
        t0 = ci * Tc
        eo1t = ot[(ci + 1) % 2]
        for tt in range(ntt):
            r = ci * ntt + tt
            p.dma("sync", gtile[:, r, :], g_dram[t0 + tt * 128:t0 + (tt + 1) * 128, :])
            p.dma("sync", xt[:, tt, :], x_dram[t0 + tt * 128:t0 + (tt + 1) * 128, :])
            p.dma("gpsimd", eo1t[:, tt, :],
                  eo1_dram[t0 + tt * 128:t0 + (tt + 1) * 128, :])
        ln_chunk(ctx, xt[:, 0:ntt, :], ntt, 1024, xn_sc[:, 0:ntt, :])
        p.stt("vector", bstb[:, 0:ntt, :], eo1t[:, 0:ntt, :], 0.5,
              xn_sc[:, 0:ntt, :], ALU.mult, ALU.add)
        ctx.transpose_tmaj_to_dmaj(bstTf, bstb, ntt, 8, dst_t0=t0)
    for e in range(E):
        ctx.wload_into(w1[e % 2], W["W1"][e * 1024:(e + 1) * 1024, :], 1024, 2048,
                       eng="vector")
        ctx.wload_into(w2, W["W2"][e * 2048:(e + 1) * 2048, :], 2048, 1024,
                       eng="vector")
        p.dma("sync", b1[e % 2][:, :], W["b1"][e * 128:(e + 1) * 128, :])
        ctx.row_tile_into(b2r[e % 2], W["b2"][e:e + 1, :], 1024, rowbuf)
        if False:
            pass
        for ci in range(nch):
            t0 = ci * Tc
            for j in range(16):
                pm = ctx.pmm[j % 2]
                p.mm_acc(pm[:, 0:Tc], [(w1[e % 2][:, k, j * 128:(j + 1) * 128],
                                        bstTf[:, k, t0:t0 + Tc]) for k in range(8)])
                p.act("scalar", hb[:, j, :], pm[:, 0:Tc], AF.Relu,
                      bias=b1[e % 2][:, j:j + 1])
            o = ot[ci % 2]
            if e > 0:
                for tt in range(ntt):
                    p.dma("scalar", o[:, tt, :],
                          xo_dram[t0 + tt * 128:t0 + (tt + 1) * 128, :],
                          rkey=f"f2xo@{e}_{ci}")
            for mt in range(ntt):
                r = ci * ntt + mt
                w2c = ctx.tmp("f2_w2c", [128, 1])
                p.tt("vector", w2c[:, :], gtile[:, r, 1:2],
                     gtile[:, r, 10 + e:11 + e], ALU.mult)
                for no in range(0, 1024, 512):
                    pm = ctx.pmm[2 + (no // 512) % 2]
                    p.mm_acc(pm[:, 0:512],
                             [(hb[:, k, mt * 128:(mt + 1) * 128],
                               w2[:, k, no:no + 512]) for k in range(16)])
                    tmp = ctx.tmp("ff_tmp", [128, 512])
                    p.tt("vector", tmp[:, :], pm[:, 0:512],
                         b2r[e % 2][:, no:no + 512], ALU.add)
                    if e == 0:
                        p.ts("vector", o[:, mt, no:no + 512], tmp[:, :],
                             w2c[:, 0:1], None, ALU.mult)
                    else:
                        p.stt("vector", o[:, mt, no:no + 512], tmp[:, :],
                              w2c[:, 0:1], o[:, mt, no:no + 512],
                              ALU.mult, ALU.add)
            for tt in range(ntt):
                p.dma("sync", xo_dram[t0 + tt * 128:t0 + (tt + 1) * 128, :],
                      o[:, tt, :], wkey=f"f2xo@{e + 1}_{ci}")
    for ci in range(nch):
        t0 = ci * Tc
        o = ot[ci % 2]
        eo1t = ot[(ci + 1) % 2]
        for tt in range(ntt):
            r = ci * ntt + tt
            p.dma("sync", xt[:, tt, :], x_dram[t0 + tt * 128:t0 + (tt + 1) * 128, :])
            p.dma("gpsimd", eo1t[:, tt, :],
                  eo1_dram[t0 + tt * 128:t0 + (tt + 1) * 128, :])
            p.dma("scalar", o[:, tt, :],
                  xo_dram[t0 + tt * 128:t0 + (tt + 1) * 128, :],
                  rkey=f"f2xo@{E}_{ci}")
        ln_chunk(ctx, xt[:, 0:ntt, :], ntt, 1024, xn_sc[:, 0:ntt, :])
        for tt in range(ntt):
            r = ci * ntt + tt
            p.stt("vector", o[:, tt, :], eo1t[:, tt, :], gtile[:, r, 0:1],
                  o[:, tt, :], ALU.mult, ALU.add)
        p.tt("vector", o[:, 0:ntt, :], o[:, 0:ntt, :], xn_sc[:, 0:ntt, :],
             ALU.add)
        ln_chunk(ctx, o[:, 0:ntt, :], ntt, 1024, xn_sc[:, 0:ntt, :],
                 gam_row=gamr, beta_row=betr)
        p.tt("vector", xt[:, 0:ntt, :], xt[:, 0:ntt, :], xn_sc[:, 0:ntt, :],
             ALU.add)
        for tt in range(ntt):
            p.dma("sync", xo_dram[t0 + tt * 128:t0 + (tt + 1) * 128, :],
                  xt[:, tt, :], out_is_output=True, wkey=f"f2xo@{E + 1}_{ci}")


def emit_l3_attn(ctx, x_dram, xo_dram, W):
    p = ctx.p
    T, Tc = ctx.T, ctx.Tc
    ntt = Tc // 128
    nch = T // Tc
    NTT = T // 128
    H = 16
    maskd = ctx.nc.dram_tensor("c_mask", [128, 128], F32, kind="ExternalInput").ap()
    mask = p.sb("l3_mask", [128, 128], F32)
    p.dma("sync", mask[:, :], maskd)
    wproj = ctx.wload(W["WprojT"], 1024, 1024, "l3wp")
    scrB = ctx.tmp("scrB_bf16", [128, 4096], BF16)
    xt = p.sb(ctx.n("a3_xt"), [128, ntt, 1024], F32)
    xnb = scrB[:, 0:ntt * 1024].rearrange("p (a b) -> p a b", b=1024)
    xnT = p.sb(ctx.n("a3_xnT"), [128, 8, Tc], BF16)
    qT = p.sb("l3_qT", [128, 8, T], BF16)
    kT = p.sb("l3_kT", [128, 8, T], BF16)
    vt = p.sb("l3_vt", [128, NTT, 1024], BF16)
    ws = p.sb("l3_ws", [128, 8, 1024], BF16)
    S = p.sb("l3_S", [128, 2048], F32)
    Pb = p.sb("l3_Pb", [128, 2048], BF16)
    PTb = [p.sb(f"l3_PT{i}", [128, 128], BF16) for i in range(2)]
    nmax = p.sb("l3_nmax", [128, 1], F32)
    esum = p.sb("l3_esum", [128, 1], F32)
    oT = p.sb("l3_oT", [128, 8, 128], BF16)
    xq = p.sb("l3_xq", [128, 1024], F32)
    pO = ctx.pbc[1]

    for which in ("q", "k", "v"):
        ctx.wload_into(ws, W["W%sT" % which], 1024, 1024)
        for ci in range(nch):
            t0 = ci * Tc
            for tt in range(ntt):
                p.dma("sync", xt[:, tt, :],
                      x_dram[t0 + tt * 128:t0 + (tt + 1) * 128, :])
            ln_chunk(ctx, xt[:, 0:ntt, :], ntt, 1024, xnb[:, 0:ntt, :])
            ctx.transpose_tmaj_to_dmaj(xnT, xnb, ntt, 8)
            if which in ("q", "k"):
                dst = qT if which == "q" else kT
                scale = 0.125 if which == "q" else 1.0
                for j in range(8):
                    pm = ctx.pmm[j % 2]
                    p.mm_acc(pm[:, 0:Tc], [(ws[:, k, j * 128:(j + 1) * 128],
                                            xnT[:, k, 0:Tc]) for k in range(8)])
                    if scale != 1.0:
                        p.ts("vector", dst[:, j, t0:t0 + Tc], pm[:, 0:Tc],
                             scale, None, ALU.mult)
                    else:
                        p.cp("vector", dst[:, j, t0:t0 + Tc], pm[:, 0:Tc])
            else:
                for mt in range(ntt):
                    for no in range(0, 1024, 512):
                        pm = ctx.pmm[2 + (no // 512) % 2]
                        p.mm_acc(pm[:, 0:512],
                                 [(xnT[:, k, mt * 128:(mt + 1) * 128],
                                   ws[:, k, no:no + 512]) for k in range(8)])
                        p.cp("vector", vt[:, ci * ntt + mt, no:no + 512],
                             pm[:, 0:512])
    for qt in range(NTT):
        nkb = qt + 1
        for h in range(H):
            hp = (h % 2) * 64
            hg = h // 2
            for kb in range(nkb):
                pm = ctx.pmm[kb % 4]
                p.mm(pm[:, 0:128], qT[hp:hp + 64, hg, qt * 128:(qt + 1) * 128],
                     kT[hp:hp + 64, hg, kb * 128:(kb + 1) * 128], True, True)
                if kb == qt:
                    p.tt("vector", S[:, kb * 128:(kb + 1) * 128], pm[:, 0:128],
                         mask[:, :], ALU.add)
                else:
                    p.cp("vector", S[:, kb * 128:(kb + 1) * 128], pm[:, 0:128])
            p.red("vector", nmax[:, :], S[:, 0:nkb * 128], ALU.max, negate=True)
            p.act("scalar", S[:, 0:nkb * 128], S[:, 0:nkb * 128], AF.Exp,
                  bias=nmax[:, :], accum_out=esum[:, :])
            p.recip("vector", esum[:, :], esum[:, :])
            p.ts("vector", Pb[:, 0:nkb * 128], S[:, 0:nkb * 128], esum[:, 0:1],
                 None, ALU.mult)
            for kb in range(nkb):
                pt = ctx.ptr[kb % 2]
                p.transpose(pt[:, :], Pb[:, kb * 128:(kb + 1) * 128],
                            ctx.identb[:, :])
                ptb = PTb[kb % 2]
                p.cp("vector", ptb[:, :], pt[:, :])
                p.mm(pO[0:64, 0:128], vt[:, kb, h * 64:(h + 1) * 64], ptb[:, :],
                     kb == 0, kb == nkb - 1)
            p.cp("vector", oT[hp:hp + 64, hg, :], pO[0:64, 0:128])
        p.dma("sync", xq[:, :], x_dram[qt * 128:(qt + 1) * 128, :])
        for no in range(0, 1024, 512):
            pm = ctx.pmm[(no // 512) % 2]
            p.mm_acc(pm[:, 0:512], [(oT[:, k, :], wproj[:, k, no:no + 512])
                                    for k in range(8)])
            p.tt("vector", xq[:, no:no + 512], xq[:, no:no + 512], pm[:, 0:512],
                 ALU.add)
        p.dma("sync", xo_dram[qt * 128:(qt + 1) * 128, :], xq[:, :],
              out_is_output=True)


def emit_full(ctx, x_in, x_out, Wd):
    p = ctx.p
    nc = ctx.nc
    T = ctx.T
    xs = [nc.dram_tensor(f"xs{i}", [T, 1024], F32).ap() for i in range(7)]
    yf = nc.dram_tensor("yf_s", [128, 8, T], BF16).ap()
    eo1 = nc.dram_tensor("eo1_s", [T, 1024], F32).ap()
    gsc = nc.dram_tensor("g_s", [T, 18], F32).ap()
    ctx.prewarm()

    def sub(pre):
        return {k[len(pre):]: v for k, v in Wd.items() if k.startswith(pre)}

    seq = x_in
    ctx.Tc = 256
    with p.scope():
        emit_l0_dir(ctx, seq, xs[0], sub("l0f_"), False)
    with p.scope():
        emit_l0_dir(ctx, seq, xs[0], sub("l0b_"), True)
    seq = xs[0]
    ctx.Tc = 256
    with p.scope():
        emit_l0_ffn(ctx, seq, xs[1], sub("f0_"))
    seq = xs[1]
    xcd_f = nc.dram_tensor("l1f_xcd", [128, 16, T], BF16).ap()
    zd_f = nc.dram_tensor("l1f_zd", [128, 16, T], BF16).ap()
    xcd_b = nc.dram_tensor("l1b_xcd", [128, 16, T], BF16).ap()
    zd_b = nc.dram_tensor("l1b_zd", [128, 16, T], BF16).ap()
    with p.scope():
        emit_l1_phA(ctx, seq, sub("l1f_"), False, xcd_f, zd_f, "l1f")
    with p.scope():
        ctx.Tc = 256
        emit_l1_phB(ctx, seq, sub("l1f_"), False, xcd_f, zd_f, "l1f", yf, None)
    with p.scope():
        emit_l1_phA(ctx, seq, sub("l1b_"), True, xcd_b, zd_b, "l1b")
    with p.scope():
        ctx.Tc = 256
        emit_l1_phB(ctx, seq, sub("l1b_"), True, xcd_b, zd_b, "l1b", yf, xs[2])
    seq = xs[2]
    ctx.Tc = 256
    with p.scope():
        emit_l1_ffn(ctx, seq, xs[3], sub("f1_"))
    seq = xs[3]
    ctx.Tc = 256
    with p.scope():
        emit_l2_attn(ctx, seq, xs[4], sub("l2_"))
    seq = xs[4]
    ctx.Tc = 256
    with p.scope():
        emit_l2_ffn_a(ctx, seq, eo1, gsc, sub("f2_"))
    with p.scope():
        emit_l2_ffn_b(ctx, seq, eo1, gsc, xs[5], sub("f2_"))
    seq = xs[5]
    ctx.Tc = 256
    with p.scope():
        emit_l3_attn(ctx, seq, xs[6], sub("a3_"))
    seq = xs[6]
    ctx.Tc = 512
    with p.scope():
        emit_l3_ffn(ctx, seq, x_out, sub("f3_"))


import ml_dtypes



def dmaj_vec(v):
    v = np.asarray(v, np.float32)
    G = v.shape[0] // 128
    return np.ascontiguousarray(v.reshape(G, 128).T)


def dmaj_mat(m):
    m = np.asarray(m, np.float32)
    D, W = m.shape
    G = D // 128
    return np.ascontiguousarray(m.reshape(G, 128, W).transpose(1, 0, 2).reshape(128, G * W))


def kxm(m):
    return np.ascontiguousarray(np.asarray(m, np.float32))


def row(v):
    return np.ascontiguousarray(np.asarray(v, np.float32).reshape(1, -1))


def consts():
    seln = np.zeros((16, 16 * 128), ml_dtypes.bfloat16)
    for n in range(16):
        seln[n, n * 128:(n + 1) * 128] = 1.0
    return {
        "c_identb": np.eye(128, dtype=ml_dtypes.bfloat16),
        "c_ones": np.ones((1, 128), np.float32),
        "c_seln": seln,
    }


def l2_attn_inputs(p):
    return {
        "l2_W_in": kxm(p["W_in"]),
        "l2_conv_w": dmaj_mat(p["conv_w"][:, 0, :]),
        "l2_conv_b": dmaj_vec(p["conv_b"]),
        "l2_W_xproj": kxm(p["W_xproj"]),
        "l2_log_A": dmaj_mat(p["log_A"]),
        "l2_D_param": dmaj_vec(p["D_param"]),
        "l2_W_out": kxm(p["W_out"]),
    }


def l0_dir_inputs(p, pre, tag):
    return {
        tag + "inw_lhsT": kxm(p[pre + "_in_proj_w"].T),
        tag + "conv_w": dmaj_mat(p[pre + "_conv_w"][:, 0, :]),
        tag + "conv_b": dmaj_vec(p[pre + "_conv_b"]),
        tag + "xproj_lhsT": kxm(p[pre + "_x_proj_w"].T),
        tag + "dtw_lhsT": kxm(p[pre + "_dt_proj_w"].T),
        tag + "dt_b": dmaj_vec(p[pre + "_dt_proj_b"]),
        tag + "A_log": dmaj_mat(p[pre + "_A_log"]),
        tag + "D": dmaj_vec(p[pre + "_D"]),
        tag + "outw_rhs": kxm(p[pre + "_out_proj_w"].T),
        tag + "gam_row": row(p["norm_w"]),
        tag + "beta_row": row(p["norm_b"]),
    }


def l1_dir_inputs(p, pre, tag):
    d = {
        tag + "inw_lhsT": kxm(p[pre + "_in_w"].T),
        tag + "conv_w": dmaj_mat(p[pre + "_conv_w"]),
        tag + "conv_b": dmaj_vec(p[pre + "_conv_b"]),
        tag + "xproj_lhsT": kxm(p[pre + "_xproj_w"].T),
        tag + "dtw_lhsT": kxm(p[pre + "_dt_w"].T),
        tag + "dt_b": dmaj_vec(p[pre + "_dt_b"]),
        tag + "A_log": dmaj_mat(p[pre + "_A_log"]),
        tag + "D": dmaj_vec(p[pre + "_D"]),
        tag + "outw_lhsT": kxm(p[pre + "_out_w"].T),
        tag + "gam_row": row(p["ln_g"]),
        tag + "beta_row": row(p["ln_b"]),
    }
    if pre == "bwd":
        d[tag + "merge_rhs"] = kxm(p["merge_w"].T)
        d[tag + "merge_b_row"] = row(p["merge_b"])
    return d


def l3_ffn_inputs(p):
    ew = np.asarray(p["expert_w"], np.float32)
    ewT = np.concatenate([ew[e].T for e in range(8)], axis=0)
    return {
        "f3_router_wT": kxm(p["router_w"].T),
        "f3_router_b_row": row(p["router_b"]),
        "f3_expert_wT": np.ascontiguousarray(ewT),
        "f3_expert_b": np.ascontiguousarray(np.asarray(p["expert_b"], np.float32)),
    }


def l0_ffn_inputs(p):
    return {
        "f0_gate_wT": kxm(p["gate_w"].T),
        "f0_w1_e0": kxm(p["w1"][0]), "f0_w1_e1": kxm(p["w1"][1]),
        "f0_b1_e0": dmaj_vec(p["b1"][0]), "f0_b1_e1": dmaj_vec(p["b1"][1]),
        "f0_w2_e0": kxm(p["w2"][0]), "f0_w2_e1": kxm(p["w2"][1]),
        "f0_b2_e0": row(p["b2"][0]), "f0_b2_e1": row(p["b2"][1]),
        "f0_sw1": kxm(p["sw1"]), "f0_sb1": dmaj_vec(p["sb1"]),
        "f0_sw2": kxm(p["sw2"]), "f0_sb2": row(p["sb2"]),
    }


def l1_ffn_inputs(p):
    W1 = np.asarray(p["W1"], np.float32)
    W2 = np.asarray(p["W2"], np.float32)
    b1 = np.asarray(p["b1"], np.float32)
    return {
        "f1_Wr": kxm(p["Wr"]),
        "f1_br_row": row(p["br"]),
        "f1_W1": np.ascontiguousarray(W1.reshape(16 * 1024, 2048)),
        "f1_b1": np.concatenate([dmaj_vec(b1[e]) for e in range(16)], axis=0),
        "f1_W2": np.ascontiguousarray(W2.reshape(16 * 2048, 1024)),
        "f1_b2": np.ascontiguousarray(np.asarray(p["b2"], np.float32)),
    }


def l2_ffn_inputs(p):
    W1 = np.asarray(p["W1"], np.float32)
    W2 = np.asarray(p["W2"], np.float32)
    b1 = np.asarray(p["b1"], np.float32)
    return {
        "f2_gate_wT": kxm(p["gate_W"]),
        "f2_gate_b_row": row(p["gate_b"]),
        "f2_W1": np.ascontiguousarray(W1.reshape(8 * 1024, 2048)),
        "f2_b1": np.concatenate([dmaj_vec(b1[e]) for e in range(8)], axis=0),
        "f2_W2": np.ascontiguousarray(W2.reshape(8 * 2048, 1024)),
        "f2_b2": np.ascontiguousarray(np.asarray(p["b2"], np.float32)),
        "f2_ln_g_row": row(p["ln_gamma"]),
        "f2_ln_b_row": row(p["ln_beta"]),
    }


def l3_attn_inputs(p):
    Wa = np.asarray(p["W_attn"], np.float32)
    m = np.zeros((128, 128), np.float32)
    iu = np.triu_indices(128, 1)
    m[iu] = -3e38
    return {
        "a3_WqT": kxm(Wa[0:1024].T),
        "a3_WkT": kxm(Wa[1024:2048].T),
        "a3_WvT": kxm(Wa[2048:3072].T),
        "a3_WprojT": kxm(p["W_proj"].T),
        "c_mask": m,
    }



from concourse.bass_utils import run_bass_kernel_spmd

T_FULL = 2048
NCORES = 8


def _split_params(inputs):
    names = ["L0_attn", "L0_ffn", "L1_attn", "L1_ffn", "L2_attn", "L2_ffn",
             "L3_attn", "L3_ffn"]
    out = {n: {} for n in names}
    for k, v in inputs.items():
        if k == "L0_attn_x":
            continue
        for n in names:
            if k.startswith(n + "_"):
                out[n][k[len(n) + 1:]] = np.asarray(v)
                break
    return out


def _host_weights(params):
    W = {}
    W.update(l0_dir_inputs(params["L0_attn"], "fwd", "l0f_"))
    W.update(l0_dir_inputs(params["L0_attn"], "bwd", "l0b_"))
    W.update({k.replace("f0_", "f0_"): v for k, v in l0_ffn_inputs(params["L0_ffn"]).items()})
    W.update({k.replace("l1_", "l1f_"): v for k, v in l1_dir_inputs(params["L1_attn"], "fwd", "l1_").items()})
    W.update({k.replace("l1_", "l1b_"): v for k, v in l1_dir_inputs(params["L1_attn"], "bwd", "l1_").items()})
    W.update(l1_ffn_inputs(params["L1_ffn"]))
    W.update(l2_attn_inputs(params["L2_attn"]))
    W.update(l2_ffn_inputs(params["L2_ffn"]))
    aw = l3_attn_inputs(params["L3_attn"])
    W.update(aw)
    W.update(l3_ffn_inputs(params["L3_ffn"]))
    return W


def kernel(**inputs):
    x_full = np.ascontiguousarray(np.asarray(inputs["L0_attn_x"], np.float32))
    B, T, D = x_full.shape
    assert (B, T, D) == (8, 2048, 1024)
    params = _split_params(inputs)
    C = consts()
    W = _host_weights(params)

    nc = build_nc()
    x = nc.dram_tensor("x", [T_FULL, 1024], F32, kind="ExternalInput").ap()
    xo = nc.dram_tensor("xo", [T_FULL, 1024], F32, kind="ExternalOutput").ap()
    Wd = {}
    for k, v in W.items():
        if k == "c_mask":
            continue
        Wd[k] = nc.dram_tensor(k, list(v.shape), F32, kind="ExternalInput").ap()
    p = Prog(nc, "mega")
    ctx = Ctx(p, T_FULL, 512)
    emit_full(ctx, x, xo, Wd)
    p.finalize()

    in_maps = [{"x": np.ascontiguousarray(x_full[b]), **W, **C}
               for b in range(B)]
    res = run_bass_kernel_spmd(nc, in_maps, core_ids=list(range(NCORES)))
    return np.stack([r["xo"] for r in res.results], axis=0).astype(np.float32)


# version 10
# speedup vs baseline: 1.1945x; 1.1945x over previous
import sys
sys.path.insert(0, "/opt/trn_rl_repo")
from contextlib import ExitStack

import numpy as np
import concourse.bass as bass
import concourse.mybir as mybir

F32 = mybir.dt.float32
BF16 = mybir.dt.bfloat16
AF = mybir.ActivationFunctionType
ALU = mybir.AluOpType
AX = mybir.AxisListType

ENGINES = ("sync", "tensor", "vector", "scalar", "gpsimd")


def _key(x):
    if isinstance(x, str):
        return x
    return x.name


class Op:
    __slots__ = ("eng", "emit", "deps", "inc", "kind", "cnt", "tag", "idx", "sem_idx")

    def __init__(self, eng, emit, kind, tag):
        self.eng = eng
        self.emit = emit
        self.kind = kind
        self.deps = []
        self.inc = False
        self.cnt = None
        self.tag = tag
        self.sem_idx = 0


class Prog:
    DMA_SEM_POOL = 12

    def __init__(self, nc, name="prog"):
        self.nc = nc
        self.name = name
        self.stack = ExitStack()
        self.scopes = []
        self.fence_op = None
        self.ops = {e: [] for e in ENGINES}
        self.order = []
        self.last_writer = {}
        self.readers = {}
        self.out_dmas = []
        self.sbuf_bytes_pp = 0
        self._uid = 0

    def scope(self):
        import contextlib

        @contextlib.contextmanager
        def _cm():
            st = ExitStack()
            self.scopes.append(st)
            try:
                yield
            finally:
                self.barrier()
                self.scopes.pop()
                st.close()
        return _cm()

    def barrier(self):
        lasts = [self.ops[e][-1] for e in ENGINES if self.ops[e]]
        o = Op("vector", lambda E: E.nop(), "exec", "fence")
        for d in lasts:
            if d is not o:
                o.deps.append(d)
        self.ops["vector"].append(o)
        self.order.append(o)
        self.fence_op = o

    def sb(self, name, shape, dtype, root=False):
        tgt = self.stack if (root or not self.scopes) else self.scopes[-1]
        t = tgt.enter_context(self.nc.sbuf_tensor(name, list(shape), dtype))
        isz = np.dtype(mybir.dt.np(dtype)).itemsize
        per = int(np.prod(shape[1:])) * isz if len(shape) > 1 else isz
        self.sbuf_bytes_pp += per
        if not hasattr(self, "sbuf_allocs"):
            self.sbuf_allocs = []
        self.sbuf_allocs.append((name, per))
        return t

    def ps(self, name, shape, dtype=F32):
        return self.stack.enter_context(self.nc.psum_tensor(name, list(shape), dtype))

    def op(self, eng, emit, reads=(), writes=(), dma=False, tag=None):
        o = Op(eng, emit, "dma" if dma else "exec", tag)
        rk = [_key(r) for r in reads]
        wk = [_key(w) for w in writes]
        deps = []
        for k in rk:
            w = self.last_writer.get(k)
            if w is not None:
                deps.append(w)
        for k in wk:
            w = self.last_writer.get(k)
            if w is not None and (w.eng != eng or w.kind == "dma" or dma):
                deps.append(w)
            for r in self.readers.get(k, ()):
                if r.eng != eng or r.kind == "dma" or dma:
                    deps.append(r)
        if self.fence_op is not None and self.fence_op is not o:
            deps.append(self.fence_op)
        seen = set()
        for d in deps:
            if d is o or id(d) in seen:
                continue
            seen.add(id(d))
            o.deps.append(d)
        for k in wk:
            self.last_writer[k] = o
            self.readers[k] = []
        for k in rk:
            self.readers.setdefault(k, []).append(o)
        self.ops[eng].append(o)
        self.order.append(o)
        return o

    def dma(self, eng, out, in_, out_is_output=False, rkey=None, wkey=None, **kw):
        o = self.op(eng, lambda E: E.dma_start(out=out, in_=in_, **kw),
                    reads=[rkey if rkey is not None else in_],
                    writes=[wkey if wkey is not None else out], dma=True, tag="dma")
        if out_is_output:
            self.out_dmas.append(o)
        return o

    def cp(self, eng, out, in_, extra_reads=()):
        return self.op(eng, lambda E: E.tensor_copy(out, in_),
                       reads=[in_, *extra_reads], writes=[out], tag="cp")

    def tt(self, eng, out, a, b, aluop):
        return self.op(eng, lambda E: E.tensor_tensor(out, a, b, aluop),
                       reads=[a, b], writes=[out], tag="tt")

    def ts(self, eng, out, in_, s1, s2, op0, op1=None, accum_out=None):
        rd = [in_] + [s for s in (s1, s2) if hasattr(s, "name")]
        wr = [out] + ([accum_out] if accum_out is not None else [])
        if op1 is None:
            return self.op(eng, lambda E: E.tensor_scalar(out, in_, s1, s2, op0),
                           reads=rd, writes=wr, tag="ts")
        return self.op(eng,
                       lambda E: E.tensor_scalar(out, in_, s1, s2, op0, op1,
                                                 accum_out=accum_out),
                       reads=rd, writes=wr, tag="ts")

    def stt(self, eng, out, in0, scalar, in1, op0, op1):
        rd = [in0, in1] + ([scalar] if hasattr(scalar, "name") else [])
        return self.op(eng, lambda E: E.scalar_tensor_tensor(out, in0, scalar, in1, op0, op1),
                       reads=rd, writes=[out], tag="stt")

    def act(self, eng, out, in_, fn, bias=0.0, scale=1.0, accum_out=None):
        rd = [in_] + [s for s in (bias, scale) if hasattr(s, "name")]
        wr = [out] + ([accum_out] if accum_out is not None else [])
        return self.op(eng,
                       lambda E: E.activation(out, in_, fn, bias=bias, scale=scale,
                                              accum_out=accum_out),
                       reads=rd, writes=wr, tag="act")

    def red(self, eng, out, in_, aluop, axis=AX.X, negate=None):
        return self.op(eng,
                       lambda E: E.tensor_reduce(out, in_, axis=axis, op=aluop,
                                                 negate=negate),
                       reads=[in_], writes=[out], tag="red")

    def memset(self, eng, ap, v):
        return self.op(eng, lambda E: E.memset(ap, v), writes=[ap], tag="memset")

    def mm(self, psum, lhsT, rhs, start, stop, extra_reads=()):
        return self.op("tensor",
                       lambda E: E.matmul(psum, lhsT, rhs, start=start, stop=stop),
                       reads=[lhsT, rhs, *extra_reads], writes=[psum], tag="mm")

    def mm_acc(self, psum, pairs, extra_reads=()):
        n = len(pairs)
        ops = []
        for i, (l, r) in enumerate(pairs):
            ops.append(self.mm(psum, l, r, i == 0, i == n - 1,
                               extra_reads=extra_reads if i == 0 else ()))
        return ops

    def transpose(self, psum, in_, ident):
        return self.op("tensor", lambda E: E.transpose(psum, in_, ident),
                       reads=[in_, ident], writes=[psum], tag="tr")

    def scan(self, eng, out, data0, data1, initial, op0=ALU.mult, op1=ALU.add):
        rd = [data0, data1] + ([initial] if hasattr(initial, "name") else [])
        return self.op(eng,
                       lambda E: E.tensor_tensor_scan(out, data0, data1, initial,
                                                      op0, op1),
                       reads=rd, writes=[out], tag="scan")

    def mr(self, eng, out, m8, vals, imm):
        return self.op(eng, lambda E: E.match_replace(out, m8, vals, imm),
                       reads=[m8, vals], writes=[out], tag="mr")

    def recip(self, eng, out, in_):
        return self.op(eng, lambda E: E.reciprocal(out, in_),
                       reads=[in_], writes=[out], tag="recip")

    def nop_dep(self, eng, reads, tag="join"):
        return self.op(eng, lambda E: E.nop(), reads=reads, tag=tag)

    def finish(self):
        deps = list(self.out_dmas)
        for e in ENGINES:
            if e != "sync" and self.ops[e]:
                deps.append(self.ops[e][-1])
        o = Op("sync", lambda E: E.nop(), "exec", "finish")
        seen = set()
        for d in deps:
            if id(d) not in seen:
                seen.add(id(d))
                o.deps.append(d)
        self.ops["sync"].append(o)
        self.order.append(o)

    def finalize(self, verbose=False):
        self.finish()
        for e in ENGINES:
            for op in self.ops[e]:
                for d in op.deps:
                    d.inc = True
        for o in self.out_dmas:
            o.inc = True
        sems = {}
        for e in ENGINES:
            c_exec = 0
            dma_i = 0
            dma_counts = {}
            any_exec_inc = False
            for op in self.ops[e]:
                if op.kind == "dma":
                    si = dma_i % self.DMA_SEM_POOL
                    dma_i += 1
                    op.sem_idx = si
                    dma_counts[si] = dma_counts.get(si, 0) + 16
                    op.cnt = dma_counts[si]
                    op.inc = True
                elif op.inc:
                    c_exec += 1
                    op.cnt = c_exec
                    any_exec_inc = True
                else:
                    op.cnt = None
            if any_exec_inc:
                sems[(e, "exec", 0)] = self.stack.enter_context(
                    self.nc.semaphore(f"s_{e}_x"))
            for si in sorted(dma_counts):
                sems[(e, "dma", si)] = self.stack.enter_context(
                    self.nc.semaphore(f"s_{e}_d{si}"))
        self.sems = sems

        nins = {e: len(self.ops[e]) for e in ENGINES if self.ops[e]}
        if verbose:
            print(f"[bk] {self.name}: ops={nins} sbuf_pp={self.sbuf_bytes_pp}")

        block = self.stack.enter_context(self.nc.Block())
        for e in ENGINES:
            if not self.ops[e]:
                continue
            self._emit_section(block, e)

    def _emit_section(self, block, e):
        ops = self.ops[e]
        sems = self.sems

        def section(E):
            seen_wm = {}
            for op in ops:
                for d in op.deps:
                    k = (d.eng, d.kind, d.sem_idx if d.kind == "dma" else 0)
                    sem = sems[k]
                    if seen_wm.get(k, -1) < d.cnt:
                        E.wait_ge(sem, d.cnt)
                        seen_wm[k] = d.cnt
                inst = op.emit(E)
                if op.kind == "dma":
                    inst.then_inc(sems[(e, "dma", op.sem_idx)], 16)
                elif op.inc:
                    inst.then_inc(sems[(e, "exec", 0)], 1)

        getattr(block, e)(section)


def build_nc():
    return bass.Bass("TRN2", target_bir_lowering=False)




NEG_BIG = -1e30


class Ctx:
    def __init__(self, p, T, Tc):
        self.p = p
        self.nc = p.nc
        self.T = T
        self.Tc = Tc
        self.uid = 0
        nc = self.nc
        self.identb_d = nc.dram_tensor("c_identb", [128, 128], BF16,
                                       kind="ExternalInput").ap()
        self.ones_d = nc.dram_tensor("c_ones", [1, 128], F32,
                                     kind="ExternalInput").ap()
        self.identb = p.sb("identb", [128, 128], BF16)
        self.ones = p.sb("ones", [1, 128], F32)
        p.dma("sync", self.identb[:, :], self.identb_d)
        p.dma("sync", self.ones[:, :], self.ones_d)
        self.colconst = {}
        self.wstage = [p.sb("wstage0", [128, 512], F32),
                       p.sb("wstage1", [128, 512], F32)]
        self.seln_d = nc.dram_tensor("c_seln", [16, 16 * 128], BF16,
                                     kind="ExternalInput").ap()
        self.seln = p.sb("seln", [16, 16 * 128], BF16)
        self.rowbuf = p.sb("rowbuf", [1, 1024], F32)
        self.bstage = p.sb("bstage", [128, 512], BF16)
        self._bf16_cache = {}
        p.dma("sync", self.seln[:, :], self.seln_d)
        self._wstage_i = 0
        self.pmm = [p.ps(f"pmm{i}", [128, 512], F32) for i in range(4)]
        self.ptr = [p.ps(f"ptr{i}", [128, 128], BF16) for i in range(2)]
        self.pbc = [p.ps(f"pbc{i}", [128, 512], F32) for i in range(2)]

    def n(self, base):
        self.uid += 1
        return f"{base}_{self.uid}"

    def tmp(self, tag, shape, dtype=F32):
        if not hasattr(self, "_tmps"):
            self._tmps = {}
        key = (tag, tuple(shape), dtype)
        if key not in self._tmps:
            assert not self.p.scopes, f"ctx.tmp({tag}) first-created inside scope"
            self._tmps[key] = self.p.sb(self.n("tmp_" + tag), list(shape), dtype,
                                        root=True)
        return self._tmps[key]

    def prewarm(self):
        self.tmp("scrA", [128, 4096])
        self.tmp("scrB_bf16", [128, 4096], BF16)
        for t in ("ln_sum", "ln_ssq", "ln_mu", "ln_var", "ln_rstd"):
            self.tmp(t, [128, 4])
        self.tmp("sp_e1", [1, 512])
        self.tmp("mb_e1", [128, 256])
        self.tmp("rt_logits", [128, 4, 16])
        self.tmp("rt_probs", [128, 4, 16])
        self.tmp("rt_nmax", [128, 1])
        self.tmp("rt_esum", [128, 1])
        self.tmp("tk_m8", [128, 8])
        self.tmp("tk_mr1", [128, 16])
        self.tmp("tk_mr2", [128, 16])
        self.tmp("tk_oh", [128, 4, 32])
        self.tmp("tk_mx", [128, 4, 2])
        self.tmp("ff_tmp", [128, 512])
        self.tmp("f1_s12", [128, 1])
        self.tmp("f1_c1", [128, 1])
        self.tmp("f1_c2", [128, 1])
        self.tmp("f2_w2c", [128, 1])
        self.const_col(1e-5)
        self.const_col(1.0)

    def const_col(self, val, P=128):
        if val not in self.colconst:
            t = self.p.sb(self.n("ccol"), [128, 1], F32)
            self.p.memset("vector", t[:, :], float(val))
            self.colconst[val] = t
        return self.colconst[val][0:P, :]

    def wload(self, dram_ap, K, M, name="w", dt=BF16):
        p = self.p
        G = K // 128
        assert K % 128 == 0
        out = p.sb(self.n(name), [128, G, M], dt)
        for g in range(G):
            mo = 0
            while mo < M:
                mw = min(512, M - mo)
                st = self.wstage[self._wstage_i]
                self._wstage_i = (self._wstage_i + 1) % len(self.wstage)
                p.dma("sync", st[:, 0:mw], dram_ap[g * 128:(g + 1) * 128, mo:mo + mw])
                p.cp("vector", out[:, g, mo:mo + mw], st[:, 0:mw])
                mo += mw
        return out

    def to_bf16_dram(self, dram_f32_ap, K, M, name):
        if name in self._bf16_cache:
            return self._bf16_cache[name]
        p = self.p
        out = self.nc.dram_tensor(name, [K, M], BF16).ap()
        for g in range(K // 128):
            mo = 0
            while mo < M:
                mw = min(512, M - mo)
                st = self.wstage[self._wstage_i]
                self._wstage_i = (self._wstage_i + 1) % len(self.wstage)
                p.dma("sync", st[:, 0:mw],
                      dram_f32_ap[g * 128:(g + 1) * 128, mo:mo + mw])
                p.cp("gpsimd", self.bstage[:, 0:mw], st[:, 0:mw])
                p.dma("scalar", out[g * 128:(g + 1) * 128, mo:mo + mw],
                      self.bstage[:, 0:mw])
                mo += mw
        self._bf16_cache[name] = out
        return out

    def wload_into(self, out, dram_ap, K, M, eng="gpsimd"):
        p = self.p
        G = K // 128
        for g in range(G):
            mo = 0
            while mo < M:
                mw = min(512, M - mo)
                st = self.wstage[self._wstage_i]
                self._wstage_i = (self._wstage_i + 1) % len(self.wstage)
                p.dma("sync", st[:, 0:mw], dram_ap[g * 128:(g + 1) * 128, mo:mo + mw])
                p.cp(eng, out[:, g, mo:mo + mw], st[:, 0:mw])
                mo += mw
        return out

    def row_tile_into(self, out, dram_row_ap, W, rowbuf):
        p = self.p
        p.dma("sync", rowbuf[0:1, 0:W], dram_row_ap)
        o = 0
        while o < W:
            w = min(512, W - o)
            pb = self.pbc[0]
            p.mm(pb[:, 0:w], self.ones[0:1, :], rowbuf[0:1, o:o + w], True, True)
            p.cp("vector", out[:, o:o + w], pb[:, 0:w])
            o += w
        return out

    def vload(self, dram_ap, G, name="v"):
        p = self.p
        out = p.sb(self.n(name), [128, G], F32)
        p.dma("sync", out[:, :], dram_ap)
        return out

    def row_tile(self, dram_row_ap, W, name="row", dt=F32):
        p = self.p
        out = p.sb(self.n(name), [128, W], dt)
        return self.row_tile_into(out, dram_row_ap, W, self.rowbuf)

    def transpose_tmaj_to_dmaj(self, dst, src_bf16, ntt, dgroups,
                               src_t0=0, src_d0=0, dst_t0=0, dst_g0=0):
        p = self.p
        for tt in range(ntt):
            for g in range(dgroups):
                pt = self.ptr[(tt * dgroups + g) % 2]
                p.transpose(pt[:, :],
                            src_bf16[:, src_t0 + tt,
                                     src_d0 + g * 128: src_d0 + (g + 1) * 128],
                            self.identb[:, :])
                p.cp("vector",
                     dst[:, dst_g0 + g, dst_t0 + tt * 128: dst_t0 + (tt + 1) * 128],
                     pt[:, :])


def _tn(t):
    try:
        return t.name
    except AttributeError:
        return t[:, 0:1].name


def bct(ap2, ntt, D):
    return ap2.unsqueeze(2).broadcast_to([128, ntt, D])


def ln_chunk(ctx, x_ap, ntt, D, out_ap, gam_row=None, beta_row=None):
    p = ctx.p
    sm = ctx.tmp("ln_sum", [128, 4])[:, 0:ntt]
    sq = ctx.tmp("scrA", [128, 4096])[:, 0:D]
    ssq = ctx.tmp("ln_ssq", [128, 4])[:, 0:ntt]
    mu = ctx.tmp("ln_mu", [128, 4])[:, 0:ntt]
    var = ctx.tmp("ln_var", [128, 4])[:, 0:ntt]
    rstd = ctx.tmp("ln_rstd", [128, 4])[:, 0:ntt]
    xc = ctx.tmp("scrA", [128, 4096])[:, 0:ntt * D].rearrange(
        "p (a b) -> p a b", b=D)
    p.red("vector", sm, x_ap, ALU.add)
    for tt in range(ntt):
        p.act("scalar", sq, x_ap[:, tt, :], AF.Square,
              accum_out=ssq[:, tt:tt + 1])
    p.ts("vector", mu, sm, 1.0 / D, None, ALU.mult)
    p.ts("vector", ssq, ssq, 1.0 / D, None, ALU.mult)
    p.tt("vector", var, mu, mu, ALU.mult)
    p.tt("vector", var, ssq, var, ALU.subtract)
    p.act("scalar", rstd, var, AF.Ln, bias=ctx.const_col(1e-5))
    p.act("scalar", rstd, rstd, AF.Exp, scale=-0.5)
    p.tt("vector", xc, x_ap, bct(rstd if False else mu, ntt, D), ALU.subtract)
    if gam_row is None:
        p.tt("vector", out_ap, xc, bct(rstd, ntt, D), ALU.mult)
    else:
        p.tt("vector", xc, xc, bct(rstd, ntt, D), ALU.mult)
        p.tt("vector", xc, xc,
             gam_row[:, :].unsqueeze(1).broadcast_to([128, ntt, D]), ALU.mult)
        p.tt("vector", out_ap, xc,
             beta_row[:, :].unsqueeze(1).broadcast_to([128, ntt, D]), ALU.add)
    return mu, rstd


def softplus(ctx, out_ap, in_ap, shape):
    p = ctx.p
    e1 = ctx.tmp("sp_e1", [1, 512])[0:shape[0], 0:shape[1]]
    sl = tuple([slice(None)] * len(shape))
    p.act("scalar", e1[sl], in_ap, AF.Abs)
    p.act("scalar", e1[sl], e1[sl], AF.Exp, scale=-1.0)
    p.act("scalar", e1[sl], e1[sl], AF.Ln, bias=ctx.const_col(1.0, shape[0]))
    p.act("scalar", out_ap, in_ap, AF.Relu)
    p.tt("vector", out_ap, out_ap, e1[sl], ALU.add)


def conv_silu(ctx, xc, xi, cw, cb, G, Tc, tmp):
    p = ctx.p
    for k in range(4):
        cwb = cw[:, :, k].broadcast_to([128, G]).unsqueeze(2).broadcast_to([128, G, Tc]) \
            if False else cw[:, :, k:k + 1].broadcast_to([128, G, Tc])
        if k == 0:
            p.tt("vector", xc[:, :, 0:Tc], xi[:, :, 0:Tc], cwb, ALU.mult)
        else:
            p.tt("vector", tmp[:, :, 0:Tc], xi[:, :, k:k + Tc], cwb, ALU.mult)
            p.tt("vector", xc[:, :, 0:Tc], xc[:, :, 0:Tc], tmp[:, :, 0:Tc], ALU.add)
    p.tt("vector", xc[:, :, 0:Tc], xc[:, :, 0:Tc],
         cb[:, :].unsqueeze(2).broadcast_to([128, G, Tc]), ALU.add)
    p.act("scalar", xc[:, :, 0:Tc], xc[:, :, 0:Tc], AF.Silu)


def emit_l2_attn(ctx, x_dram, xo_dram, W):
    p = ctx.p
    T, Tc = ctx.T, ctx.Tc
    ntt = Tc // 128
    nch = T // Tc
    G = 8
    N = 16

    win = ctx.wload(W["W_in"], 1024, 2048, "l2win")
    wxp = ctx.wload(W["W_xproj"], 1024, 33, "l2wxp")
    wout = ctx.wload(W["W_out"], 1024, 1024, "l2wout")
    cw = ctx.vload(W["conv_w"], 32, "l2cw")
    cb = ctx.vload(W["conv_b"], G, "l2cb")
    Dp = ctx.vload(W["D_param"], G, "l2dp")
    Alog = ctx.vload(W["log_A"], G * N, "l2alog")
    A = p.sb("l2A", [128, G * N], F32)
    p.act("scalar", A[:, :], Alog[:, :], AF.Exp)
    p.ts("vector", A[:, :], A[:, :], -1.0, None, ALU.mult)

    xt = p.sb("l2_xt", [128, ntt, 1024], F32)
    scrB = ctx.tmp("scrB_bf16", [128, 4096], BF16)
    xnb = scrB[:, 0:ntt * 1024].rearrange("p (a b) -> p a b", b=1024)
    xnT = p.sb("l2_xnT", [128, G, Tc], BF16)
    xi = p.sb("l2_xi", [128, G, Tc + 3], F32)
    z = p.sb("l2_z", [128, G, Tc], F32)
    xc = p.sb("l2_xc", [128, G, Tc], F32)
    xcb = scrB[:, 0:G * Tc].rearrange("p (a b) -> p a b", b=Tc)
    gtmp = ctx.tmp("scrA", [128, 4096])[:, 0:G * Tc].rearrange(
        "p (a b) -> p a b", b=Tc)
    sB = p.sb("l2_sB", [16, Tc], F32)
    sC = p.sb("l2_sC", [16, Tc], F32)
    sDl = p.sb("l2_sDl", [1, Tc], F32)
    brow = p.sb("l2_brow", [1, Tc], F32)
    crow = p.sb("l2_crow", [1, Tc], F32)
    dtrow = p.sb("l2_dtrow", [1, Tc], F32)
    dt_sb = p.sb("l2_dt", [128, Tc], F32)
    at = [p.sb("l2_a0", [128, Tc], F32)]
    ut = [p.sb("l2_u0", [128, Tc], F32)]
    ht = [p.sb(f"l2_h{i}", [128, Tc], F32) for i in range(2)]
    stmp = p.sb("l2_stmp", [128, Tc], F32)
    y = p.sb("l2_y", [128, G, Tc], F32)
    yvb = scrB[:, 0:G * Tc].rearrange("p (a b) -> p a b", b=Tc)
    hst = p.sb("l2_hst", [128, G, N], F32)

    p.memset("vector", hst[:, :, :], 0.0)
    p.memset("vector", xi[:, :, 0:3], 0.0)
    cw3 = cw[:, :].rearrange("p (g k) -> p g k", k=4)

    for c in range(nch):
        t0 = c * Tc
        for tt in range(ntt):
            r0 = t0 + tt * 128
            p.dma("sync", xt[:, tt, :], x_dram[r0:r0 + 128, :])
        ln_chunk(ctx, xt[:, 0:ntt, :], ntt, 1024, xnb[:, 0:ntt, :])
        ctx.transpose_tmaj_to_dmaj(xnT, xnb, ntt, G)
        if c > 0:
            p.cp("vector", xi[:, :, 0:3], xi[:, :, Tc:Tc + 3])
        for j in range(16):
            pm = ctx.pmm[j % 2]
            p.mm_acc(pm[:, 0:Tc],
                     [(win[:, k, j * 128:(j + 1) * 128], xnT[:, k, 0:Tc])
                      for k in range(8)])
            if j < 8:
                p.cp("vector", xi[:, j, 3:3 + Tc], pm[:, 0:Tc])
            else:
                p.cp("vector", z[:, j - 8, 0:Tc], pm[:, 0:Tc])
        conv_silu(ctx, xc, xi, cw3, cb, G, Tc, gtmp)
        p.cp("vector", xcb[:, :, :], xc[:, :, :])
        pB = ctx.pmm[2]
        pC = ctx.pmm[3]
        pD = ctx.pmm[0]
        p.mm_acc(pB[0:16, 0:Tc], [(wxp[:, k, 0:16], xcb[:, k, 0:Tc]) for k in range(8)])
        p.mm_acc(pC[0:16, 0:Tc], [(wxp[:, k, 16:32], xcb[:, k, 0:Tc]) for k in range(8)])
        p.mm_acc(pD[0:1, 0:Tc], [(wxp[:, k, 32:33], xcb[:, k, 0:Tc]) for k in range(8)])
        p.cp("vector", sB[:, :], pB[0:16, 0:Tc])
        p.cp("vector", sC[:, :], pC[0:16, 0:Tc])
        p.cp("vector", sDl[:, :], pD[0:1, 0:Tc])
        softplus(ctx, dtrow[:, :], sDl[:, :], [1, Tc])
        assert Tc <= 512
        pb = ctx.pbc[0]
        p.mm(pb[:, 0:Tc], ctx.ones[0:1, :], dtrow[0:1, 0:Tc], True, True)
        p.cp("vector", dt_sb[:, 0:Tc], pb[:, 0:Tc])
        for n in range(N):
            pBb = ctx.pbc[0]
            pCb = ctx.pbc[1]
            p.dma("sync", brow[:, :], sB[n:n + 1, :])
            p.dma("sync", crow[:, :], sC[n:n + 1, :])
            p.tt("vector", brow[:, :], brow[:, :], dtrow[:, :], ALU.mult)
            p.mm(pBb[:, 0:Tc], ctx.ones[0:1, :], brow[0:1, 0:Tc], True, True)
            p.mm(pCb[:, 0:Tc], ctx.ones[0:1, :], crow[0:1, 0:Tc], True, True)
            for g in range(G):
                a = at[0]
                u = ut[0]
                h = ht[g % 2]
                p.act("scalar", a[:, 0:Tc], dt_sb[:, 0:Tc], AF.Exp,
                      scale=A[:, g * N + n: g * N + n + 1])
                p.tt("vector", u[:, 0:Tc], xc[:, g, 0:Tc], pBb[:, 0:Tc], ALU.mult)
                p.scan("vector", h[:, 0:Tc], a[:, 0:Tc], u[:, 0:Tc],
                       hst[:, g, n:n + 1])
                p.cp("vector", hst[:, g, n:n + 1], h[:, Tc - 1:Tc])
                if n == 0:
                    p.tt("vector", y[:, g, 0:Tc], h[:, 0:Tc], pCb[:, 0:Tc], ALU.mult)
                else:
                    p.tt("vector", stmp[:, 0:Tc], h[:, 0:Tc], pCb[:, 0:Tc], ALU.mult)
                    p.tt("vector", y[:, g, 0:Tc], y[:, g, 0:Tc], stmp[:, 0:Tc],
                         ALU.add)
        p.tt("vector", gtmp[:, :, :], xc[:, :, :],
             Dp[:, :].unsqueeze(2).broadcast_to([128, G, Tc]), ALU.mult)
        p.tt("vector", y[:, :, :], y[:, :, :], gtmp[:, :, :], ALU.add)
        p.act("scalar", z[:, :, :], z[:, :, :], AF.Silu)
        p.tt("vector", y[:, :, :], y[:, :, :], z[:, :, :], ALU.mult)
        p.cp("vector", yvb[:, :, :], y[:, :, :])
        for m in range(ntt):
            for no in range(0, 1024, 512):
                pm = ctx.pmm[(m * 2 + no // 512) % 2]
                p.mm_acc(pm[:, 0:512],
                         [(yvb[:, k, m * 128:(m + 1) * 128],
                           wout[:, k, no:no + 512]) for k in range(G)])
                p.tt("vector", xt[:, m, no:no + 512], xt[:, m, no:no + 512],
                     pm[:, 0:512], ALU.add)
        for tt in range(ntt):
            r0 = t0 + tt * 128
            p.dma("sync", xo_dram[r0:r0 + 128, :], xt[:, tt, :], out_is_output=True)



class MambaCfg:

    def __init__(self, tag, d_in, DI, R, rev, W, stream_inw=False):
        self.tag = tag
        self.d_in = d_in
        self.DI = DI
        self.GI = DI // 128
        self.R = R
        self.rev = rev
        self.W = W
        self.stream_inw = stream_inw


def mamba_dir_setup(ctx, cfg):
    p = ctx.p
    t = cfg.tag
    h = {}
    if not cfg.stream_inw:
        h["inw"] = ctx.wload(cfg.W["inw_lhsT"], cfg.d_in, 2 * cfg.DI, t + "inw")
    h["xproj"] = ctx.wload(cfg.W["xproj_lhsT"], cfg.DI, cfg.R + 32, t + "xp")
    dtw = p.sb(ctx.n(t + "dtw"), [cfg.R, cfg.DI], BF16)
    mo = 0
    while mo < cfg.DI:
        mw = min(512, cfg.DI - mo)
        st = ctx.wstage[0]
        p.dma("sync", st[0:cfg.R, 0:mw], cfg.W["dtw_lhsT"][:, mo:mo + mw])
        p.cp("gpsimd", dtw[:, mo:mo + mw], st[0:cfg.R, 0:mw])
        mo += mw
    h["dtw"] = dtw
    h["cw"] = ctx.vload(cfg.W["conv_w"], cfg.GI * 4, t + "cw")
    h["cb"] = ctx.vload(cfg.W["conv_b"], cfg.GI, t + "cb")
    h["dtb"] = ctx.vload(cfg.W["dt_b"], cfg.GI, t + "dtb")
    h["Dp"] = ctx.vload(cfg.W["D"], cfg.GI, t + "Dp")
    Alog = ctx.vload(cfg.W["A_log"], cfg.GI * 16, t + "Alog")
    A = p.sb(ctx.n(t + "A"), [128, cfg.GI * 16], F32)
    p.act("scalar", A[:, :], Alog[:, :], AF.Exp)
    p.ts("vector", A[:, :], A[:, :], -1.0, None, ALU.mult)
    h["A"] = A
    return h


def mamba_dir_chunk(ctx, cfg, h, buf, xnT_ap, kgroups, ci, nch, yv_out):
    p = ctx.p
    Tc = ctx.Tc
    GI, R, N = cfg.GI, cfg.R, 16
    rev = cfg.rev
    first = (ci == (nch - 1 if rev else 0))
    xi, z, xc, xcb = buf["xi"], buf["z"], buf["xc"], buf["xcb"]
    dt, dtx = buf["dt"], buf["dtx"]
    hst = buf["hst"]
    sB, sC, sDt = buf["sB"], buf["sC"], buf["sDt"]
    dtrawb = buf["dtrawb"]
    y = buf["y"]

    dataoff = 3 if not rev else 0
    haloslice = (slice(0, 3) if not rev else slice(Tc, Tc + 3))
    saveslice = (slice(Tc, Tc + 3) if not rev else slice(0, 3))
    if first:
        p.memset("vector", hst[:, :, :], 0.0)
        p.memset("vector", xi[:, :, haloslice], 0.0)
    else:
        p.cp("vector", xi[:, :, haloslice], buf["halo3"][:, :, :])
    if cfg.stream_inw:
        wj = buf["wjk"]
        inwb = buf["inwb_dram"]
        for j in range(2 * GI):
            wjt = wj[j % 2]
            for ki, kg in enumerate(kgroups):
                p.dma("sync", wjt[:, ki, :],
                      inwb[kg * 128:(kg + 1) * 128, j * 128:(j + 1) * 128])
            pm = ctx.pmm[j % 2]
            p.mm_acc(pm[:, 0:Tc],
                     [(wjt[:, ki, :], xnT_ap[:, kg, 0:Tc])
                      for ki, kg in enumerate(kgroups)])
            if j < GI:
                p.cp("vector", xi[:, j, dataoff:dataoff + Tc], pm[:, 0:Tc])
            else:
                p.cp("vector", z[:, j - GI, 0:Tc], pm[:, 0:Tc])
    else:
        inw = h["inw"]
        for j in range(2 * GI):
            pm = ctx.pmm[j % 2]
            p.mm_acc(pm[:, 0:Tc],
                     [(inw[:, ki, j * 128:(j + 1) * 128], xnT_ap[:, kg, 0:Tc])
                      for ki, kg in enumerate(kgroups)])
            if j < GI:
                p.cp("vector", xi[:, j, dataoff:dataoff + Tc], pm[:, 0:Tc])
            else:
                p.cp("vector", z[:, j - GI, 0:Tc], pm[:, 0:Tc])
    cw3 = h["cw"][:, :].rearrange("p (g k) -> p g k", k=4)
    gt = buf["gtmp"]
    acc = dt
    for k in range(4):
        wk = cw3[:, :, (k if not rev else 3 - k): (k if not rev else 3 - k) + 1]
        cwb = wk.broadcast_to([128, GI, Tc])
        if k == 0:
            p.tt("vector", acc[:, :, 0:Tc], xi[:, :, k:k + Tc], cwb, ALU.mult)
        else:
            p.tt("vector", gt[:, :, 0:Tc], xi[:, :, k:k + Tc], cwb, ALU.mult)
            p.tt("vector", acc[:, :, 0:Tc], acc[:, :, 0:Tc], gt[:, :, 0:Tc],
                 ALU.add)
    p.cp("vector", buf["halo3"][:, :, :], xi[:, :, saveslice])
    p.tt("vector", acc[:, :, 0:Tc], acc[:, :, 0:Tc],
         h["cb"][:, :].unsqueeze(2).broadcast_to([128, GI, Tc]), ALU.add)
    p.act("scalar", xc[:, :, 0:Tc], acc[:, :, 0:Tc], AF.Silu)
    xp = h["xproj"]
    pDt, pB, pC = ctx.pmm[2], ctx.pmm[3], ctx.pmm[0]
    p.mm_acc(pDt[0:R, 0:Tc], [(xp[:, k, 0:R], xcb[:, k, 0:Tc]) for k in range(GI)])
    p.mm_acc(pB[0:16, 0:Tc], [(xp[:, k, R:R + 16], xcb[:, k, 0:Tc])
                              for k in range(GI)])
    p.mm_acc(pC[0:16, 0:Tc], [(xp[:, k, R + 16:R + 32], xcb[:, k, 0:Tc])
                              for k in range(GI)])
    p.cp("vector", sB[:, :], pB[0:16, 0:Tc])
    p.cp("vector", sC[:, :], pC[0:16, 0:Tc])
    p.cp("vector", dtrawb[:, :], pDt[0:R, 0:Tc])
    for m in range(GI):
        pm = ctx.pmm[1]
        p.mm(pm[:, 0:Tc], h["dtw"][:, m * 128:(m + 1) * 128], dtrawb[:, 0:Tc],
             True, True)
        dtb_col = h["dtb"][:, m:m + 1]
        e1 = ctx.tmp("mb_e1", [128, 256])
        p.act("scalar", e1[:, 0:Tc], pm[:, 0:Tc], AF.Abs, bias=dtb_col)
        p.act("scalar", e1[:, 0:Tc], e1[:, 0:Tc], AF.Exp, scale=-1.0)
        p.act("scalar", e1[:, 0:Tc], e1[:, 0:Tc], AF.Ln, bias=ctx.const_col(1.0))
        p.act("scalar", dt[:, m, 0:Tc], pm[:, 0:Tc], AF.Relu, bias=dtb_col)
        p.tt("vector", dt[:, m, 0:Tc], dt[:, m, 0:Tc], e1[:, 0:Tc], ALU.add)
    p.tt("vector", dtx[:, :, 0:Tc], dt[:, :, 0:Tc], xc[:, :, 0:Tc], ALU.mult)
    A = h["A"]
    at = buf["at"]
    h_all, B_ball, C_ball = buf["h_all"], buf["B_ball"], buf["C_ball"]
    for n in range(N):
        pBb = ctx.pbc[n % 2]
        pCb = ctx.pmm[2 + n % 2]
        seln = ctx.seln[:, n * 128:(n + 1) * 128]
        p.mm(pBb[:, 0:Tc], seln, sB[0:16, 0:Tc], True, True)
        p.mm(pCb[:, 0:Tc], seln, sC[0:16, 0:Tc], True, True)
        p.cp("vector", B_ball[:, n, 0:Tc], pBb[:, 0:Tc])
        p.cp("vector", C_ball[:, n, 0:Tc], pCb[:, 0:Tc])
    a_all = buf["a_all"]
    for g in range(GI):
        emit_scan_group(ctx, h_all, a_all, at, dtx, B_ball, C_ball, dt, A,
                        hst, y, g, GI, N, Tc, rev)
    p.tt("vector", gt[:, :, 0:Tc], xc[:, :, 0:Tc],
         h["Dp"][:, :].unsqueeze(2).broadcast_to([128, GI, Tc]), ALU.mult)
    p.tt("vector", y[:, :, 0:Tc], y[:, :, 0:Tc], gt[:, :, 0:Tc], ALU.add)
    zs = buf["at"][0]
    for g in range(GI):
        p.act("scalar", zs[:, 0:Tc], z[:, g, 0:Tc], AF.Silu)
        p.tt("vector", y[:, g, 0:Tc], y[:, g, 0:Tc], zs[:, 0:Tc], ALU.mult)
    p.cp("vector", yv_out[:, :, :], y[:, :, 0:Tc])


def emit_scan_group(ctx, h_all, a_all, at, dtx_c, Bb_c, Cb_c, dt_c, A,
                    hst, y, g, GI, N, Tc, rev):
    p = ctx.p
    lo = 1 if not rev else 0
    padc = 0 if not rev else Tc
    hk = _tn(h_all)
    hkey = f"{hk}@{g % 2}"
    akey = f"{_tn(a_all)}@{g % 2}"
    hstk = f"{_tn(hst)}@{g}"
    p.op("vector", (lambda ha=h_all, dx=dtx_c, bb=Bb_c, g=g:
                    lambda E: E.tensor_tensor(
                        ha[:, :, lo:lo + Tc],
                        dx[:, g, 0:Tc].unsqueeze(1).broadcast_to([128, N, Tc]),
                        bb[:, :, 0:Tc], ALU.mult))(),
         reads=[dtx_c[:, 0:1, 0:1], Bb_c[:, 0:1, 0:1]], writes=[hkey],
         tag="ubulk")
    p.op("vector", (lambda ha=h_all, hs=hst, g=g:
                    lambda E: E.tensor_copy(ha[:, :, padc:padc + 1],
                                            hs[:, g, :].unsqueeze(2)))(),
         reads=[hstk], writes=[hkey], tag="hinj")
    for n in range(N):
        p.op("scalar", (lambda aa=a_all, dc=dt_c, g=g, n=n:
                        lambda E: E.activation(
                            aa[:, n, lo:lo + Tc], dc[:, g, 0:Tc], AF.Exp,
                            scale=A[:, g * N + n: g * N + n + 1]))(),
             reads=[dt_c[:, 0:1, 0:1], A[:, 0:1]], writes=[akey + f"_{n}"],
             tag="aexp")
    akeys = [akey + f"_{n}" for n in range(N)]
    Tp = Tc + 1
    if not rev:
        p.op("vector", (lambda ha=h_all, aa=a_all:
                        lambda E: E.tensor_tensor_scan(
                            ha[:, :, :].rearrange("p a b -> p (a b)"),
                            aa[:, :, :].rearrange("p a b -> p (a b)"),
                            ha[:, :, :].rearrange("p a b -> p (a b)"),
                            0.0, ALU.mult, ALU.add))(),
             reads=[hkey] + akeys, writes=[hkey], tag="scan")
        p.op("vector", (lambda ha=h_all, hs=hst, g=g:
                        lambda E: E.tensor_copy(
                            hs[:, g, :].unsqueeze(2),
                            ha[:, :, Tc:Tc + 1]))(),
             reads=[hkey], writes=[hstk], tag="hsave")
    else:
        p.op("vector", (lambda ha=h_all, aa=a_all:
                        lambda E: E.tensor_tensor_scan(
                            ha[:, :, :].rearrange("p a b -> p (a b)")[:, ::-1],
                            aa[:, :, :].rearrange("p a b -> p (a b)")[:, ::-1],
                            ha[:, :, :].rearrange("p a b -> p (a b)")[:, ::-1],
                            0.0, ALU.mult, ALU.add))(),
             reads=[hkey] + akeys, writes=[hkey], tag="scan")
        p.op("vector", (lambda ha=h_all, hs=hst, g=g:
                        lambda E: E.tensor_copy(
                            hs[:, g, :].unsqueeze(2), ha[:, :, 0:1]))(),
             reads=[hkey], writes=[hstk], tag="hsave")
    p.op("vector", (lambda ha=h_all, cb=Cb_c:
                    lambda E: E.tensor_tensor(
                        ha[:, :, lo:lo + Tc], ha[:, :, lo:lo + Tc],
                        cb[:, :, 0:Tc], ALU.mult))(),
         reads=[Cb_c[:, 0:1, 0:1], hkey], writes=[hkey], tag="cmul")
    w = N
    while w > 2:
        w //= 2
        p.op("vector", (lambda ha=h_all, w=w:
                        lambda E: E.tensor_tensor(
                            ha[:, 0:w, lo:lo + Tc], ha[:, 0:w, lo:lo + Tc],
                            ha[:, w:2 * w, lo:lo + Tc], ALU.add))(),
             reads=[hkey], writes=[hkey], tag="tree")
    p.op("vector", (lambda ha=h_all, yy=y, g=g:
                    lambda E: E.tensor_tensor(
                        yy[:, g, 0:Tc], ha[:, 0, lo:lo + Tc],
                        ha[:, 1, lo:lo + Tc], ALU.add))(),
         reads=[hkey], writes=[y[:, 0:1, 0:1]], tag="yg")


def mamba_bufs(ctx, GI, Tc, R, scrB):
    p = ctx.p
    b = {}
    b["xi"] = p.sb(ctx.n("mb_xi"), [128, GI, Tc + 3], F32)
    b["y"] = b["xi"][:, :, 0:Tc]
    b["halo3"] = p.sb(ctx.n("mb_halo"), [128, GI, 3], F32)
    b["z"] = p.sb(ctx.n("mb_z"), [128, GI, Tc], BF16)
    b["xc"] = p.sb(ctx.n("mb_xc"), [128, GI, Tc], BF16)
    b["xcb"] = b["xc"]
    b["dt"] = p.sb(ctx.n("mb_dt"), [128, GI, Tc], F32)
    b["dtx"] = ctx.tmp("scrA", [128, 4096])[:, 0:GI * Tc].rearrange(
        "p (a b) -> p a b", b=Tc)
    b["hst"] = p.sb(ctx.n("mb_hst"), [128, GI, 16], F32)
    b["sB"] = p.sb(ctx.n("mb_sB"), [16, Tc], BF16)
    b["sC"] = p.sb(ctx.n("mb_sC"), [16, Tc], BF16)
    b["sDt"] = None
    b["dtrawb"] = p.sb(ctx.n("mb_dtraw"), [R, Tc], BF16)
    b["at"] = [p.sb(ctx.n(f"mb_at{i}"), [128, Tc], F32) for i in range(2)]
    b["h_all"] = p.sb(ctx.n("mb_hall"), [128, 16, Tc + 1], F32)
    b["a_all"] = p.sb(ctx.n("mb_aall"), [128, 16, Tc + 1], F32)
    p.memset("vector", b["a_all"][:, :, 0:1], 0.0)
    p.memset("vector", b["a_all"][:, :, Tc:Tc + 1], 0.0)
    b["B_ball"] = p.sb(ctx.n("mb_Bball"), [128, 16, Tc], BF16)
    b["C_ball"] = p.sb(ctx.n("mb_Cball"), [128, 16, Tc], BF16)

    b["gtmp"] = ctx.tmp("scrA", [128, 4096])[:, 0:GI * Tc].rearrange(
        "p (a b) -> p a b", b=Tc)
    return b


def _halo_save(ctx, cfg, buf):
    p = ctx.p
    Tc = ctx.Tc
    if not cfg.rev:
        p.cp("vector", buf["halo3"][:, :, :], buf["xi"][:, :, Tc:Tc + 3])
    else:
        p.cp("vector", buf["halo3"][:, :, :], buf["xi"][:, :, 0:3])


def emit_l0_dir(ctx, x_dram, xo_dram, W, rev):
    p = ctx.p
    T, Tc = ctx.T, ctx.Tc
    ntt = Tc // 128
    nch = T // Tc
    GI, R = 8, 32
    cfg = MambaCfg("l0b" if rev else "l0f", 512, 1024, 32, rev, W)
    h = mamba_dir_setup(ctx, cfg)
    outw = ctx.wload(W["outw_rhs"], 1024, 512, cfg.tag + "ow")
    gam = ctx.row_tile(W["gam_row"], 1024, cfg.tag + "g")
    bet = ctx.row_tile(W["beta_row"], 1024, cfg.tag + "b")
    scrB = ctx.tmp("scrB_bf16", [128, 4096], BF16)

    xt = p.sb(ctx.n("l0_xt"), [128, ntt, 1024], F32)
    xn_sc = p.sb(ctx.n("l0_xn"), [128, ntt, 1024], F32)
    xn2b = scrB[:, 0:ntt * 1024].rearrange("p (a b) -> p a b", b=1024)
    xn2T = p.sb(ctx.n("l0_xn2T"), [128, 8, Tc], BF16)
    buf = mamba_bufs(ctx, GI, Tc, R, scrB)
    yv = scrB[:, 0:GI * Tc].rearrange("p (a b) -> p a b", b=Tc)
    ot = p.sb(ctx.n("l0_ot"), [128, ntt, 512], F32)

    kgroups = [4, 5, 6, 7] if rev else [0, 1, 2, 3]
    c0, c1 = (0, 512) if not rev else (512, 1024)
    order = range(nch - 1, -1, -1) if rev else range(nch)
    xnd = W.get("_xnd")
    x2Td = W.get("_x2Td")
    for ci in order:
        t0 = ci * Tc
        if rev and xnd is not None:
            for tt in range(ntt):
                r0 = t0 + tt * 128
                p.dma("sync", xt[:, tt, :], x_dram[r0:r0 + 128, :])
                p.dma("gpsimd", xn_sc[:, tt, :], xnd[r0:r0 + 128, :])
            for g in range(8):
                p.dma("scalar", xn2T[:, g, 0:Tc], x2Td[:, g, t0:t0 + Tc])
        else:
            for tt in range(ntt):
                r0 = t0 + tt * 128
                p.dma("sync", xt[:, tt, :], x_dram[r0:r0 + 128, :])
            ln_chunk(ctx, xt[:, 0:ntt, :], ntt, 1024, xn_sc[:, 0:ntt, :])
            ln_chunk(ctx, xn_sc[:, 0:ntt, :], ntt, 1024, xn2b[:, 0:ntt, :],
                     gam_row=gam, beta_row=bet)
            ctx.transpose_tmaj_to_dmaj(xn2T, xn2b, ntt, 8)
            if xnd is not None:
                for tt in range(ntt):
                    r0 = t0 + tt * 128
                    p.dma("scalar", xnd[r0:r0 + 128, :], xn_sc[:, tt, :])
                for g in range(8):
                    p.dma("gpsimd", x2Td[:, g, t0:t0 + Tc], xn2T[:, g, 0:Tc])
        mamba_dir_chunk(ctx, cfg, h, buf, xn2T, kgroups, ci, nch, yv)
        for m in range(ntt):
            pm = ctx.pmm[m % 2]
            p.mm_acc(pm[:, 0:512],
                     [(yv[:, k, m * 128:(m + 1) * 128], outw[:, k, 0:512])
                      for k in range(GI)])
            p.tt("vector", ot[:, m, :], xt[:, m, c0:c1], pm[:, 0:512], ALU.add)
            p.tt("vector", ot[:, m, :], ot[:, m, :], xn_sc[:, m, c0:c1], ALU.add)
        for tt in range(ntt):
            r0 = t0 + tt * 128
            p.dma("sync", xo_dram[r0:r0 + 128, c0:c1], ot[:, tt, :],
                  out_is_output=True)


def _stream_upproj(ctx, wdram, KG, m, rhs_tensor, GI, Tc, psum, wb):
    p = ctx.p
    wt = wb[m % 2]
    for k in range(KG):
        p.dma("sync", wt[:, k, :], wdram[k * 128:(k + 1) * 128,
                                         m * 128:(m + 1) * 128])
    p.mm_acc(psum[:, 0:Tc], [(wt[:, k, :], rhs_tensor[:, k, 0:Tc])
                             for k in range(KG)])


def emit_l1_phA(ctx, x_dram, W, rev, xcd, zd, tag):
    p = ctx.p
    T = ctx.T
    Tc = 512
    ntt = Tc // 128
    nch = T // Tc
    GI = 16
    inw = ctx.wload(W["inw_lhsT"], 1024, 4096, tag + "inwA")
    cw = ctx.vload(W["conv_w"], GI * 4, tag + "cwA")
    cb = ctx.vload(W["conv_b"], GI, tag + "cbA")
    gam = ctx.row_tile(W["gam_row"], 1024, tag + "gA", dt=BF16)
    bet = ctx.row_tile(W["beta_row"], 1024, tag + "bA", dt=BF16)
    scrB = ctx.tmp("scrB_bf16", [128, 4096], BF16)
    xt = p.sb(ctx.n(tag + "A_xt"), [128, ntt, 1024], F32)
    xn2b = scrB[:, 0:ntt * 1024].rearrange("p (a b) -> p a b", b=1024)
    xn2T = [p.sb(ctx.n(tag + f"A_x2T{i}"), [128, 8, Tc], BF16) for i in range(2)]
    xi = p.sb(ctx.n(tag + "A_xi"), [128, GI, Tc + 3], F32)
    halo3 = p.sb(ctx.n(tag + "A_halo"), [128, GI, 3], F32)
    zrot = [p.sb(ctx.n(tag + f"A_zr{i}"), [128, Tc], BF16) for i in range(2)]
    xcb = p.sb(ctx.n(tag + "A_xcb"), [128, 8, Tc], BF16)
    cw3 = cw[:, :].rearrange("p (g k) -> p g k", k=4)
    conv_acc = ctx.tmp("scrA", [128, 4096])[:, :].rearrange(
        "p (g t) -> p g t", t=Tc)

    dataoff = 3 if not rev else 0
    haloslice = (slice(0, 3) if not rev else slice(Tc, Tc + 3))
    saveslice = (slice(Tc, Tc + 3) if not rev else slice(0, 3))
    order = range(nch - 1, -1, -1) if rev else range(nch)
    first = True
    for ci in order:
        t0 = ci * Tc
        for tt in range(ntt):
            p.dma("sync", xt[:, tt, :], x_dram[t0 + tt * 128:t0 + (tt + 1) * 128, :])
        ln_chunk(ctx, xt[:, 0:ntt, :], ntt, 1024, xt[:, 0:ntt, :])
        ln_chunk(ctx, xt[:, 0:ntt, :], ntt, 1024, xn2b[:, 0:ntt, :],
                 gam_row=gam, beta_row=bet)
        x2T = xn2T[ci % 2]
        ctx.transpose_tmaj_to_dmaj(x2T, xn2b, ntt, 8)
        if first:
            p.memset("vector", xi[:, :, haloslice], 0.0)
        else:
            p.cp("vector", xi[:, :, haloslice], halo3[:, :, :])
        for j in range(2 * GI):
            pm = ctx.pmm[j % 4]
            p.mm_acc(pm[:, 0:Tc],
                     [(inw[:, k, j * 128:(j + 1) * 128], x2T[:, k, 0:Tc])
                      for k in range(8)])
            if j < GI:
                p.cp("vector", xi[:, j, dataoff:dataoff + Tc], pm[:, 0:Tc])
            else:
                zr = zrot[j % 2]
                p.cp("vector", zr[:, 0:Tc], pm[:, 0:Tc])
                p.dma("scalar" if j % 2 else "sync",
                      zd[:, j - GI, t0:t0 + Tc], zr[:, 0:Tc])
        p.cp("vector", halo3[:, :, :], xi[:, :, saveslice])
        for gh in range(2):
            g0 = gh * 8
            acc = conv_acc
            for k in range(4):
                kk = k if not rev else 3 - k
                cwb = cw3[:, g0:g0 + 8, kk:kk + 1].broadcast_to([128, 8, Tc])
                if k == 0:
                    p.tt("vector", acc[:, :, :], xi[:, g0:g0 + 8, 0:Tc], cwb,
                         ALU.mult)
                else:
                    gt2 = xt[:, :, :].rearrange("p a b -> p (a b)").rearrange(
                        "p (g t) -> p g t", t=Tc)
                    p.tt("vector", gt2[:, :, 0:Tc], xi[:, g0:g0 + 8, k:k + Tc],
                         cwb, ALU.mult)
                    p.tt("vector", acc[:, :, :], acc[:, :, :], gt2[:, :, 0:Tc],
                         ALU.add)
            p.tt("vector", acc[:, :, :], acc[:, :, :],
                 cb[:, g0:g0 + 8].unsqueeze(2).broadcast_to([128, 8, Tc]), ALU.add)
            p.act("scalar", xcb[:, 0:8, 0:Tc], acc[:, :, :], AF.Silu)
            for g in range(8):
                p.dma("scalar" if g % 2 else "sync",
                      xcd[:, g0 + g, t0:t0 + Tc], xcb[:, g, 0:Tc])
        first = False


def emit_l1_phB(ctx, x_dram, W, rev, xcd, zd, tag, yf_dram, xo_dram):
    p = ctx.p
    T = ctx.T
    Tc = 256
    ntt = Tc // 128
    nch = T // Tc
    GI, R, N = 16, 64, 16
    xp = ctx.wload(W["xproj_lhsT"], 2048, R + 32, tag + "xpB")
    dtw = p.sb(ctx.n(tag + "dtwB"), [R, 2048], BF16)
    mo = 0
    while mo < 2048:
        mw = min(512, 2048 - mo)
        st = ctx.wstage[0]
        p.dma("sync", st[0:R, 0:mw], W["dtw_lhsT"][:, mo:mo + mw])
        p.cp("gpsimd", dtw[:, mo:mo + mw], st[0:R, 0:mw])
        mo += mw
    dtb = ctx.vload(W["dt_b"], GI, tag + "dtbB")
    Dp = ctx.vload(W["D"], GI, tag + "DpB")
    Alog = ctx.vload(W["A_log"], GI * 16, tag + "AlB")
    A = p.sb(ctx.n(tag + "AB"), [128, GI * 16], F32)
    p.act("scalar", A[:, :], Alog[:, :], AF.Exp)
    p.ts("vector", A[:, :], A[:, :], -1.0, None, ALU.mult)
    outwb_dram = ctx.to_bf16_dram(W["outw_lhsT"], 2048, 1024, tag + "_outwbB")
    upwb = [p.sb(ctx.n(tag + f"upwB{i}"), [128, 16, 128], BF16) for i in range(2)]
    scrB = ctx.tmp("scrB_bf16", [128, 4096], BF16)
    xcb = [p.sb(ctx.n(tag + f"B_xc{i}"), [128, GI, Tc], BF16) for i in range(2)]
    zb = [p.sb(ctx.n(tag + "B_z0"), [128, GI, Tc], BF16)] * 2
    dt = [p.sb(ctx.n(tag + "B_dt0"), [128, GI, Tc], F32)] * 2
    _dtx0 = ctx.tmp("scrA", [128, 4096])[:, 0:GI * Tc].rearrange(
        "p (a b) -> p a b", b=Tc)
    dtx = [_dtx0, _dtx0] if rev else [
        _dtx0, p.sb(ctx.n(tag + "B_dtx1"), [128, GI, Tc], F32)]
    y = p.sb(ctx.n(tag + "B_y"), [128, GI, Tc], F32)
    hst = p.sb(ctx.n(tag + "B_hst"), [128, GI, N], F32)
    sB = p.sb(ctx.n(tag + "B_sB"), [16, Tc], BF16)
    sC = p.sb(ctx.n(tag + "B_sC"), [16, Tc], BF16)
    dtrawb = p.sb(ctx.n(tag + "B_dtr"), [R, Tc], BF16)
    at = [p.sb(ctx.n(tag + f"B_at{i}"), [128, Tc], F32) for i in range(2)]
    h_all = p.sb(ctx.n(tag + "B_hall"), [128, 16, Tc + 1], F32)
    a_all = p.sb(ctx.n(tag + "B_aall"), [128, 16, Tc + 1], F32)
    p.memset("vector", a_all[:, :, 0:1], 0.0)
    p.memset("vector", a_all[:, :, Tc:Tc + 1], 0.0)
    B_ball = [p.sb(ctx.n(tag + f"B_Bb{i}"), [128, 16, Tc], BF16) for i in range(2)]
    C_ball = [p.sb(ctx.n(tag + f"B_Cb{i}"), [128, 16, Tc], BF16) for i in range(2)]
    yv = scrB[:, 0:GI * Tc].rearrange("p (a b) -> p a b", b=Tc)
    yfT = p.sb(ctx.n(tag + "B_yfT"), [128, 8, Tc], BF16)
    if rev:
        ybT = p.sb(ctx.n(tag + "B_ybT"), [128, 8, Tc], BF16)
        mbr = ctx.row_tile(W["merge_b_row"], 1024, tag + "mbB", dt=BF16)
        mergeb_dram = ctx.to_bf16_dram(W["merge_rhs"], 2048, 1024,
                                       tag + "_mergebB")
        mgb = [p.sb(ctx.n(tag + f"mgbB{i}"), [128, 512], BF16) for i in range(2)]
        xt = p.sb(ctx.n(tag + "B_xt"), [128, ntt, 1024], F32)
        xn_sc = scrB[:, 0:ntt * 1024].rearrange("p (a b) -> p a b", b=1024)

    p.memset("vector", hst[:, :, :], 0.0)
    order = range(nch - 1, -1, -1) if rev else range(nch)
    for ci in order:
        t0 = ci * Tc
        xc_c = xcb[ci % 2]
        z_c = zb[ci % 2]
        dt_c = dt[ci % 2]
        for g in range(GI):
            p.dma("scalar" if g % 2 else "sync", xc_c[:, g, 0:Tc],
                  xcd[:, g, t0:t0 + Tc])
            p.dma("scalar" if g % 2 else "sync", z_c[:, g, 0:Tc],
                  zd[:, g, t0:t0 + Tc])
        pDt, pB, pC = ctx.pmm[2], ctx.pmm[3], ctx.pmm[0]
        p.mm_acc(pDt[0:R, 0:Tc], [(xp[:, k, 0:R], xc_c[:, k, 0:Tc])
                                  for k in range(GI)])
        p.mm_acc(pB[0:16, 0:Tc], [(xp[:, k, R:R + 16], xc_c[:, k, 0:Tc])
                                  for k in range(GI)])
        p.mm_acc(pC[0:16, 0:Tc], [(xp[:, k, R + 16:R + 32], xc_c[:, k, 0:Tc])
                                  for k in range(GI)])
        p.cp("vector", sB[:, :], pB[0:16, 0:Tc])
        p.cp("vector", sC[:, :], pC[0:16, 0:Tc])
        p.cp("vector", dtrawb[:, :], pDt[0:R, 0:Tc])
        for m in range(GI):
            pm = ctx.pmm[1]
            p.mm(pm[:, 0:Tc], dtw[:, m * 128:(m + 1) * 128], dtrawb[:, 0:Tc],
                 True, True)
            dtb_col = dtb[:, m:m + 1]
            e1 = ctx.tmp("mb_e1", [128, 256])
            p.act("scalar", e1[:, 0:Tc], pm[:, 0:Tc], AF.Abs, bias=dtb_col)
            p.act("scalar", e1[:, 0:Tc], e1[:, 0:Tc], AF.Exp, scale=-1.0)
            p.act("scalar", e1[:, 0:Tc], e1[:, 0:Tc], AF.Ln,
                  bias=ctx.const_col(1.0))
            p.act("scalar", dt_c[:, m, 0:Tc], pm[:, 0:Tc], AF.Relu, bias=dtb_col)
            p.tt("vector", dt_c[:, m, 0:Tc], dt_c[:, m, 0:Tc], e1[:, 0:Tc],
                 ALU.add)
        dtx_c = dtx[ci % 2]
        Bb_c = B_ball[ci % 2]
        Cb_c = C_ball[ci % 2]
        p.tt("vector", dtx_c[:, :, 0:Tc], dt_c[:, :, 0:Tc], xc_c[:, :, 0:Tc],
             ALU.mult)
        hk = _tn(h_all)
        hallk = [f"{hk}@{n}" for n in range(N)]
        for n in range(N):
            pBb = ctx.pbc[n % 2]
            pCb = ctx.pmm[2 + n % 2]
            seln = ctx.seln[:, n * 128:(n + 1) * 128]
            p.mm(pBb[:, 0:Tc], seln, sB[0:16, 0:Tc], True, True)
            p.mm(pCb[:, 0:Tc], seln, sC[0:16, 0:Tc], True, True)
            p.cp("vector", Bb_c[:, n, 0:Tc], pBb[:, 0:Tc])
            p.cp("vector", Cb_c[:, n, 0:Tc], pCb[:, 0:Tc])
        for g in range(GI):
            emit_scan_group(ctx, h_all, a_all, at, dtx_c, Bb_c, Cb_c, dt_c, A,
                            hst, y, g, GI, N, Tc, rev)
        gt = dtx_c
        p.tt("vector", gt[:, :, 0:Tc], xc_c[:, :, 0:Tc],
             Dp[:, :].unsqueeze(2).broadcast_to([128, GI, Tc]), ALU.mult)
        p.tt("vector", y[:, :, 0:Tc], y[:, :, 0:Tc], gt[:, :, 0:Tc], ALU.add)
        zs = at[0]
        for g in range(GI):
            p.act("scalar", zs[:, 0:Tc], z_c[:, g, 0:Tc], AF.Silu)
            p.tt("vector", y[:, g, 0:Tc], y[:, g, 0:Tc], zs[:, 0:Tc], ALU.mult)
        p.cp("vector", yv[:, :, :], y[:, :, 0:Tc])
        for m in range(8):
            pm = ctx.pmm[m % 2]
            _stream_upproj(ctx, outwb_dram, GI, m, yv, GI, Tc, pm, upwb)
            if not rev:
                p.cp("vector", yfT[:, m, :], pm[:, 0:Tc])
                p.dma("sync", yf_dram[:, m, t0:t0 + Tc], yfT[:, m, :],
                      out_is_output=True)
            else:
                p.cp("vector", ybT[:, m, :], pm[:, 0:Tc])
        if rev:
            for tt in range(ntt):
                p.dma("sync", xt[:, tt, :],
                      x_dram[t0 + tt * 128:t0 + (tt + 1) * 128, :])
            ln_chunk(ctx, xt[:, 0:ntt, :], ntt, 1024, xn_sc[:, 0:ntt, :])
            for m in range(8):
                p.dma("gpsimd", yfT[:, m, :], yf_dram[:, m, t0:t0 + Tc])
            for mt in range(ntt):
                for no in range(0, 1024, 512):
                    pm = ctx.pmm[2 + (no // 512) % 2]
                    for k in range(16):
                        mg = mgb[k % 2]
                        p.dma("sync", mg[:, :],
                              mergeb_dram[k * 128:(k + 1) * 128, no:no + 512])
                        src = yfT if k < 8 else ybT
                        p.mm(pm[:, 0:512],
                             src[:, k % 8, mt * 128:(mt + 1) * 128], mg[:, :],
                             k == 0, k == 15)
                    p.tt("vector", xt[:, mt, no:no + 512], xt[:, mt, no:no + 512],
                         pm[:, 0:512], ALU.add)
                    p.tt("vector", xt[:, mt, no:no + 512], xt[:, mt, no:no + 512],
                         mbr[:, no:no + 512], ALU.add)
                    p.tt("vector", xt[:, mt, no:no + 512], xt[:, mt, no:no + 512],
                         xn_sc[:, mt, no:no + 512], ALU.add)
            for tt in range(ntt):
                p.dma("sync", xo_dram[t0 + tt * 128:t0 + (tt + 1) * 128, :],
                      xt[:, tt, :], out_is_output=True)


def emit_l1_fwd(ctx, x_dram, yf_dram, W):
    p = ctx.p
    T, Tc = ctx.T, ctx.Tc
    ntt = Tc // 128
    nch = T // Tc
    GI, R = 16, 64
    cfg = MambaCfg("l1f", 1024, 2048, 64, False, W, stream_inw=True)
    inwb_dram = ctx.to_bf16_dram(W["inw_lhsT"], 1024, 4096, "l1fw_inwb")
    outwb_dram = ctx.to_bf16_dram(W["outw_lhsT"], 2048, 1024, "l1fw_outwb")
    h = mamba_dir_setup(ctx, cfg)
    gam = ctx.row_tile(W["gam_row"], 1024, "l1g", dt=BF16)
    bet = ctx.row_tile(W["beta_row"], 1024, "l1b", dt=BF16)
    scrB = ctx.tmp("scrB_bf16", [128, 4096], BF16)
    xt = p.sb(ctx.n("l1_xt"), [128, ntt, 1024], F32)
    xn_sc = p.sb(ctx.n("l1_xn"), [128, ntt, 1024], F32)
    xn2b = scrB[:, 0:ntt * 1024].rearrange("p (a b) -> p a b", b=1024)
    xn2T = p.sb(ctx.n("l1_xn2T"), [128, 8, Tc], BF16)
    buf = mamba_bufs(ctx, GI, Tc, R, scrB)
    buf["wjk"] = [p.sb(ctx.n(f"l1wjk{i}"), [128, 8, 128], BF16) for i in range(2)]
    buf["inwb_dram"] = inwb_dram
    upwb = [p.sb(ctx.n("upw0"), [128, 16, 128], BF16)] * 2
    yv = scrB[:, 0:GI * Tc].rearrange("p (a b) -> p a b", b=Tc)
    yfT = p.sb(ctx.n("l1_yfT"), [128, 8, Tc], BF16)
    kgroups = list(range(8))
    for ci in range(nch):
        t0 = ci * Tc
        for tt in range(ntt):
            r0 = t0 + tt * 128
            p.dma("sync", xt[:, tt, :], x_dram[r0:r0 + 128, :])
        ln_chunk(ctx, xt[:, 0:ntt, :], ntt, 1024, xn_sc[:, 0:ntt, :])
        ln_chunk(ctx, xn_sc[:, 0:ntt, :], ntt, 1024, xn2b[:, 0:ntt, :],
                 gam_row=gam, beta_row=bet)
        ctx.transpose_tmaj_to_dmaj(xn2T, xn2b, ntt, 8)
        mamba_dir_chunk(ctx, cfg, h, buf, xn2T, kgroups, ci, nch, yv)
        for m in range(8):
            pm = ctx.pmm[m % 2]
            _stream_upproj(ctx, outwb_dram, GI, m, yv, GI, Tc, pm, upwb)
            p.cp("vector", yfT[:, m, :], pm[:, 0:Tc])
            p.dma("sync", yf_dram[:, m, t0:t0 + Tc], yfT[:, m, :],
                  out_is_output=True)


def emit_l1_bwd(ctx, x_dram, yf_dram, xo_dram, W):
    p = ctx.p
    T, Tc = ctx.T, ctx.Tc
    ntt = Tc // 128
    nch = T // Tc
    GI, R = 16, 64
    cfg = MambaCfg("l1b", 1024, 2048, 64, True, W, stream_inw=True)
    inwb_dram = ctx.to_bf16_dram(W["inw_lhsT"], 1024, 4096, "l1bw_inwb")
    outwb_dram = ctx.to_bf16_dram(W["outw_lhsT"], 2048, 1024, "l1bw_outwb")
    h = mamba_dir_setup(ctx, cfg)
    gam = ctx.row_tile(W["gam_row"], 1024, "l1g", dt=BF16)
    bet = ctx.row_tile(W["beta_row"], 1024, "l1b", dt=BF16)
    mbr = ctx.row_tile(W["merge_b_row"], 1024, "l1mb")
    mergeb_dram = ctx.to_bf16_dram(W["merge_rhs"], 2048, 1024, "l1_mergeb")
    mgb = [p.sb(ctx.n(f"l1mgb{i}"), [128, 512], BF16) for i in range(2)]
    scrB = ctx.tmp("scrB_bf16", [128, 4096], BF16)
    xt = p.sb(ctx.n("l1_xt"), [128, ntt, 1024], F32)
    xn_sc = p.sb(ctx.n("l1_xn"), [128, ntt, 1024], BF16)
    xn2b = scrB[:, 0:ntt * 1024].rearrange("p (a b) -> p a b", b=1024)
    xn2T = p.sb(ctx.n("l1_xn2T"), [128, 8, Tc], BF16)
    buf = mamba_bufs(ctx, GI, Tc, R, scrB)
    buf["wjk"] = [p.sb(ctx.n(f"l1wjk{i}"), [128, 8, 128], BF16) for i in range(2)]
    buf["inwb_dram"] = inwb_dram
    upwb = [p.sb(ctx.n("upw0"), [128, 16, 128], BF16)] * 2
    yv = scrB[:, 0:GI * Tc].rearrange("p (a b) -> p a b", b=Tc)
    yfT = p.sb(ctx.n("l1_yfT"), [128, 8, Tc], BF16)
    ybT = p.sb(ctx.n("l1_ybT"), [128, 8, Tc], BF16)
    ot = xt
    kgroups = list(range(8))
    for ci in range(nch - 1, -1, -1):
        t0 = ci * Tc
        for tt in range(ntt):
            r0 = t0 + tt * 128
            p.dma("sync", xt[:, tt, :], x_dram[r0:r0 + 128, :])
        ln_chunk(ctx, xt[:, 0:ntt, :], ntt, 1024, xn_sc[:, 0:ntt, :])
        ln_chunk(ctx, xn_sc[:, 0:ntt, :], ntt, 1024, xn2b[:, 0:ntt, :],
                 gam_row=gam, beta_row=bet)
        ctx.transpose_tmaj_to_dmaj(xn2T, xn2b, ntt, 8)
        mamba_dir_chunk(ctx, cfg, h, buf, xn2T, kgroups, ci, nch, yv)
        for m in range(8):
            pm = ctx.pmm[m % 2]
            _stream_upproj(ctx, outwb_dram, GI, m, yv, GI, Tc, pm, upwb)
            p.cp("vector", ybT[:, m, :], pm[:, 0:Tc])
            p.dma("sync", yfT[:, m, :], yf_dram[:, m, t0:t0 + Tc])
        for mt in range(ntt):
            for no in range(0, 1024, 512):
                pm = ctx.pmm[2 + (no // 512) % 2]
                for k in range(16):
                    mg = mgb[k % 2]
                    p.dma("sync", mg[:, :],
                          mergeb_dram[k * 128:(k + 1) * 128, no:no + 512])
                    src = yfT if k < 8 else ybT
                    p.mm(pm[:, 0:512],
                         src[:, k % 8, mt * 128:(mt + 1) * 128], mg[:, :],
                         k == 0, k == 15)
                p.tt("vector", ot[:, mt, no:no + 512], pm[:, 0:512],
                     xt[:, mt, no:no + 512], ALU.add)
                p.tt("vector", ot[:, mt, no:no + 512], ot[:, mt, no:no + 512],
                     mbr[:, no:no + 512], ALU.add)
                p.tt("vector", ot[:, mt, no:no + 512], ot[:, mt, no:no + 512],
                     xn_sc[:, mt, no:no + 512], ALU.add)
        for tt in range(ntt):
            r0 = t0 + tt * 128
            p.dma("sync", xo_dram[r0:r0 + 128, :], ot[:, tt, :], out_is_output=True)



def router_chunk(ctx, xnT, Wr_sb, E, ntt, brow=None):
    p = ctx.p
    Tc = ctx.Tc
    logits = ctx.tmp("rt_logits", [128, 4, 16])
    probs = ctx.tmp("rt_probs", [128, 4, 16])
    nmax = ctx.tmp("rt_nmax", [128, 1])
    esum = ctx.tmp("rt_esum", [128, 1])
    for tt in range(ntt):
        pm = ctx.pbc[0]
        p.mm_acc(pm[:, 0:E], [(xnT[:, k, tt * 128:(tt + 1) * 128],
                               Wr_sb[:, k, 0:E]) for k in range(8)])
        if brow is None:
            p.cp("vector", logits[:, tt, 0:E], pm[:, 0:E])
        else:
            p.tt("vector", logits[:, tt, 0:E], pm[:, 0:E], brow[:, 0:E], ALU.add)
        p.red("vector", nmax[:, :], logits[:, tt, 0:E], ALU.max, negate=True)
        p.act("scalar", probs[:, tt, 0:E], logits[:, tt, 0:E], AF.Exp,
              bias=nmax[:, :], accum_out=esum[:, :])
        p.recip("vector", esum[:, :], esum[:, :])
        p.ts("vector", probs[:, tt, 0:E], probs[:, tt, 0:E], esum[:, 0:1], None,
             ALU.mult)
    return probs, logits


def topk_masks(ctx, vals_ap, E, k, out_oh, out_m, tt):
    p = ctx.p
    m8 = ctx.tmp("tk_m8", [128, 8])
    mr1 = ctx.tmp("tk_mr1", [128, 16])
    mr2 = ctx.tmp("tk_mr2", [128, 16])
    p.red("vector", out_m[:, tt, 0:1], vals_ap, ALU.max)
    p.memset("vector", m8[:, :], -1e38)
    p.cp("vector", m8[:, 0:1], out_m[:, tt, 0:1])
    p.mr("vector", mr1[:, 0:E], m8[:, :], vals_ap, -1e30)
    p.tt("vector", out_oh[:, tt, 0:E], vals_ap, mr1[:, 0:E], ALU.not_equal)
    if k == 2:
        p.red("vector", out_m[:, tt, 1:2], mr1[:, 0:E], ALU.max)
        p.memset("vector", m8[:, :], -1e38)
        p.cp("vector", m8[:, 0:1], out_m[:, tt, 1:2])
        p.mr("vector", mr2[:, 0:E], m8[:, :], mr1[:, 0:E], -1e30)
        p.tt("vector", out_oh[:, tt, E:2 * E], mr1[:, 0:E], mr2[:, 0:E],
             ALU.not_equal)


def sel_to_row(ctx, sel_col_ap, selrow, tt):
    ctx.p.dma("sync", selrow[0:1, tt * 128:(tt + 1) * 128], sel_col_ap)


def emit_l3_ffn(ctx, x_dram, xo_dram, W):
    p = ctx.p
    T, Tc = ctx.T, ctx.Tc
    ntt = Tc // 128
    nch = T // Tc
    NTT = T // 128
    E = 8
    wr = ctx.wload(W["router_wT"], 1024, E, "l3r")
    brow = ctx.row_tile(W["router_b_row"], E, "l3rb")
    scrB = ctx.tmp("scrB_bf16", [128, 4096], BF16)
    xt = p.sb(ctx.n("f3_xt"), [128, ntt, 1024], F32)
    xnb = scrB[:, 0:ntt * 1024].rearrange("p (a b) -> p a b", b=1024)
    xnTf = p.sb(ctx.n("f3_xnTf"), [128, 8, T], BF16)
    comb = p.sb(ctx.n("f3_comb"), [128, NTT, E], F32)
    oh = ctx.tmp("tk_oh", [128, 4, 32])
    mx = ctx.tmp("tk_mx", [128, 4, 2])
    ot = [p.sb(ctx.n(f"f3_ot{i}"), [128, ntt, 1024], F32) for i in range(2)]
    we = [p.sb(ctx.n(f"l3we{i}"), [128, 8, 1024], BF16) for i in range(2)]
    ber = [p.sb(ctx.n(f"l3ber{i}"), [128, 1024], F32) for i in range(2)]
    rowbuf = ctx.rowbuf
    for ci in range(nch):
        t0 = ci * Tc
        p.dma("gpsimd", xo_dram[t0:t0 + Tc, :], x_dram[t0:t0 + Tc, :],
              wkey=f"f3xo@0_{ci}")
        for tt in range(ntt):
            p.dma("sync", xt[:, tt, :], x_dram[t0 + tt * 128:t0 + (tt + 1) * 128, :])
        ln_chunk(ctx, xt[:, 0:ntt, :], ntt, 1024, xnb[:, 0:ntt, :])
        ctx.transpose_tmaj_to_dmaj(xnTf, xnb, ntt, 8, dst_t0=t0)
        probs, logits = router_chunk(ctx, xnTf[:, :, t0:t0 + Tc], wr, E, ntt,
                                     brow=brow)
        for tt in range(ntt):
            topk_masks(ctx, logits[:, tt, 0:E], E, 1, oh, mx, tt)
            p.red("vector", mx[:, tt, 1:2], probs[:, tt, 0:E], ALU.max)
            p.ts("vector", comb[:, ci * ntt + tt, 0:E], oh[:, tt, 0:E],
                 mx[:, tt, 1:2], None, ALU.mult)
    ctx.wload_into(we[0], W["expert_wT"][0:1024, :], 1024, 1024, eng="vector")
    ctx.row_tile_into(ber[0], W["expert_b"][0:1, :], 1024, rowbuf)
    for e in range(E):
        if e + 1 < E:
            ctx.wload_into(we[(e + 1) % 2],
                           W["expert_wT"][(e + 1) * 1024:(e + 2) * 1024, :],
                           1024, 1024, eng="vector")
            ctx.row_tile_into(ber[(e + 1) % 2], W["expert_b"][e + 1:e + 2, :],
                              1024, rowbuf)
        for ci in range(nch):
            t0 = ci * Tc
            o = ot[ci % 2]
            for tt in range(ntt):
                p.dma("scalar", o[:, tt, :],
                      xo_dram[t0 + tt * 128:t0 + (tt + 1) * 128, :],
                      rkey=f"f3xo@{e}_{ci}")
            for mt in range(ntt):
                for no in range(0, 1024, 512):
                    pm = ctx.pmm[(mt * 2 + no // 512) % 4]
                    p.mm_acc(pm[:, 0:512],
                             [(xnTf[:, k, t0 + mt * 128:t0 + (mt + 1) * 128],
                               we[e % 2][:, k, no:no + 512]) for k in range(8)])
                    tmp = ctx.tmp("ff_tmp", [128, 512])
                    p.tt("vector", tmp[:, :], pm[:, 0:512],
                         ber[e % 2][:, no:no + 512], ALU.add)
                    p.stt("vector", o[:, mt, no:no + 512], tmp[:, :],
                          comb[:, ci * ntt + mt, e:e + 1],
                          o[:, mt, no:no + 512], ALU.mult, ALU.add)
            for tt in range(ntt):
                p.dma("sync", xo_dram[t0 + tt * 128:t0 + (tt + 1) * 128, :],
                      o[:, tt, :], out_is_output=(e == E - 1),
                      wkey=f"f3xo@{e + 1}_{ci}")


def emit_l3_ffn_init(ctx, x_dram, xo_dram):
    p = ctx.p
    T = ctx.T
    st = min(512, T)
    for r0 in range(0, T, st):
        p.dma("gpsimd", xo_dram[r0:r0 + st, :], x_dram[r0:r0 + st, :])


def emit_l0_ffn(ctx, x_dram, xo_dram, W):
    p = ctx.p
    T, Tc = ctx.T, ctx.Tc
    ntt = Tc // 128
    nch = T // Tc
    NTT = T // 128
    gw = ctx.wload(W["gate_wT"], 1024, 2, "f0gw")
    scrB = ctx.tmp("scrB_bf16", [128, 4096], BF16)
    xt = p.sb(ctx.n("f0_xt"), [128, ntt, 1024], F32)
    xnb = scrB[:, 0:ntt * 1024].rearrange("p (a b) -> p a b", b=1024)
    xnTf = p.sb(ctx.n("f0_xnTf"), [128, 8, T], BF16)
    xselT = scrB[:, 0:8 * Tc].rearrange("p (a b) -> p a b", b=Tc)
    seltn = p.sb(ctx.n("f0_sel"), [128, NTT, 2], F32)
    selrow = p.sb(ctx.n("f0_selrow"), [1, Tc], F32)
    hb = p.sb(ctx.n("f0_hb"), [128, 16, Tc], BF16)
    ot = [p.sb(ctx.n("f0_ot0"), [128, ntt, 1024], F32)] * 2
    w1 = [p.sb(ctx.n(f"f0_w1{i}"), [128, 8, 2048], BF16) for i in range(2)]
    w2 = p.sb(ctx.n("f0_w2"), [128, 16, 1024], BF16)
    b1 = [p.sb(ctx.n(f"f0_b1{i}"), [128, 16], F32) for i in range(2)]
    b2r = [p.sb(ctx.n("f0_b2r0"), [128, 1024], F32)] * 2
    rowbuf = ctx.rowbuf

    for ci in range(nch):
        t0 = ci * Tc
        p.dma("gpsimd", xo_dram[t0:t0 + Tc, :], x_dram[t0:t0 + Tc, :],
              wkey=f"f0xo@0_{ci}")
    for ci in range(nch):
        t0 = ci * Tc
        for tt in range(ntt):
            p.dma("sync", xt[:, tt, :], x_dram[t0 + tt * 128:t0 + (tt + 1) * 128, :])
        ln_chunk(ctx, xt[:, 0:ntt, :], ntt, 1024, xnb[:, 0:ntt, :])
        ctx.transpose_tmaj_to_dmaj(xnTf, xnb, ntt, 8, dst_t0=t0)
        probs, _ = router_chunk(ctx, xnTf[:, :, t0:t0 + Tc], gw, 2, ntt)
        for tt in range(ntt):
            r = ci * ntt + tt
            p.tt("vector", seltn[:, r, 0:1], probs[:, tt, 0:1], probs[:, tt, 1:2],
                 ALU.is_ge)
            p.ts("vector", seltn[:, r, 1:2], seltn[:, r, 0:1], -1.0, 1.0,
                 ALU.mult, ALU.add)
    phases = []
    for ei, (pw1, pw2, pb1, pb2) in enumerate(
            [(W["w1_e0"], W["w2_e0"], W["b1_e0"], W["b2_e0"]),
             (W["w1_e1"], W["w2_e1"], W["b1_e1"], W["b2_e1"]),
             (W["sw1"], W["sw2"], W["sb1"], W["sb2"])]):
        for hh in range(2):
            phases.append((ei, hh, pw1, pw2, pb1, pb2))
    w1p0 = phases[0]
    ctx.wload_into(w1[0], w1p0[2][:, 0:2048], 1024, 2048)
    for pi, (ei, hh, pw1, pw2, pb1, pb2) in enumerate(phases):
        ctx.wload_into(w2, pw2[hh * 2048:(hh + 1) * 2048, :], 2048, 1024,
                       eng="vector")
        p.dma("sync", b1[pi % 2][:, :], pb1[:, hh * 16:(hh + 1) * 16])
        if hh == 0:
            ctx.row_tile_into(b2r[ei % 2], pb2, 1024, rowbuf)
        if pi + 1 < len(phases):
            nei, nhh, npw1 = phases[pi + 1][0], phases[pi + 1][1], phases[pi + 1][2]
            ctx.wload_into(w1[(pi + 1) % 2],
                           npw1[:, nhh * 2048:(nhh + 1) * 2048], 1024, 2048,
                           eng="vector")
        for ci in range(nch):
            t0 = ci * Tc
            if ei < 2:
                for tt in range(ntt):
                    sel_to_row(ctx, seltn[:, ci * ntt + tt, ei:ei + 1],
                               selrow, tt)
                pb = ctx.pbc[1]
                p.mm(pb[:, 0:Tc], ctx.ones[0:1, :], selrow[0:1, 0:Tc], True, True)
                for g in range(8):
                    p.tt("vector", xselT[:, g, :], xnTf[:, g, t0:t0 + Tc],
                         pb[:, 0:Tc], ALU.mult)
                src = xselT
                soff = 0
            else:
                src = xnTf
                soff = t0
            for j in range(16):
                pm = ctx.pmm[j % 2]
                p.mm_acc(pm[:, 0:Tc], [(w1[pi % 2][:, k, j * 128:(j + 1) * 128],
                                        src[:, k, soff:soff + Tc]) for k in range(8)])
                p.act("scalar", hb[:, j, :], pm[:, 0:Tc], AF.Relu,
                      bias=b1[pi % 2][:, j:j + 1])
            o = ot[ci % 2]
            for tt in range(ntt):
                p.dma("scalar", o[:, tt, :],
                      xo_dram[t0 + tt * 128:t0 + (tt + 1) * 128, :],
                      rkey=f"f0xo@{pi}_{ci}")
            for mt in range(ntt):
                for no in range(0, 1024, 512):
                    pm = ctx.pmm[2 + (no // 512) % 2]
                    p.mm_acc(pm[:, 0:512],
                             [(hb[:, k, mt * 128:(mt + 1) * 128],
                               w2[:, k, no:no + 512]) for k in range(16)])
                    if hh == 0:
                        tmp = ctx.tmp("ff_tmp", [128, 512])
                        p.tt("vector", tmp[:, :], pm[:, 0:512],
                             b2r[ei % 2][:, no:no + 512], ALU.add)
                        p.tt("vector", o[:, mt, no:no + 512],
                             o[:, mt, no:no + 512], tmp[:, :], ALU.add)
                    else:
                        p.tt("vector", o[:, mt, no:no + 512],
                             o[:, mt, no:no + 512], pm[:, 0:512], ALU.add)
            last = (pi == len(phases) - 1)
            for tt in range(ntt):
                p.dma("sync", xo_dram[t0 + tt * 128:t0 + (tt + 1) * 128, :],
                      o[:, tt, :], out_is_output=last,
                      wkey=f"f0xo@{pi + 1}_{ci}")


def emit_l1_ffn(ctx, x_dram, xo_dram, W):
    p = ctx.p
    T, Tc = ctx.T, ctx.Tc
    ntt = Tc // 128
    nch = T // Tc
    NTT = T // 128
    E = 16
    wr = ctx.wload(W["Wr"], 1024, E, "f1wr")
    brow = ctx.row_tile(W["br_row"], E, "f1br")
    scrB = ctx.tmp("scrB_bf16", [128, 4096], BF16)
    xt = p.sb(ctx.n("f1_xt"), [128, ntt, 1024], F32)
    xnb = scrB[:, 0:ntt * 1024].rearrange("p (a b) -> p a b", b=1024)
    xnTf = p.sb(ctx.n("f1_xnTf"), [128, 8, T], BF16)
    comb = p.sb(ctx.n("f1_comb"), [128, NTT, E], F32)
    oh = ctx.tmp("tk_oh", [128, 4, 32])
    mx = ctx.tmp("tk_mx", [128, 4, 2])
    hb = ctx.tmp("scrA", [128, 4096])[:, :].bitcast(BF16)[:, 0:16 * Tc].rearrange(
        "p (a b) -> p a b", b=Tc)
    ot = [p.sb(ctx.n(f"f1_ot{i}"), [128, ntt, 1024], F32) for i in range(2)]
    xn_sc = ot[0]
    w1 = [p.sb(ctx.n(f"f1_w1{i}"), [128, 8, 2048], BF16) for i in range(2)]
    w2 = p.sb(ctx.n("f1_w2"), [128, 16, 1024], BF16)
    b1 = [p.sb(ctx.n(f"f1_b1{i}"), [128, 16], F32) for i in range(2)]
    b2r = [p.sb(ctx.n("f1_b2r0"), [128, 1024], F32)] * 2
    rowbuf = ctx.rowbuf

    for ci in range(nch):
        t0 = ci * Tc
        for tt in range(ntt):
            p.dma("sync", xt[:, tt, :], x_dram[t0 + tt * 128:t0 + (tt + 1) * 128, :])
        ln_chunk(ctx, xt[:, 0:ntt, :], ntt, 1024, xn_sc[:, 0:ntt, :])
        p.cp("vector", xnb[:, 0:ntt, :], xn_sc[:, 0:ntt, :])
        ctx.transpose_tmaj_to_dmaj(xnTf, xnb, ntt, 8, dst_t0=t0)
        probs, _ = router_chunk(ctx, xnTf[:, :, t0:t0 + Tc], wr, E, ntt, brow=brow)
        for tt in range(ntt):
            r = ci * ntt + tt
            topk_masks(ctx, probs[:, tt, 0:E], E, 2, oh, mx, tt)
            s12 = ctx.tmp("f1_s12", [128, 1])
            p.tt("vector", s12[:, :], mx[:, tt, 0:1], mx[:, tt, 1:2], ALU.add)
            p.ts("vector", s12[:, :], s12[:, :], 1e-9, None, ALU.max)
            p.recip("vector", s12[:, :], s12[:, :])
            c1 = ctx.tmp("f1_c1", [128, 1])
            c2 = ctx.tmp("f1_c2", [128, 1])
            p.tt("vector", c1[:, :], mx[:, tt, 0:1], s12[:, :], ALU.mult)
            p.tt("vector", c2[:, :], mx[:, tt, 1:2], s12[:, :], ALU.mult)
            p.ts("vector", comb[:, r, 0:E], oh[:, tt, 0:E], c1[:, 0:1], None,
                 ALU.mult)
            p.stt("vector", comb[:, r, 0:E], oh[:, tt, E:2 * E], c2[:, 0:1],
                  comb[:, r, 0:E], ALU.mult, ALU.add)
        p.tt("vector", xt[:, 0:ntt, :], xt[:, 0:ntt, :], xn_sc[:, 0:ntt, :],
             ALU.add)
        for tt in range(ntt):
            p.dma("sync", xo_dram[t0 + tt * 128:t0 + (tt + 1) * 128, :],
                  xt[:, tt, :], wkey=f"f1xo@0_{ci}")
    ctx.wload_into(w1[0], W["W1"][0:1024, :], 1024, 2048)
    for e in range(E):
        ctx.wload_into(w2, W["W2"][e * 2048:(e + 1) * 2048, :], 2048, 1024,
                       eng="vector")
        p.dma("sync", b1[e % 2][:, :], W["b1"][e * 128:(e + 1) * 128, :])
        ctx.row_tile_into(b2r[e % 2], W["b2"][e:e + 1, :], 1024, rowbuf)
        if e + 1 < E:
            ctx.wload_into(w1[(e + 1) % 2],
                           W["W1"][(e + 1) * 1024:(e + 2) * 1024, :], 1024, 2048,
                           eng="vector")
        for ci in range(nch):
            t0 = ci * Tc
            for j in range(16):
                pm = ctx.pmm[j % 2]
                p.mm_acc(pm[:, 0:Tc], [(w1[e % 2][:, k, j * 128:(j + 1) * 128],
                                        xnTf[:, k, t0:t0 + Tc]) for k in range(8)])
                p.act("scalar", hb[:, j, :], pm[:, 0:Tc], AF.Gelu,
                      bias=b1[e % 2][:, j:j + 1])
            o = ot[ci % 2]
            for tt in range(ntt):
                p.dma("scalar", o[:, tt, :],
                      xo_dram[t0 + tt * 128:t0 + (tt + 1) * 128, :],
                      rkey=f"f1xo@{e}_{ci}")
            for mt in range(ntt):
                for no in range(0, 1024, 512):
                    pm = ctx.pmm[2 + (no // 512) % 2]
                    p.mm_acc(pm[:, 0:512],
                             [(hb[:, k, mt * 128:(mt + 1) * 128],
                               w2[:, k, no:no + 512]) for k in range(16)])
                    tmp = ctx.tmp("ff_tmp", [128, 512])
                    p.tt("vector", tmp[:, :], pm[:, 0:512],
                         b2r[e % 2][:, no:no + 512], ALU.add)
                    p.stt("vector", o[:, mt, no:no + 512], tmp[:, :],
                          comb[:, ci * ntt + mt, e:e + 1],
                          o[:, mt, no:no + 512], ALU.mult, ALU.add)
            last = (e == E - 1)
            for tt in range(ntt):
                p.dma("sync", xo_dram[t0 + tt * 128:t0 + (tt + 1) * 128, :],
                      o[:, tt, :], out_is_output=last, wkey=f"f1xo@{e + 1}_{ci}")


def emit_l2_ffn_a(ctx, x_dram, eo1_dram, g_dram, W):
    p = ctx.p
    T, Tc = ctx.T, ctx.Tc
    ntt = Tc // 128
    nch = T // Tc
    NTT = T // 128
    E = 8
    gw = ctx.wload(W["gate_wT"], 1024, E, "f2gw")
    brow = ctx.row_tile(W["gate_b_row"], E, "f2gb")
    scrB = ctx.tmp("scrB_bf16", [128, 4096], BF16)
    xt = p.sb(ctx.n("f2_xt"), [128, ntt, 1024], F32)
    xnb = scrB[:, 0:ntt * 1024].rearrange("p (a b) -> p a b", b=1024)
    xnTf = p.sb(ctx.n("f2_xnTf"), [128, 8, T], BF16)
    gtile = p.sb(ctx.n("f2_g"), [128, NTT, 18], F32)
    oh = ctx.tmp("tk_oh", [128, 4, 32])
    mx = ctx.tmp("tk_mx", [128, 4, 2])
    hb = ctx.tmp("scrA", [128, 4096])[:, :].bitcast(BF16)[:, 0:16 * Tc].rearrange(
        "p (a b) -> p a b", b=Tc)
    ot = [p.sb(ctx.n(f"f2_ot{i}"), [128, ntt, 1024], F32) for i in range(2)]
    w1 = [p.sb(ctx.n("f2_w10"), [128, 8, 2048], BF16)] * 2
    w2 = p.sb(ctx.n("f2_w2"), [128, 16, 1024], BF16)
    b1 = [p.sb(ctx.n(f"f2_b1{i}"), [128, 16], F32) for i in range(2)]
    b2r = [p.sb(ctx.n("f2_b2r0"), [128, 1024], F32)] * 2
    rowbuf = ctx.rowbuf

    for ci in range(nch):
        t0 = ci * Tc
        for tt in range(ntt):
            p.dma("sync", xt[:, tt, :], x_dram[t0 + tt * 128:t0 + (tt + 1) * 128, :])
        ln_chunk(ctx, xt[:, 0:ntt, :], ntt, 1024, xnb[:, 0:ntt, :])
        ctx.transpose_tmaj_to_dmaj(xnTf, xnb, ntt, 8, dst_t0=t0)
        probs, _ = router_chunk(ctx, xnTf[:, :, t0:t0 + Tc], gw, E, ntt, brow=brow)
        for tt in range(ntt):
            r = ci * ntt + tt
            topk_masks(ctx, probs[:, tt, 0:E], E, 2, oh, mx, tt)
            p.cp("vector", gtile[:, r, 0:2], mx[:, tt, 0:2])
            p.cp("vector", gtile[:, r, 2:10], oh[:, tt, 0:E])
            p.cp("vector", gtile[:, r, 10:18], oh[:, tt, E:2 * E])
            p.dma("sync", g_dram[t0 + tt * 128:t0 + (tt + 1) * 128, :],
                  gtile[:, r, :], out_is_output=True)
    for e in range(E):
        ctx.wload_into(w1[e % 2], W["W1"][e * 1024:(e + 1) * 1024, :], 1024, 2048,
                       eng="vector")
        ctx.wload_into(w2, W["W2"][e * 2048:(e + 1) * 2048, :], 2048, 1024,
                       eng="vector")
        p.dma("sync", b1[e % 2][:, :], W["b1"][e * 128:(e + 1) * 128, :])
        ctx.row_tile_into(b2r[e % 2], W["b2"][e:e + 1, :], 1024, rowbuf)
        if False:
            pass
        for ci in range(nch):
            t0 = ci * Tc
            for j in range(16):
                pm = ctx.pmm[j % 2]
                p.mm_acc(pm[:, 0:Tc], [(w1[e % 2][:, k, j * 128:(j + 1) * 128],
                                        xnTf[:, k, t0:t0 + Tc]) for k in range(8)])
                p.act("scalar", hb[:, j, :], pm[:, 0:Tc], AF.Relu,
                      bias=b1[e % 2][:, j:j + 1])
            o = ot[ci % 2]
            if e > 0:
                for tt in range(ntt):
                    p.dma("scalar", o[:, tt, :],
                          eo1_dram[t0 + tt * 128:t0 + (tt + 1) * 128, :],
                          rkey=f"f2eo@{e}_{ci}")
            for mt in range(ntt):
                for no in range(0, 1024, 512):
                    pm = ctx.pmm[2 + (no // 512) % 2]
                    p.mm_acc(pm[:, 0:512],
                             [(hb[:, k, mt * 128:(mt + 1) * 128],
                               w2[:, k, no:no + 512]) for k in range(16)])
                    tmp = ctx.tmp("ff_tmp", [128, 512])
                    p.tt("vector", tmp[:, :], pm[:, 0:512],
                         b2r[e % 2][:, no:no + 512], ALU.add)
                    if e == 0:
                        p.ts("vector", o[:, mt, no:no + 512], tmp[:, :],
                             gtile[:, ci * ntt + mt, 2 + e:3 + e], None, ALU.mult)
                    else:
                        p.stt("vector", o[:, mt, no:no + 512], tmp[:, :],
                              gtile[:, ci * ntt + mt, 2 + e:3 + e],
                              o[:, mt, no:no + 512], ALU.mult, ALU.add)
            for tt in range(ntt):
                p.dma("sync", eo1_dram[t0 + tt * 128:t0 + (tt + 1) * 128, :],
                      o[:, tt, :], out_is_output=(e == E - 1),
                      wkey=f"f2eo@{e + 1}_{ci}")


def emit_l2_ffn_b(ctx, x_dram, eo1_dram, g_dram, xo_dram, W):
    p = ctx.p
    T, Tc = ctx.T, ctx.Tc
    ntt = Tc // 128
    nch = T // Tc
    NTT = T // 128
    E = 8
    gamr = ctx.row_tile(W["ln_g_row"], 1024, "f2lg")
    betr = ctx.row_tile(W["ln_b_row"], 1024, "f2lb")
    scrB = ctx.tmp("scrB_bf16", [128, 4096], BF16)
    xt = p.sb(ctx.n("f2_xt"), [128, ntt, 1024], F32)
    xn_sc = p.sb(ctx.n("f2_xns"), [128, ntt, 1024], F32)
    bstb = scrB[:, 0:ntt * 1024].rearrange("p (a b) -> p a b", b=1024)
    bstTf = p.sb(ctx.n("f2_bTf"), [128, 8, T], BF16)
    gtile = p.sb(ctx.n("f2_g2"), [128, NTT, 18], F32)
    hb = ctx.tmp("scrA", [128, 4096])[:, :].bitcast(BF16)[:, 0:16 * Tc].rearrange(
        "p (a b) -> p a b", b=Tc)
    ot = [p.sb(ctx.n(f"f2_ot{i}"), [128, ntt, 1024], F32) for i in range(2)]
    w1 = [p.sb(ctx.n("f2_w10"), [128, 8, 2048], BF16)] * 2
    w2 = p.sb(ctx.n("f2_w2"), [128, 16, 1024], BF16)
    b1 = [p.sb(ctx.n(f"f2_b1{i}"), [128, 16], F32) for i in range(2)]
    b2r = [p.sb(ctx.n("f2_b2r0"), [128, 1024], F32)] * 2
    rowbuf = ctx.rowbuf

    for ci in range(nch):
        t0 = ci * Tc
        eo1t = ot[(ci + 1) % 2]
        for tt in range(ntt):
            r = ci * ntt + tt
            p.dma("sync", gtile[:, r, :], g_dram[t0 + tt * 128:t0 + (tt + 1) * 128, :])
            p.dma("sync", xt[:, tt, :], x_dram[t0 + tt * 128:t0 + (tt + 1) * 128, :])
            p.dma("gpsimd", eo1t[:, tt, :],
                  eo1_dram[t0 + tt * 128:t0 + (tt + 1) * 128, :])
        ln_chunk(ctx, xt[:, 0:ntt, :], ntt, 1024, xn_sc[:, 0:ntt, :])
        p.stt("vector", bstb[:, 0:ntt, :], eo1t[:, 0:ntt, :], 0.5,
              xn_sc[:, 0:ntt, :], ALU.mult, ALU.add)
        ctx.transpose_tmaj_to_dmaj(bstTf, bstb, ntt, 8, dst_t0=t0)
    for e in range(E):
        ctx.wload_into(w1[e % 2], W["W1"][e * 1024:(e + 1) * 1024, :], 1024, 2048,
                       eng="vector")
        ctx.wload_into(w2, W["W2"][e * 2048:(e + 1) * 2048, :], 2048, 1024,
                       eng="vector")
        p.dma("sync", b1[e % 2][:, :], W["b1"][e * 128:(e + 1) * 128, :])
        ctx.row_tile_into(b2r[e % 2], W["b2"][e:e + 1, :], 1024, rowbuf)
        if False:
            pass
        for ci in range(nch):
            t0 = ci * Tc
            for j in range(16):
                pm = ctx.pmm[j % 2]
                p.mm_acc(pm[:, 0:Tc], [(w1[e % 2][:, k, j * 128:(j + 1) * 128],
                                        bstTf[:, k, t0:t0 + Tc]) for k in range(8)])
                p.act("scalar", hb[:, j, :], pm[:, 0:Tc], AF.Relu,
                      bias=b1[e % 2][:, j:j + 1])
            o = ot[ci % 2]
            if e > 0:
                for tt in range(ntt):
                    p.dma("scalar", o[:, tt, :],
                          xo_dram[t0 + tt * 128:t0 + (tt + 1) * 128, :],
                          rkey=f"f2xo@{e}_{ci}")
            for mt in range(ntt):
                r = ci * ntt + mt
                w2c = ctx.tmp("f2_w2c", [128, 1])
                p.tt("vector", w2c[:, :], gtile[:, r, 1:2],
                     gtile[:, r, 10 + e:11 + e], ALU.mult)
                for no in range(0, 1024, 512):
                    pm = ctx.pmm[2 + (no // 512) % 2]
                    p.mm_acc(pm[:, 0:512],
                             [(hb[:, k, mt * 128:(mt + 1) * 128],
                               w2[:, k, no:no + 512]) for k in range(16)])
                    tmp = ctx.tmp("ff_tmp", [128, 512])
                    p.tt("vector", tmp[:, :], pm[:, 0:512],
                         b2r[e % 2][:, no:no + 512], ALU.add)
                    if e == 0:
                        p.ts("vector", o[:, mt, no:no + 512], tmp[:, :],
                             w2c[:, 0:1], None, ALU.mult)
                    else:
                        p.stt("vector", o[:, mt, no:no + 512], tmp[:, :],
                              w2c[:, 0:1], o[:, mt, no:no + 512],
                              ALU.mult, ALU.add)
            for tt in range(ntt):
                p.dma("sync", xo_dram[t0 + tt * 128:t0 + (tt + 1) * 128, :],
                      o[:, tt, :], wkey=f"f2xo@{e + 1}_{ci}")
    for ci in range(nch):
        t0 = ci * Tc
        o = ot[ci % 2]
        eo1t = ot[(ci + 1) % 2]
        for tt in range(ntt):
            r = ci * ntt + tt
            p.dma("sync", xt[:, tt, :], x_dram[t0 + tt * 128:t0 + (tt + 1) * 128, :])
            p.dma("gpsimd", eo1t[:, tt, :],
                  eo1_dram[t0 + tt * 128:t0 + (tt + 1) * 128, :])
            p.dma("scalar", o[:, tt, :],
                  xo_dram[t0 + tt * 128:t0 + (tt + 1) * 128, :],
                  rkey=f"f2xo@{E}_{ci}")
        ln_chunk(ctx, xt[:, 0:ntt, :], ntt, 1024, xn_sc[:, 0:ntt, :])
        for tt in range(ntt):
            r = ci * ntt + tt
            p.stt("vector", o[:, tt, :], eo1t[:, tt, :], gtile[:, r, 0:1],
                  o[:, tt, :], ALU.mult, ALU.add)
        p.tt("vector", o[:, 0:ntt, :], o[:, 0:ntt, :], xn_sc[:, 0:ntt, :],
             ALU.add)
        ln_chunk(ctx, o[:, 0:ntt, :], ntt, 1024, xn_sc[:, 0:ntt, :],
                 gam_row=gamr, beta_row=betr)
        p.tt("vector", xt[:, 0:ntt, :], xt[:, 0:ntt, :], xn_sc[:, 0:ntt, :],
             ALU.add)
        for tt in range(ntt):
            p.dma("sync", xo_dram[t0 + tt * 128:t0 + (tt + 1) * 128, :],
                  xt[:, tt, :], out_is_output=True, wkey=f"f2xo@{E + 1}_{ci}")


def emit_l3_attn(ctx, x_dram, xo_dram, W):
    p = ctx.p
    T, Tc = ctx.T, ctx.Tc
    ntt = Tc // 128
    nch = T // Tc
    NTT = T // 128
    H = 16
    maskd = ctx.nc.dram_tensor("c_mask", [128, 128], F32, kind="ExternalInput").ap()
    mask = p.sb("l3_mask", [128, 128], F32)
    p.dma("sync", mask[:, :], maskd)
    wproj = ctx.wload(W["WprojT"], 1024, 1024, "l3wp")
    scrB = ctx.tmp("scrB_bf16", [128, 4096], BF16)
    xt = p.sb(ctx.n("a3_xt"), [128, ntt, 1024], F32)
    xnb = scrB[:, 0:ntt * 1024].rearrange("p (a b) -> p a b", b=1024)
    xnT = p.sb(ctx.n("a3_xnT"), [128, 8, Tc], BF16)
    qT = p.sb("l3_qT", [128, 8, T], BF16)
    kT = p.sb("l3_kT", [128, 8, T], BF16)
    vt = p.sb("l3_vt", [128, NTT, 1024], BF16)
    ws = p.sb("l3_ws", [128, 8, 1024], BF16)
    S = p.sb("l3_S", [128, 2048], F32)
    Pb = p.sb("l3_Pb", [128, 2048], BF16)
    PTb = [p.sb(f"l3_PT{i}", [128, 128], BF16) for i in range(2)]
    nmax = p.sb("l3_nmax", [128, 1], F32)
    esum = p.sb("l3_esum", [128, 1], F32)
    oT = p.sb("l3_oT", [128, 8, 128], BF16)
    xq = p.sb("l3_xq", [128, 1024], F32)
    pO = ctx.pbc[1]

    for which in ("q", "k", "v"):
        ctx.wload_into(ws, W["W%sT" % which], 1024, 1024)
        for ci in range(nch):
            t0 = ci * Tc
            for tt in range(ntt):
                p.dma("sync", xt[:, tt, :],
                      x_dram[t0 + tt * 128:t0 + (tt + 1) * 128, :])
            ln_chunk(ctx, xt[:, 0:ntt, :], ntt, 1024, xnb[:, 0:ntt, :])
            ctx.transpose_tmaj_to_dmaj(xnT, xnb, ntt, 8)
            if which in ("q", "k"):
                dst = qT if which == "q" else kT
                scale = 0.125 if which == "q" else 1.0
                for j in range(8):
                    pm = ctx.pmm[j % 2]
                    p.mm_acc(pm[:, 0:Tc], [(ws[:, k, j * 128:(j + 1) * 128],
                                            xnT[:, k, 0:Tc]) for k in range(8)])
                    if scale != 1.0:
                        p.ts("vector", dst[:, j, t0:t0 + Tc], pm[:, 0:Tc],
                             scale, None, ALU.mult)
                    else:
                        p.cp("vector", dst[:, j, t0:t0 + Tc], pm[:, 0:Tc])
            else:
                for mt in range(ntt):
                    for no in range(0, 1024, 512):
                        pm = ctx.pmm[2 + (no // 512) % 2]
                        p.mm_acc(pm[:, 0:512],
                                 [(xnT[:, k, mt * 128:(mt + 1) * 128],
                                   ws[:, k, no:no + 512]) for k in range(8)])
                        p.cp("vector", vt[:, ci * ntt + mt, no:no + 512],
                             pm[:, 0:512])
    for qt in range(NTT):
        nkb = qt + 1
        for h in range(H):
            hp = (h % 2) * 64
            hg = h // 2
            for kb in range(nkb):
                pm = ctx.pmm[kb % 4]
                p.mm(pm[:, 0:128], qT[hp:hp + 64, hg, qt * 128:(qt + 1) * 128],
                     kT[hp:hp + 64, hg, kb * 128:(kb + 1) * 128], True, True)
                if kb == qt:
                    p.tt("vector", S[:, kb * 128:(kb + 1) * 128], pm[:, 0:128],
                         mask[:, :], ALU.add)
                else:
                    p.op("scalar", (lambda S=S, pm=pm, kb=kb:
                                    lambda E: E.copy(
                                        S[:, kb * 128:(kb + 1) * 128],
                                        pm[:, 0:128]))(),
                         reads=[pm[:, 0:1]], writes=[f"{_tn(S)}@{kb}"],
                         tag="sevac")
            skeys = [f"{_tn(S)}@{kb}" for kb in range(nkb)] + [S[:, 0:1]]
            p.op("vector", (lambda S=S, nm=nmax, nkb=nkb:
                            lambda E: E.tensor_reduce(
                                nm[:, :], S[:, 0:nkb * 128], axis=AX.X,
                                op=ALU.max, negate=True))(),
                 reads=skeys, writes=[nmax[:, :]], tag="smax")
            p.op("scalar", (lambda S=S, nm=nmax, es=esum, nkb=nkb:
                            lambda E: E.activation(
                                S[:, 0:nkb * 128], S[:, 0:nkb * 128], AF.Exp,
                                bias=nm[:, :], accum_out=es[:, :]))(),
                 reads=skeys + [nmax[:, :]],
                 writes=[S[:, 0:1], esum[:, :]] + [f"{_tn(S)}@{kb}"
                                                   for kb in range(nkb)],
                 tag="sexp")
            p.recip("vector", esum[:, :], esum[:, :])
            p.ts("vector", Pb[:, 0:nkb * 128], S[:, 0:nkb * 128], esum[:, 0:1],
                 None, ALU.mult)
            for kb in range(nkb):
                pt = ctx.ptr[kb % 2]
                p.transpose(pt[:, :], Pb[:, kb * 128:(kb + 1) * 128],
                            ctx.identb[:, :])
                ptb = PTb[kb % 2]
                p.op("scalar", (lambda d=ptb, src=pt:
                                lambda E: E.copy(d[:, :], src[:, :]))(),
                     reads=[pt[:, 0:1]], writes=[ptb[:, 0:1]], tag="ptevac")
                p.mm(pO[0:64, 0:128], vt[:, kb, h * 64:(h + 1) * 64], ptb[:, :],
                     kb == 0, kb == nkb - 1)
            p.cp("vector", oT[hp:hp + 64, hg, :], pO[0:64, 0:128])
        p.dma("sync", xq[:, :], x_dram[qt * 128:(qt + 1) * 128, :])
        for no in range(0, 1024, 512):
            pm = ctx.pmm[(no // 512) % 2]
            p.mm_acc(pm[:, 0:512], [(oT[:, k, :], wproj[:, k, no:no + 512])
                                    for k in range(8)])
            p.tt("vector", xq[:, no:no + 512], xq[:, no:no + 512], pm[:, 0:512],
                 ALU.add)
        p.dma("sync", xo_dram[qt * 128:(qt + 1) * 128, :], xq[:, :],
              out_is_output=True)


def emit_full(ctx, x_in, x_out, Wd):
    p = ctx.p
    nc = ctx.nc
    T = ctx.T
    xs = [nc.dram_tensor(f"xs{i}", [T, 1024], F32).ap() for i in range(7)]
    yf = nc.dram_tensor("yf_s", [128, 8, T], BF16).ap()
    eo1 = nc.dram_tensor("eo1_s", [T, 1024], F32).ap()
    gsc = nc.dram_tensor("g_s", [T, 18], F32).ap()
    ctx.prewarm()

    def sub(pre):
        return {k[len(pre):]: v for k, v in Wd.items() if k.startswith(pre)}

    seq = x_in
    ctx.Tc = 256
    xnd = nc.dram_tensor("l0_xnd", [T, 1024], F32).ap()
    x2Td = nc.dram_tensor("l0_x2Td", [128, 8, T], BF16).ap()
    with p.scope():
        wf = sub("l0f_")
        wf["_xnd"] = xnd
        wf["_x2Td"] = x2Td
        emit_l0_dir(ctx, seq, xs[0], wf, False)
    with p.scope():
        wb = sub("l0b_")
        wb["_xnd"] = xnd
        wb["_x2Td"] = x2Td
        emit_l0_dir(ctx, seq, xs[0], wb, True)
    seq = xs[0]
    ctx.Tc = 256
    with p.scope():
        emit_l0_ffn(ctx, seq, xs[1], sub("f0_"))
    seq = xs[1]
    xcd_f = nc.dram_tensor("l1f_xcd", [128, 16, T], BF16).ap()
    zd_f = nc.dram_tensor("l1f_zd", [128, 16, T], BF16).ap()
    xcd_b = nc.dram_tensor("l1b_xcd", [128, 16, T], BF16).ap()
    zd_b = nc.dram_tensor("l1b_zd", [128, 16, T], BF16).ap()
    with p.scope():
        emit_l1_phA(ctx, seq, sub("l1f_"), False, xcd_f, zd_f, "l1f")
    with p.scope():
        ctx.Tc = 256
        emit_l1_phB(ctx, seq, sub("l1f_"), False, xcd_f, zd_f, "l1f", yf, None)
    with p.scope():
        emit_l1_phA(ctx, seq, sub("l1b_"), True, xcd_b, zd_b, "l1b")
    with p.scope():
        ctx.Tc = 256
        emit_l1_phB(ctx, seq, sub("l1b_"), True, xcd_b, zd_b, "l1b", yf, xs[2])
    seq = xs[2]
    ctx.Tc = 256
    with p.scope():
        emit_l1_ffn(ctx, seq, xs[3], sub("f1_"))
    seq = xs[3]
    ctx.Tc = 256
    with p.scope():
        emit_l2_attn(ctx, seq, xs[4], sub("l2_"))
    seq = xs[4]
    ctx.Tc = 256
    with p.scope():
        emit_l2_ffn_a(ctx, seq, eo1, gsc, sub("f2_"))
    with p.scope():
        emit_l2_ffn_b(ctx, seq, eo1, gsc, xs[5], sub("f2_"))
    seq = xs[5]
    ctx.Tc = 256
    with p.scope():
        emit_l3_attn(ctx, seq, xs[6], sub("a3_"))
    seq = xs[6]
    ctx.Tc = 512
    with p.scope():
        emit_l3_ffn(ctx, seq, x_out, sub("f3_"))


import ml_dtypes



def dmaj_vec(v):
    v = np.asarray(v, np.float32)
    G = v.shape[0] // 128
    return np.ascontiguousarray(v.reshape(G, 128).T)


def dmaj_mat(m):
    m = np.asarray(m, np.float32)
    D, W = m.shape
    G = D // 128
    return np.ascontiguousarray(m.reshape(G, 128, W).transpose(1, 0, 2).reshape(128, G * W))


def kxm(m):
    return np.ascontiguousarray(np.asarray(m, np.float32))


def row(v):
    return np.ascontiguousarray(np.asarray(v, np.float32).reshape(1, -1))


def consts():
    seln = np.zeros((16, 16 * 128), ml_dtypes.bfloat16)
    for n in range(16):
        seln[n, n * 128:(n + 1) * 128] = 1.0
    return {
        "c_identb": np.eye(128, dtype=ml_dtypes.bfloat16),
        "c_ones": np.ones((1, 128), np.float32),
        "c_seln": seln,
    }


def l2_attn_inputs(p):
    return {
        "l2_W_in": kxm(p["W_in"]),
        "l2_conv_w": dmaj_mat(p["conv_w"][:, 0, :]),
        "l2_conv_b": dmaj_vec(p["conv_b"]),
        "l2_W_xproj": kxm(p["W_xproj"]),
        "l2_log_A": dmaj_mat(p["log_A"]),
        "l2_D_param": dmaj_vec(p["D_param"]),
        "l2_W_out": kxm(p["W_out"]),
    }


def l0_dir_inputs(p, pre, tag):
    return {
        tag + "inw_lhsT": kxm(p[pre + "_in_proj_w"].T),
        tag + "conv_w": dmaj_mat(p[pre + "_conv_w"][:, 0, :]),
        tag + "conv_b": dmaj_vec(p[pre + "_conv_b"]),
        tag + "xproj_lhsT": kxm(p[pre + "_x_proj_w"].T),
        tag + "dtw_lhsT": kxm(p[pre + "_dt_proj_w"].T),
        tag + "dt_b": dmaj_vec(p[pre + "_dt_proj_b"]),
        tag + "A_log": dmaj_mat(p[pre + "_A_log"]),
        tag + "D": dmaj_vec(p[pre + "_D"]),
        tag + "outw_rhs": kxm(p[pre + "_out_proj_w"].T),
        tag + "gam_row": row(p["norm_w"]),
        tag + "beta_row": row(p["norm_b"]),
    }


def l1_dir_inputs(p, pre, tag):
    d = {
        tag + "inw_lhsT": kxm(p[pre + "_in_w"].T),
        tag + "conv_w": dmaj_mat(p[pre + "_conv_w"]),
        tag + "conv_b": dmaj_vec(p[pre + "_conv_b"]),
        tag + "xproj_lhsT": kxm(p[pre + "_xproj_w"].T),
        tag + "dtw_lhsT": kxm(p[pre + "_dt_w"].T),
        tag + "dt_b": dmaj_vec(p[pre + "_dt_b"]),
        tag + "A_log": dmaj_mat(p[pre + "_A_log"]),
        tag + "D": dmaj_vec(p[pre + "_D"]),
        tag + "outw_lhsT": kxm(p[pre + "_out_w"].T),
        tag + "gam_row": row(p["ln_g"]),
        tag + "beta_row": row(p["ln_b"]),
    }
    if pre == "bwd":
        d[tag + "merge_rhs"] = kxm(p["merge_w"].T)
        d[tag + "merge_b_row"] = row(p["merge_b"])
    return d


def l3_ffn_inputs(p):
    ew = np.asarray(p["expert_w"], np.float32)
    ewT = np.concatenate([ew[e].T for e in range(8)], axis=0)
    return {
        "f3_router_wT": kxm(p["router_w"].T),
        "f3_router_b_row": row(p["router_b"]),
        "f3_expert_wT": np.ascontiguousarray(ewT),
        "f3_expert_b": np.ascontiguousarray(np.asarray(p["expert_b"], np.float32)),
    }


def l0_ffn_inputs(p):
    return {
        "f0_gate_wT": kxm(p["gate_w"].T),
        "f0_w1_e0": kxm(p["w1"][0]), "f0_w1_e1": kxm(p["w1"][1]),
        "f0_b1_e0": dmaj_vec(p["b1"][0]), "f0_b1_e1": dmaj_vec(p["b1"][1]),
        "f0_w2_e0": kxm(p["w2"][0]), "f0_w2_e1": kxm(p["w2"][1]),
        "f0_b2_e0": row(p["b2"][0]), "f0_b2_e1": row(p["b2"][1]),
        "f0_sw1": kxm(p["sw1"]), "f0_sb1": dmaj_vec(p["sb1"]),
        "f0_sw2": kxm(p["sw2"]), "f0_sb2": row(p["sb2"]),
    }


def l1_ffn_inputs(p):
    W1 = np.asarray(p["W1"], np.float32)
    W2 = np.asarray(p["W2"], np.float32)
    b1 = np.asarray(p["b1"], np.float32)
    return {
        "f1_Wr": kxm(p["Wr"]),
        "f1_br_row": row(p["br"]),
        "f1_W1": np.ascontiguousarray(W1.reshape(16 * 1024, 2048)),
        "f1_b1": np.concatenate([dmaj_vec(b1[e]) for e in range(16)], axis=0),
        "f1_W2": np.ascontiguousarray(W2.reshape(16 * 2048, 1024)),
        "f1_b2": np.ascontiguousarray(np.asarray(p["b2"], np.float32)),
    }


def l2_ffn_inputs(p):
    W1 = np.asarray(p["W1"], np.float32)
    W2 = np.asarray(p["W2"], np.float32)
    b1 = np.asarray(p["b1"], np.float32)
    return {
        "f2_gate_wT": kxm(p["gate_W"]),
        "f2_gate_b_row": row(p["gate_b"]),
        "f2_W1": np.ascontiguousarray(W1.reshape(8 * 1024, 2048)),
        "f2_b1": np.concatenate([dmaj_vec(b1[e]) for e in range(8)], axis=0),
        "f2_W2": np.ascontiguousarray(W2.reshape(8 * 2048, 1024)),
        "f2_b2": np.ascontiguousarray(np.asarray(p["b2"], np.float32)),
        "f2_ln_g_row": row(p["ln_gamma"]),
        "f2_ln_b_row": row(p["ln_beta"]),
    }


def l3_attn_inputs(p):
    Wa = np.asarray(p["W_attn"], np.float32)
    m = np.zeros((128, 128), np.float32)
    iu = np.triu_indices(128, 1)
    m[iu] = -3e38
    return {
        "a3_WqT": kxm(Wa[0:1024].T),
        "a3_WkT": kxm(Wa[1024:2048].T),
        "a3_WvT": kxm(Wa[2048:3072].T),
        "a3_WprojT": kxm(p["W_proj"].T),
        "c_mask": m,
    }



from concourse.bass_utils import run_bass_kernel_spmd

T_FULL = 2048
NCORES = 8


def _split_params(inputs):
    names = ["L0_attn", "L0_ffn", "L1_attn", "L1_ffn", "L2_attn", "L2_ffn",
             "L3_attn", "L3_ffn"]
    out = {n: {} for n in names}
    for k, v in inputs.items():
        if k == "L0_attn_x":
            continue
        for n in names:
            if k.startswith(n + "_"):
                out[n][k[len(n) + 1:]] = np.asarray(v)
                break
    return out


def _host_weights(params):
    W = {}
    W.update(l0_dir_inputs(params["L0_attn"], "fwd", "l0f_"))
    W.update(l0_dir_inputs(params["L0_attn"], "bwd", "l0b_"))
    W.update({k.replace("f0_", "f0_"): v for k, v in l0_ffn_inputs(params["L0_ffn"]).items()})
    W.update({k.replace("l1_", "l1f_"): v for k, v in l1_dir_inputs(params["L1_attn"], "fwd", "l1_").items()})
    W.update({k.replace("l1_", "l1b_"): v for k, v in l1_dir_inputs(params["L1_attn"], "bwd", "l1_").items()})
    W.update(l1_ffn_inputs(params["L1_ffn"]))
    W.update(l2_attn_inputs(params["L2_attn"]))
    W.update(l2_ffn_inputs(params["L2_ffn"]))
    aw = l3_attn_inputs(params["L3_attn"])
    W.update(aw)
    W.update(l3_ffn_inputs(params["L3_ffn"]))
    return W


def kernel(**inputs):
    x_full = np.ascontiguousarray(np.asarray(inputs["L0_attn_x"], np.float32))
    B, T, D = x_full.shape
    assert (B, T, D) == (8, 2048, 1024)
    params = _split_params(inputs)
    C = consts()
    W = _host_weights(params)

    nc = build_nc()
    x = nc.dram_tensor("x", [T_FULL, 1024], F32, kind="ExternalInput").ap()
    xo = nc.dram_tensor("xo", [T_FULL, 1024], F32, kind="ExternalOutput").ap()
    Wd = {}
    for k, v in W.items():
        if k == "c_mask":
            continue
        Wd[k] = nc.dram_tensor(k, list(v.shape), F32, kind="ExternalInput").ap()
    p = Prog(nc, "mega")
    ctx = Ctx(p, T_FULL, 512)
    emit_full(ctx, x, xo, Wd)
    p.finalize()

    in_maps = [{"x": np.ascontiguousarray(x_full[b]), **W, **C}
               for b in range(B)]
    res = run_bass_kernel_spmd(nc, in_maps, core_ids=list(range(NCORES)))
    return np.stack([r["xo"] for r in res.results], axis=0).astype(np.float32)
